# version 03
# baseline (speedup 1.0000x reference)
import contextlib
import numpy as np
import ml_dtypes
import concourse.bass as bass
import concourse.mybir as mybir
from concourse.bass_utils import run_bass_kernel_spmd

F32 = mybir.dt.float32
BF16 = mybir.dt.bfloat16
I32 = mybir.dt.int32
AF = mybir.ActivationFunctionType
ALU = mybir.AluOpType
AX = mybir.AxisListType

T = 1024
KD = 16
DM = 2048


class Sem:
    def __init__(self, h):
        self.h = h
        self.v = 0

    def inc(self, n=1):
        self.v += n
        return self.v


class Chain:
    def __init__(self, C, name):
        self.s = C.sem(name)

    def dep(self, inst):
        inst.then_inc(self.s.h, 1)
        self.s.inc()
        return inst

    def wait(self, eng):
        eng.wait_ge(self.s.h, self.s.v)


class Ctx:
    def __init__(self):
        self.nc = bass.Bass("TRN2", target_bir_lowering=False, debug=False)
        self.es = contextlib.ExitStack()
        self.root_es = self.es
        self.block = None
        self.const_tiles = {}
        self.const_dram = {}
        self.const_arrays = {}

    def start_block(self):
        self.block = self.es.enter_context(self.nc.Block())
        self.ps8 = [self.psum(f"ps8_{i}", [128, 512]) for i in range(8)]
        self.marks = []

    def mark(self, sem, v=None):
        self.marks.append((sem, sem.v if v is None else v))

    def sem(self, name):
        if getattr(self, "_in_scope", False):
            h = self.nc.alloc_semaphore(name=name)
            self._scope_sems.append(h)
            return Sem(h)
        return Sem(self.es.enter_context(self.nc.semaphore(name)))

    def sbuf(self, name, shape, dt):
        return self.es.enter_context(self.nc.sbuf_tensor(name, shape, dt))

    def psum(self, name, shape, dt=F32):
        return self.es.enter_context(self.nc.psum_tensor(name, shape, dt))

    @contextlib.contextmanager
    def scope(self):
        old = self.es
        self.es = contextlib.ExitStack()
        self._scope_sems = []
        self._scope_n = getattr(self, "_scope_n", 0) + 1
        self._in_scope = True
        try:
            yield
        finally:
            self._in_scope = False
            handles = self._scope_sems
            self._scope_sems = []
            if handles:
                s_clr = Sem(self.root_es.enter_context(
                    self.nc.semaphore(f"scopeclr{self._scope_n}")))
                b = self.block

                hs = sorted(handles, key=lambda h: h.num)
                runs = [[hs[0]]]
                for h in hs[1:]:
                    if h.num == runs[-1][-1].num + 1 and len(runs[-1]) < 10:
                        runs[-1].append(h)
                    else:
                        runs.append([h])

                @b.gpsimd
                def _(g):
                    last = None
                    for run in runs:
                        r = range(run[0].num, run[-1].num + 1)
                        g.dma_reset(r)
                        last = g.sem_clear(r)
                    last.then_inc(s_clr.h, 1)
                s_clr.inc(1)
                self.mark(s_clr)
                self.barrier()
                for h in handles:
                    self.nc.release_semaphore(h)
            self.es.close()
            self.es = old

    def const(self, name, arr, dt):
        if name in self.const_tiles:
            return self.const_tiles[name]
        import ml_dtypes
        _np = {F32: np.float32, BF16: ml_dtypes.bfloat16,
               mybir.dt.int32: np.int32}.get(dt)
        arr = np.ascontiguousarray(np.asarray(arr).astype(_np))
        d = self.nc.dram_tensor(name, list(arr.shape), dt, kind="ExternalInput").ap()
        t = self.es.enter_context(self.nc.sbuf_tensor(name + "_sb", list(arr.shape), dt))
        self.const_dram[name] = d
        self.const_tiles[name] = t
        self.const_arrays[name] = arr
        return t

    def emit_const_loads(self):
        if not hasattr(self, "_consts_loaded"):
            self._consts_loaded = set()
            self._constload_n = 0
        todo = [n for n in self.const_dram if n not in self._consts_loaded]
        if not todo:
            return None
        self._consts_loaded.update(todo)
        self._constload_n += 1
        s = self.sem(f"constload{self._constload_n}")
        b = self.block

        @b.sync
        def _(sync):
            for name in todo:
                sync.dma_start(out=self.const_tiles[name][:],
                               in_=self.const_dram[name]).then_inc(s.h, 16)
        s.inc(16 * len(todo))

        self.mark(s)
        return s

    def barrier(self):
        marks = [(s, v) for (s, v) in self.marks]
        self.marks = []
        b = self.block
        for dec in [b.tensor, b.vector, b.scalar, b.sync, b.gpsimd]:
            @dec
            def _(eng):
                for sem, v in marks:
                    eng.wait_ge(sem.h, v)


def bfc(x):
    return np.ascontiguousarray(np.asarray(x).astype(ml_dtypes.bfloat16))


def f32c(x):
    return np.ascontiguousarray(np.asarray(x), dtype=np.float32)


def make_ln_scratch(C):
    sc = {}
    sc["m"] = C.sbuf("ln_m", [1, 1024], F32)
    sc["v"] = C.sbuf("ln_v", [1, 1024], F32)
    sc["rs"] = C.sbuf("ln_rs", [1, 1024], F32)
    sc["Bm"] = C.sbuf("ln_Bm", [128, 1024], F32)
    sc["Brs"] = C.sbuf("ln_Brs", [128, 1024], F32)
    sc["tmp"] = [C.sbuf(f"ln_tmp{i}", [128, 512], F32) for i in range(2)]
    sc["sq"] = sc["tmp"]
    sc["psS1"] = C.ps8[4][0:1, :]
    sc["psS2"] = C.ps8[5][0:1, :]
    sc["psBm"] = C.ps8[6]
    sc["psBrs"] = C.ps8[7]
    sc["ones_col"] = C.const("ones_col_f32", np.ones((128, 1), np.float32), F32)
    sc["ones_row"] = C.const("ones_row_f32", np.ones((1, 128), np.float32), F32)
    sc["eps"] = C.const("eps_ln", np.full((128, 1), 1e-5, np.float32), F32)
    return sc


def emit_ln(C, sc, R, xn, mode="plain", h32=None, tag=""):
    b = C.block
    s_sq = C.sem("lnsq" + tag)
    s_mm = C.sem("lnmm" + tag)
    s_r1 = C.sem("lnr1" + tag)
    s_ev = C.sem("lnev" + tag)
    sq, tmp = sc["sq"], sc["tmp"]
    psS1, psS2 = sc["psS1"], sc["psS2"]
    psBm, psBrs = sc["psBm"], sc["psBrs"]
    m_sb, v_sb, rs_sb = sc["m"], sc["v"], sc["rs"]
    Bm, Brs = sc["Bm"], sc["Brs"]
    sq_base = s_sq.v
    mm_base = s_mm.v
    ev_base = s_ev.v

    for tb in range(2):
        @b.scalar
        def _(scalar, tb=tb):
            for k in range(KD):
                g = tb * KD + k
                if g >= 2:
                    scalar.wait_ge(s_mm.h, mm_base + 2 * (g - 1))
                scalar.activation(sq[g % 2][:],
                                  R[:, k * T + tb * 512: k * T + tb * 512 + 512],
                                  AF.Square).then_inc(s_sq.h, 1)
        s_sq.inc(KD)

        @b.tensor
        def _(tensor, tb=tb):
            if tb == 1:
                tensor.wait_ge(s_ev.h, ev_base + 2)
            for k in range(KD):
                g = tb * KD + k
                tensor.matmul(psS1, sc["ones_col"][:],
                              R[:, k * T + tb * 512: k * T + tb * 512 + 512],
                              start=(k == 0), stop=(k == KD - 1))
                tensor.wait_ge(s_sq.h, sq_base + tb * (KD + 3) + k + 1)
                tensor.matmul(psS2, sc["ones_col"][:], sq[g % 2][:],
                              start=(k == 0), stop=(k == KD - 1)).then_inc(s_mm.h, 2)
        s_mm.inc(2 * KD)

        sl = slice(tb * 512, (tb + 1) * 512)

        @b.vector
        def _(vector, sl=sl):
            vector.wait_ge(s_mm.h, s_mm.v)
            vector.tensor_scalar(m_sb[:, sl], psS1, 1.0 / DM, None,
                                 ALU.mult).then_inc(s_ev.h, 1)
        s_ev.inc()

        @b.scalar
        def _(scalar, sl=sl):
            scalar.wait_ge(s_ev.h, s_ev.v)
            scalar.activation(v_sb[:, sl], m_sb[:, sl], AF.Square).then_inc(s_sq.h, 1)
        s_sq.inc()

        @b.vector
        def _(vector, sl=sl):
            vector.wait_ge(s_sq.h, s_sq.v)
            vector.scalar_tensor_tensor(v_sb[:, sl], psS2, 1.0 / DM, v_sb[:, sl],
                                        ALU.mult, ALU.subtract).then_inc(s_ev.h, 1)
        s_ev.inc()

        @b.scalar
        def _(scalar, sl=sl):
            scalar.wait_ge(s_ev.h, s_ev.v)
            scalar.activation(rs_sb[:, sl], v_sb[:, sl], AF.Ln, bias=sc["eps"][0:1, :]).then_inc(s_sq.h, 1)
            scalar.wait_ge(s_sq.h, s_sq.v + 1)
            scalar.activation(rs_sb[:, sl], rs_sb[:, sl], AF.Exp, scale=-0.5).then_inc(s_sq.h, 1)
        s_sq.inc(2)

        @b.tensor
        def _(tensor, sl=sl, tb=tb):
            tensor.wait_ge(s_sq.h, s_sq.v)
            if tb == 1:
                tensor.wait_ge(s_ev.h, ev_base + 3)
            tensor.matmul(psBm[:], sc["ones_row"][:], m_sb[:, sl],
                          start=True, stop=True)
            tensor.matmul(psBrs[:], sc["ones_row"][:], rs_sb[:, sl],
                          start=True, stop=True).then_inc(s_r1.h, 1)
        s_r1.inc()

        @b.vector
        def _(vector, sl=sl):
            vector.wait_ge(s_r1.h, s_r1.v)
            vector.tensor_copy(Bm[:, sl], psBm[:])
            vector.tensor_copy(Brs[:, sl], psBrs[:]).then_inc(s_ev.h, 1)
        s_ev.inc()

    ch = Chain(C, "lnch" + tag)

    @b.vector
    def _(vector):
        vector.wait_ge(s_ev.h, s_ev.v)
        for k in range(KD):
            for tb in range(2):
                i = k * 2 + tb
                sl = slice(k * T + tb * 512, k * T + tb * 512 + 512)
                bsl = slice(tb * 512, (tb + 1) * 512)
                tt = tmp[i % 2]
                ch.wait(vector)
                ch.dep(vector.tensor_tensor(tt[:], R[:, sl], Bm[:, bsl], ALU.subtract))
                ch.wait(vector)
                if mode == "plain":
                    last = vector.tensor_tensor(xn[:, sl], tt[:], Brs[:, bsl], ALU.mult)
                elif h32 is None:
                    hb = tmp[1 - i % 2]
                    ch.dep(vector.tensor_tensor(hb[:], tt[:], Brs[:, bsl], ALU.mult))
                    ch.wait(vector)
                    vector.tensor_copy(xn[:, sl], hb[:])
                    last = vector.tensor_tensor(R[:, sl], R[:, sl], hb[:], ALU.add)
                    if not (k == KD - 1 and tb == 1):
                        ch.dep(last)
                else:
                    ch.dep(vector.tensor_tensor(h32[:, sl], tt[:], Brs[:, bsl], ALU.mult))
                    ch.wait(vector)
                    last = vector.tensor_copy(xn[:, sl], h32[:, sl])
                    if mode == "mamba":
                        last = vector.tensor_tensor(R[:, sl], R[:, sl], h32[:, sl], ALU.add)
                if k == KD - 1 and tb == 1:
                    last.then_inc(s_ev.h, 1)
                    s_ev.inc()
    C.mark(s_ev)
    C.mark(s_r1)
    C.mark(s_sq)
    C.mark(s_mm)


def emit_ffn(C, R, xn, wgT, wuT, wdT, hscr, pref="f"):
    b = C.block
    OB = 32
    NU = 4 * OB
    wg32 = C.sbuf(pref + "wg32", [128, KD * 256], F32)
    wu32 = C.sbuf(pref + "wu32", [128, KD * 256], F32)
    wgbf = C.sbuf(pref + "wgbf", [128, 2 * KD * 256], BF16)
    wubf = C.sbuf(pref + "wubf", [128, 2 * KD * 256], BF16)
    gs = [C.sbuf(pref + f"gs{i}", [128, 512], BF16) for i in range(2)]
    hb = [C.sbuf(pref + f"hb{i}", [128, 512], BF16) for i in range(2)]
    ps = C.ps8

    s_wdma = C.sem(pref + "wdma")
    s_cast = C.sem(pref + "cast")
    s_mm = C.sem(pref + "mm")
    s_silu = C.sem(pref + "silu")
    s_hev = C.sem(pref + "hev")
    s_hsp = [C.sem(pref + "hsp0"), C.sem(pref + "hsp1")]

    wg3 = wgT.rearrange("(k p) o -> p k o", p=128)
    wu3 = wuT.rearrange("(k p) o -> p k o", p=128)
    h3 = hscr.rearrange("(o p) t -> p o t", p=128)

    @b.sync
    def _(sync):
        for ob in range(OB):
            if ob >= 1:
                sync.wait_ge(s_cast.h, 2 * ob)
            sync.dma_start(out=wg32[:].rearrange("p (k o) -> p k o", k=KD),
                           in_=wg3[:, :, ob * 256:(ob + 1) * 256]).then_inc(s_wdma.h, 16)
            sync.dma_start(out=wu32[:].rearrange("p (k o) -> p k o", k=KD),
                           in_=wu3[:, :, ob * 256:(ob + 1) * 256]).then_inc(s_wdma.h, 16)
    s_wdma.inc(32 * OB)

    @b.vector
    def _(vector):
        def cast(ob):
            bb = ob % 2
            vector.wait_ge(s_wdma.h, 32 * (ob + 1))
            if ob >= 2:
                vector.wait_ge(s_mm.h, 8 * (ob - 1))
            vector.tensor_copy(wgbf[:, bb * KD * 256:(bb + 1) * KD * 256], wg32[:])
            vector.tensor_copy(wubf[:, bb * KD * 256:(bb + 1) * KD * 256],
                               wu32[:]).then_inc(s_cast.h, 2)

        def hev(u):
            pb = u % 2
            vector.wait_ge(s_mm.h, 2 * u + 2)
            vector.wait_ge(s_silu.h, u + 1)
            if u >= 2:
                vector.wait_ge(s_hsp[u % 2].h, 16 * (u // 2))
            vector.tensor_tensor(hb[pb][:], ps[2 + pb][:], gs[pb][:],
                                 ALU.mult).then_inc(s_hev.h, 1)

        cast(0)
        for ob in range(1, OB):
            cast(ob)
            for u in range(4 * (ob - 1), 4 * ob):
                hev(u)
        for u in range(4 * (OB - 1), NU):
            hev(u)
    s_cast.inc(2 * OB)
    s_hev.inc(NU)

    @b.tensor
    def _(tensor):
        for u in range(NU):
            ot, tb = divmod(u, 2)
            ob, sub = divmod(ot, 2)
            bb = ob % 2
            pb = u % 2
            tensor.wait_ge(s_cast.h, 2 * (ob + 1))
            if u >= 2:
                tensor.wait_ge(s_silu.h, u - 1)
                tensor.wait_ge(s_hev.h, u - 1)
            woff = bb * KD * 256 + sub * 128
            for k in range(KD):
                mm = tensor.matmul(ps[pb][:], wgbf[:, woff + k * 256: woff + k * 256 + 128],
                                   xn[:, k * T + tb * 512: k * T + tb * 512 + 512],
                                   start=(k == 0), stop=(k == KD - 1))
                if k == KD - 1:
                    mm.then_inc(s_mm.h, 1)
            for k in range(KD):
                mm = tensor.matmul(ps[2 + pb][:], wubf[:, woff + k * 256: woff + k * 256 + 128],
                                   xn[:, k * T + tb * 512: k * T + tb * 512 + 512],
                                   start=(k == 0), stop=(k == KD - 1))
                if k == KD - 1:
                    mm.then_inc(s_mm.h, 1)
    s_mm.inc(2 * NU)

    @b.scalar
    def _(scalar):
        for u in range(NU):
            pb = u % 2
            scalar.wait_ge(s_mm.h, 2 * u + 1)
            if u >= 2:
                scalar.wait_ge(s_hev.h, u - 1)
            scalar.activation(gs[pb][:], ps[pb][:], AF.Silu).then_inc(s_silu.h, 1)
    s_silu.inc(NU)

    @b.gpsimd
    def _(gpsimd):
        for u in range(NU):
            ot, tb = divmod(u, 2)
            pb = u % 2
            gpsimd.wait_ge(s_hev.h, u + 1)
            gpsimd.dma_start(out=h3[:, ot, tb * 512:(tb + 1) * 512],
                             in_=hb[pb][:]).then_inc(s_hsp[u % 2].h, 16)
    s_hsp[0].inc(16 * NU // 2)
    s_hsp[1].inc(16 * NU // 2)

    for m in (s_mm, s_silu, s_hev, s_hsp[0], s_hsp[1], s_wdma, s_cast):
        C.mark(m)
    C.barrier()

    KH = 64
    wd32 = wg32
    wdbf = wgbf
    hbuf = wubf
    s_ddma = C.sem(pref + "ddma")
    s_dcast = C.sem(pref + "dcast")
    s_hdma = [C.sem(pref + "hdma0"), C.sem(pref + "hdma1")]
    s_dmm2 = C.sem(pref + "dmm2")

    def dmm2_cover_wd(g):
        return 16 * (g // 8) + 2 * (g % 8) + 2
    s_dev = C.sem(pref + "dev")

    wd3 = wdT.rearrange("(k p) o -> p k o", p=128)

    @b.sync
    def _(sync):
        for gd in range(32):
            dg, kb = divmod(gd, 8)
            if gd >= 1:
                sync.wait_ge(s_dcast.h, gd)
            sync.dma_start(out=wd32[:].rearrange("p (k o) -> p k o", k=8),
                           in_=wd3[:, kb * 8:(kb + 1) * 8, dg * 512:(dg + 1) * 512]
                           ).then_inc(s_ddma.h, 16)
    s_ddma.inc(16 * 32)

    @b.scalar
    def _(scalar):
        for gd in range(32):
            bb = gd % 2
            scalar.wait_ge(s_ddma.h, 16 * (gd + 1))
            if gd >= 2:
                scalar.wait_ge(s_dmm2.h, dmm2_cover_wd(gd - 2))
            scalar.activation(wdbf[:, bb * 4096:(bb + 1) * 4096], wd32[:],
                              AF.Identity).then_inc(s_dcast.h, 1)
    s_dcast.inc(32)

    @b.gpsimd
    def _(gpsimd):
        for hk in range(64):
            dg, hb16 = divmod(hk, 16)
            bb = hk % 2
            if hk >= 2:
                gpsimd.wait_ge(s_dmm2.h, hk - 1)
            gpsimd.dma_start(out=hbuf[:, bb * 4096:(bb + 1) * 4096].rearrange("p (k t) -> p k t", k=4),
                             in_=h3[:, hb16 * 4:hb16 * 4 + 4, :]).then_inc(s_hdma[hk % 2].h, 16)
    s_hdma[0].inc(16 * 32)
    s_hdma[1].inc(16 * 32)

    @b.tensor
    def _(tensor):
        for dg in range(4):
            if dg >= 1:
                tensor.wait_ge(s_dev.h, 8 * dg)
            for hb16 in range(16):
                hk = dg * 16 + hb16
                bb = hk % 2
                tensor.wait_ge(s_hdma[hk % 2].h, 16 * (hk // 2 + 1))
                mm = None
                for kk in range(4):
                    k = hb16 * 4 + kk
                    gd = dg * 8 + k // 8
                    if (k % 8) == 0:
                        tensor.wait_ge(s_dcast.h, gd + 1)
                    wb = gd % 2
                    for d4 in range(4):
                        for tb in range(2):
                            mm = tensor.matmul(
                                ps[d4 * 2 + tb][:],
                                wdbf[:, wb * 4096 + (k % 8) * 512 + d4 * 128:
                                     wb * 4096 + (k % 8) * 512 + d4 * 128 + 128],
                                hbuf[:, bb * 4096 + kk * 1024 + tb * 512:
                                     bb * 4096 + kk * 1024 + tb * 512 + 512],
                                start=(k == 0), stop=(k == KH - 1))
                mm.then_inc(s_dmm2.h, 1)
                s_dmm2.inc()

    @b.vector
    def _(vector):
        for dg in range(4):
            vector.wait_ge(s_dmm2.h, 16 * (dg + 1))
            for d4 in range(4):
                for tb in range(2):
                    dt_i = dg * 4 + d4
                    sl = slice(dt_i * T + tb * 512, dt_i * T + tb * 512 + 512)
                    vector.tensor_tensor(R[:, sl], R[:, sl], ps[d4 * 2 + tb][:],
                                         ALU.add).then_inc(s_dev.h, 1)
    s_dev.inc(32)

    for m in (s_dmm2, s_dev, s_dcast, s_ddma, s_hdma[0], s_hdma[1]):
        C.mark(m)
    C.barrier()


def emit_stream_load(C, R, x_in):
    b = C.block
    s = C.sem("xload")

    @b.sync
    def _(sync):
        sync.dma_start(out=R[:].rearrange("p (k t) -> p k t", k=KD),
                       in_=x_in.rearrange("(k p) t -> p k t", p=128)).then_inc(s.h, 16)
    s.inc(16)
    C.mark(s)


def emit_stream_store(C, R, x_out):
    b = C.block
    s = C.sem("xstore")

    @b.sync
    def _(sync):
        sync.dma_start(out=x_out.rearrange("(k p) t -> p k t", p=128),
                       in_=R[:].rearrange("p (k t) -> p k t", k=KD)).then_inc(s.h, 16)
        sync.wait_ge(s.h, 16)
    s.inc(16)


def build_ffn_launch():
    C = Ctx()
    nc = C.nc
    x_in = nc.dram_tensor("x_in", [DM, T], F32, kind="ExternalInput").ap()
    x_out = nc.dram_tensor("x_out", [DM, T], F32, kind="ExternalOutput").ap()
    wgT = nc.dram_tensor("wgT", [DM, 8192], F32, kind="ExternalInput").ap()
    wuT = nc.dram_tensor("wuT", [DM, 8192], F32, kind="ExternalInput").ap()
    wdT = nc.dram_tensor("wdT", [8192, DM], F32, kind="ExternalInput").ap()
    hscr = nc.dram_tensor("hscr", [8192, T], BF16)

    R = C.sbuf("R", [128, KD * T], F32)
    xn = C.sbuf("xn", [128, KD * T], BF16)
    C.start_block()
    sc = make_ln_scratch(C)
    C.emit_const_loads()
    emit_stream_load(C, R, x_in)
    C.barrier()
    emit_ln(C, sc, R, xn, mode="plain")
    C.barrier()
    emit_ffn(C, R, xn, wgT, wuT, wdT, hscr.ap())
    emit_stream_store(C, R, x_out)
    return C


def ffn_in_maps(inp, x_stream_percore):
    wg = f32c(inp["L0_ffn_w_gate"].T)
    wu = f32c(inp["L0_ffn_w_up"].T)
    wd = f32c(inp["L0_ffn_w_down"].T)
    maps = []
    for c in range(8):
        m = {"x_in": x_stream_percore[c], "wgT": wg, "wuT": wu, "wdT": wd}
        maps.append(m)
    return maps


def shard_stream(x):
    out = []
    for c in range(8):
        b, p = c // 4, c % 4
        out.append(np.ascontiguousarray(x[b].T[:, 1024 * p:1024 * (p + 1)], dtype=np.float32))
    return out


def unshard_stream(slices):
    x = np.zeros((2, 4096, 2048), np.float32)
    for c in range(8):
        b, p = c // 4, c % 4
        x[b, 1024 * p:1024 * (p + 1), :] = slices[c].T
    return x



NCH = 8


def emit_rms_inplace(C, sc, xn, tag="r"):
    b = C.block
    normW = C.const_tiles["normW"]
    onesb = C.const_tiles["ones_col_bf16"]
    s_sq = C.sem("rmssq" + tag)
    s_mm = C.sem("rmsmm" + tag)
    s_ev = C.sem("rmsev" + tag)
    s_r1 = C.sem("rmsr1" + tag)
    s_ap = C.sem("rmsap" + tag)
    sq = sc["sq"]
    psS1, psBm = sc["psS1"], sc["psBm"]
    m_sb, rs_sb = sc["m"], sc["rs"]
    Brs = sc["Bm"]
    sqb = [sq[i][:].bitcast(BF16) for i in range(2)]

    for tb in range(2):
        sq_base = s_sq.v
        mm_base = s_mm.v

        @b.scalar
        def _(scalar, tb=tb, sq_base=sq_base, mm_base=mm_base):
            for k in range(KD):
                g = tb * KD + k
                if g >= 2:
                    scalar.wait_ge(s_mm.h, mm_base + (k - 1) if tb == 0 else
                                   (mm_base + (k - 1) if k >= 2 else mm_base))
                scalar.activation(sqb[g % 2][:, 0:512],
                                  xn[:, k * T + tb * 512: k * T + tb * 512 + 512],
                                  AF.Square).then_inc(s_sq.h, 1)
        s_sq.inc(KD)

        @b.tensor
        def _(tensor, tb=tb, sq_base=sq_base):
            if tb == 1:
                tensor.wait_ge(s_ev.h, s_ev.v)
            for k in range(KD):
                g = tb * KD + k
                tensor.wait_ge(s_sq.h, sq_base + k + 1)
                tensor.matmul(psS1, onesb[:], sqb[g % 2][:, 0:512],
                              start=(k == 0), stop=(k == KD - 1)).then_inc(s_mm.h, 1)
        s_mm.inc(KD)

        sl = slice(tb * 512, (tb + 1) * 512)

        @b.vector
        def _(vector, sl=sl):
            vector.wait_ge(s_mm.h, s_mm.v)
            vector.tensor_scalar(m_sb[:, sl], psS1, 1.0 / 2048.0, None,
                                 ALU.mult).then_inc(s_ev.h, 1)
        s_ev.inc()

        @b.scalar
        def _(scalar, sl=sl):
            scalar.wait_ge(s_ev.h, s_ev.v)
            scalar.activation(rs_sb[:, sl], m_sb[:, sl], AF.Ln,
                              bias=sc["eps"][0:1, :]).then_inc(s_sq.h, 1)
            scalar.wait_ge(s_sq.h, s_sq.v + 1)
            scalar.activation(rs_sb[:, sl], rs_sb[:, sl], AF.Exp,
                              scale=-0.5).then_inc(s_sq.h, 1)
        s_sq.inc(2)

        @b.tensor
        def _(tensor, sl=sl, tb=tb):
            tensor.wait_ge(s_sq.h, s_sq.v)
            if tb == 1:
                tensor.wait_ge(s_ev.h, s_ev.v)
            tensor.matmul(psBm[:], sc["ones_row"][:], rs_sb[:, sl],
                          start=True, stop=True).then_inc(s_r1.h, 1)
        s_r1.inc()

        @b.vector
        def _(vector, sl=sl):
            vector.wait_ge(s_r1.h, s_r1.v)
            vector.tensor_copy(Brs[:, sl], psBm[:]).then_inc(s_ev.h, 1)
        s_ev.inc()

    @b.vector
    def _(vector):
        vector.wait_ge(s_ev.h, s_ev.v)
        for k in range(KD):
            for tb in range(2):
                sl = slice(k * T + tb * 512, k * T + tb * 512 + 512)
                bsl = slice(tb * 512, (tb + 1) * 512)
                last = vector.scalar_tensor_tensor(xn[:, sl], xn[:, sl],
                                                   normW[:, k:k + 1], Brs[:, bsl],
                                                   ALU.mult, ALU.mult)
        last.then_inc(s_ap.h, 1)
    s_ap.inc(1)
    C.mark(s_ap)


def emit_mamba(C, sc, R, xn, fscr, A, inWT, outWT, dram, pref="m"):
    b = C.block
    gateT, xs_f, Ypart = dram["gateT"], dram["xs_f"], dram["Ypart"]
    ynscr = dram["ynscr"]
    cc_s_in, cc_s_out = dram["cc_s_in"], dram["cc_s_out"]
    cc_o_in, cc_o_out = dram["cc_o_in"], dram["cc_o_out"]
    h_in, h_out = dram["h_in"], dram["h_out"]
    Lf = dram["Lf"]
    Ddiag_d = dram["Ddiag"]
    ps = C.ps8
    triu = C.const_tiles["triu_f32"]
    tril01 = C.const_tiles["tril01_bf16"]
    M04 = C.const_tiles["M04_bf16"]
    ones3 = C.const_tiles["ones3_bf16"]
    negones3 = C.const_tiles["negones3_bf16"]
    identb = C.const_tiles["ident_bf16"]
    convw = C.const_tiles["convw"]
    convb = C.const_tiles["convb"]
    dtb = C.const_tiles["dtb"]
    alog = C.const_tiles["alog"]
    ohprev = C.const_tiles["ohprev"]
    coefT = C.const_tiles["coefT"]
    masklt = C.const_tiles["masklt"]
    gnw = C.const_tiles["gnw"]
    onef = C.const_tiles["ones_col_f32x"]
    ones_row = sc["ones_row"]
    ones_col = sc["ones_col"]

    B_T = A[:, 0:1024]
    C_T = A[:, 1024:2048]
    Ddb = A[:, 2048:6144]
    pb0 = 6144
    wpbf = [A[:, pb0 + i * 2048: pb0 + (i + 1) * 2048] for i in range(2)]
    hwb = wpbf
    xcbuf = [A[:, pb0 + 4096 + i * 1056: pb0 + 4096 + (i + 1) * 1056] for i in range(2)]
    sgb = [A[:, pb0 + 6208 + i * 512: pb0 + 6208 + (i + 1) * 512] for i in range(2)]
    halo = A[:, pb0 + 7232: pb0 + 7334]
    halo_gs = [A[:, pb0 + 7360 + i * 102: pb0 + 7360 + (i + 1) * 102] for i in range(4)]
    cvo = [A[:, pb0 + 7808 + i * 1024: pb0 + 7808 + (i + 1) * 1024] for i in range(2)]
    cva = [A[:, pb0 + 9856 + i * 2048: pb0 + 9856 + (i + 1) * 2048].bitcast(F32)
           for i in range(2)]
    xsT = [A[:, pb0 + i * 4096: pb0 + (i + 1) * 4096] for i in range(2)]
    xsf2 = [A[:, pb0 + 8192 + i * 4096: pb0 + 8192 + (i + 1) * 4096] for i in range(2)]
    U = A[:, pb0 + 16384: pb0 + 20480]
    rUg = [A[:, pb0 + 20480 + i * 512: pb0 + 20480 + (i + 1) * 512] for i in range(2)]
    Ysbg = [A[:, pb0 + 21504 + i * 512: pb0 + 21504 + (i + 1) * 512] for i in range(2)]
    Slbg = [A[:, pb0 + 22528 + i * 512: pb0 + 22528 + (i + 1) * 512] for i in range(2)]
    Ebuf = [A[:, pb0 + 23552 + i * 512: pb0 + 23552 + (i + 1) * 512] for i in range(2)]
    Mbuf = [A[:, pb0 + 24576 + i * 512: pb0 + 24576 + (i + 1) * 512] for i in range(2)]
    Gmb = A[:, pb0 + 25600: pb0 + 25728]
    Grep = A[:, pb0 + 25728: pb0 + 26240]
    Btok = A[:, pb0 + 26240: pb0 + 26368]
    Ybb = A[:, pb0 + 26368: pb0 + 26880]
    Lflat3 = A[0:3, pb0 + 26880: pb0 + 30976]
    yb = [A[:, pb0 + 0: pb0 + 4096]] * 2
    gateb = [A[:, pb0 + 4096: pb0 + 8192]] * 2
    Sinbf = A[:, pb0 + 8192: pb0 + 12288]
    ynfq = A[:, pb0 + 12288: pb0 + 16384]
    Ycb = A[:, pb0 + 16384: pb0 + 16896]
    ycur = A[:, pb0 + 16896: pb0 + 20992]
    ynb = [A[:, pb0 + 20992 + i * 2048: pb0 + 20992 + (i + 1) * 2048] for i in range(2)]
    wobk = [A[:, pb0 + 25088 + i * 1024: pb0 + 25088 + (i + 1) * 1024] for i in range(2)]
    ynsrc = A[:, pb0 + 26880: pb0 + 30976]
    S_g_ap = A[:, pb0 + 12288: pb0 + 20480].bitcast(F32)

    w32 = C.sbuf(pref + "w32", [128, 2048], F32)
    S_loc = C.sbuf(pref + "S_loc", [128, 4096], F32)
    dt_hm = C.sbuf(pref + "dt_hm", [64, 1024], F32)
    LTs = C.sbuf(pref + "LTs", [128, 512], F32)
    offlat = C.sbuf(pref + "offlat", [1, 512], F32)
    off = C.sbuf(pref + "off", [1, 64], F32)
    Ltr = C.sbuf(pref + "Ltr", [1, 64], F32)
    Ltr2 = C.sbuf(pref + "Ltr2", [1, 64], F32)
    packb = C.sbuf(pref + "packb", [128, 128], F32)
    packT = C.sbuf(pref + "packT", [128, 128], F32)
    Lhm = C.sbuf(pref + "Lhm", [64, 128], F32)
    Ld1 = C.sbuf(pref + "Ld1", [64, 128], F32)
    Ls = [C.sbuf(pref + f"Ls{i}", [64, 128], BF16) for i in range(3)]
    cT = C.sbuf(pref + "cT", [128, 64], F32)
    rT = C.sbuf(pref + "rT", [128, 64], F32)
    dec = C.sbuf(pref + "dec", [128, 64], F32)
    fT = cT
    t64a = C.sbuf(pref + "t64a", [64, 128], F32)
    t64b = C.sbuf(pref + "t64b", [64, 128], F32)
    t64c = C.sbuf(pref + "t64c", [64, 128], F32)
    t64d = C.sbuf(pref + "t64d", [64, 128], F32)
    negA = C.sbuf(pref + "negA", [64, 1], F32)
    Wt = C.sbuf(pref + "Wt", [4, 64], F32)
    Wflat = C.sbuf(pref + "Wflat", [1, 256], F32)
    ofg = C.sbuf(pref + "ofg", [4, 64], F32)
    ssq = C.sbuf(pref + "ssq", [128, 8], F32)
    rsc = C.sbuf(pref + "rsc", [128, 1], F32)
    S_g = S_g_ap

    s_dd = C.sem(pref + "dd")
    s_ddc = C.sem(pref + "ddc")
    s_na = C.sem(pref + "na")

    @b.sync
    def _(sync):
        sync.dma_start(out=w32[:].rearrange("p (a o) -> p a o", a=16),
                       in_=Ddiag_d.rearrange("(a p) o -> p a o", p=128)[:, 0:16, :]).then_inc(s_dd.h, 16)
        sync.wait_ge(s_ddc.h, 1)
        sync.dma_start(out=w32[:].rearrange("p (a o) -> p a o", a=16),
                       in_=Ddiag_d.rearrange("(a p) o -> p a o", p=128)[:, 16:32, :]).then_inc(s_dd.h, 16)
    s_dd.inc(32)

    @b.vector
    def _(vector):
        vector.wait_ge(s_dd.h, 16)
        vector.tensor_copy(Ddb[:, 0:2048], w32[:]).then_inc(s_ddc.h, 1)
        vector.wait_ge(s_dd.h, 32)
        vector.tensor_copy(Ddb[:, 2048:4096], w32[:]).then_inc(s_ddc.h, 1)
        vector.wait_ge(s_na.h, 1)
        vector.tensor_scalar(negA[:], negA[:], -1.0, None, ALU.mult).then_inc(s_na.h, 1)
        vector.memset(off[:], 0.0)
        vector.memset(S_loc[:], 0.0).then_inc(s_na.h, 1)
    s_ddc.inc(2)

    @b.scalar
    def _(scalar):
        scalar.activation(negA[:], alog[:], AF.Exp).then_inc(s_na.h, 1)
    s_na.inc(3)

    s_hw = C.sem(pref + "hw")
    s_hwc = C.sem(pref + "hwc")
    s_hmm = C.sem(pref + "hmm")
    s_hev = C.sem(pref + "hev")
    s_hcc = C.sem(pref + "hcc")
    s_hcd = C.sem(pref + "hcd")
    chV = Chain(C, pref + "chV")

    @b.sync
    def _(sync):
        for oi in range(34):
            if oi == 0:
                sync.wait_ge(s_ddc.h, 2)
            if oi >= 1:
                sync.wait_ge(s_hwc.h, oi)
            sync.dma_start(out=w32[:].rearrange("p (k o) -> p k o", k=KD)[:, :, 0:128],
                           in_=inWT.rearrange("(k p) o -> p k o", p=128)[:, :, 4096 + oi * 128:4096 + oi * 128 + 128]
                           ).then_inc(s_hw.h, 16)
    s_hw.inc(16 * 34)

    @b.vector
    def _(vector):
        for oi in range(34):
            par = oi % 2
            vector.wait_ge(s_hw.h, 16 * (oi + 1))
            if oi >= 2:
                vector.wait_ge(s_hmm.h, oi - 1)
            vector.tensor_copy(hwb[par], w32[:, 0:2048]).then_inc(s_hwc.h, 1)
    s_hwc.inc(34)

    @b.tensor
    def _(tensor):
        for oi in range(34):
            par = oi % 2
            tensor.wait_ge(s_hwc.h, oi + 1)
            mm = None
            for k in range(KD):
                mm = tensor.matmul(ps[1][:, 3 * oi:3 * oi + 3],
                                   hwb[par][:, k * 128:(k + 1) * 128],
                                   xn[:, k * T + 1021: k * T + 1024],
                                   start=(k == 0), stop=(k == KD - 1))
            mm.then_inc(s_hmm.h, 1)
    s_hmm.inc(34)

    @b.vector
    def _(vector):
        vector.wait_ge(s_hmm.h, 34)
        vector.tensor_copy(halo, ps[1][:, 0:102]).then_inc(s_hev.h, 1)
    s_hev.inc(1)

    @b.gpsimd
    def _(gpsimd):
        gpsimd.wait_ge(s_hev.h, 1)
        gpsimd.dma_start(out=h_in, in_=halo).then_inc(s_hcd.h, 16)
        gpsimd.wait_ge(s_hcd.h, 16)
        gpsimd.collective_compute("AllGather", ALU.bypass,
                                  replica_groups=[[0, 1, 2, 3], [4, 5, 6, 7]],
                                  ins=[h_in], outs=[h_out]).then_inc(s_hcc.h, 1)
        gpsimd.wait_ge(s_hcc.h, 1)
        for j in range(4):
            gpsimd.dma_start(out=halo_gs[j], in_=h_out[j * 128:(j + 1) * 128, :]).then_inc(s_hcd.h, 16)
    s_hcc.inc(1)
    s_hcd.inc(16 * 5)

    @b.vector
    def _(vector):
        vector.wait_ge(s_hcd.h, 16 * 5)
        chV.dep(vector.tensor_scalar(halo, halo_gs[0], ohprev[:, 0:1], None, ALU.mult))
        for j in range(1, 4):
            chV.wait(vector)
            chV.dep(vector.scalar_tensor_tensor(halo, halo_gs[j], ohprev[:, j:j + 1],
                                                halo, ALU.mult, ALU.add))
        chV.wait(vector)
        vector.tensor_copy(halo_gs[0], halo).then_inc(s_hev.h, 1)
    s_hev.inc(1)
    haloC = halo_gs[0]

    for m in (s_hev, s_hcc, s_hcd, s_hmm, s_hwc, s_ddc, s_na, s_hw, s_dd):
        C.mark(m)
    C.barrier()

    ot_seq = list(range(32)) + [66] + list(range(32, 66))
    uorder = [(ot, tb) for ot in ot_seq for tb in range(2)]
    wchunks = [(o,) for o in ot_seq]
    ot_chunk = {o: (ci, 0) for ci, (o,) in enumerate(wchunks)}
    cat = {}
    gidx = xidx = 0
    for ui, (ot, tb) in enumerate(uorder):
        if ot < 32:
            cat[ui] = ("g", gidx)
            gidx += 1
        else:
            cat[ui] = ("x", xidx)
            xidx += 1

    s_pw = C.sem(pref + "pw")
    s_pwc = C.sem(pref + "pwc")
    s_pmm = C.sem(pref + "pmm")
    s_pevG = C.sem(pref + "pevG")
    s_pevX = C.sem(pref + "pevX")
    s_psp = [C.sem(pref + "psp0"), C.sem(pref + "psp1")]
    s_cv = C.sem(pref + "cv")
    s_cvsl = C.sem(pref + "cvsl")
    s_cvs = [C.sem(pref + "cvs0"), C.sem(pref + "cvs1")]
    chC = Chain(C, pref + "chC")

    @b.sync
    def _(sync):
        for ci, (ot,) in enumerate(wchunks):
            if ci >= 1:
                sync.wait_ge(s_pwc.h, ci)
            sync.dma_start(
                out=w32[:].rearrange("p (k o) -> p k o", k=KD),
                in_=inWT.rearrange("(k p) o -> p k o", p=128)[:, :, ot * 128:(ot + 1) * 128]
                ).then_inc(s_pw.h, 16)
    s_pw.inc(16 * len(wchunks))

    s_pevX_count = [0]

    def _emit_cast(vector, ci):
        par = ci % 2
        vector.wait_ge(s_pw.h, 16 * (ci + 1))
        if ci >= 2:
            vector.wait_ge(s_pmm.h, 2 * (ci - 1))
        vector.tensor_copy(wpbf[par][:, 0:2048],
                           w32[:, 0:2048]).then_inc(s_pwc.h, 1)

    def _emit_evac(vector, ui):
        ot, tb = uorder[ui]
        if ot < 32:
            return
        c, xi = cat[ui]
        pbk = 4 + ui % 2
        vector.wait_ge(s_pmm.h, ui + 1)
        if ot == 66:
            vector.tensor_copy(dt_hm[:, tb * 512:(tb + 1) * 512],
                               ps[pbk][0:64, :]).then_inc(s_pevX.h, 1)
            s_pevX_count[0] += 1
        else:
            oi = ot - 32
            xpar = oi % 2
            if oi >= 2 and tb == 0:
                vector.wait_ge(s_cvsl.h, oi - 1)
            vector.tensor_copy(xcbuf[xpar][:, 3 + tb * 512: 3 + tb * 512 + 512],
                               ps[pbk][:]).then_inc(s_pevX.h, 1)
            s_pevX_count[0] += 1
            if tb == 1:
                vector.wait_ge(s_pevX.h, s_pevX_count[0])
                chC.dep(vector.tensor_copy(xcbuf[xpar][:, 0:3],
                                           haloC[:, 3 * oi:3 * oi + 3]))
                chC.wait(vector)
                chC.dep(vector.tensor_scalar(cva[xpar][:], xcbuf[xpar][:, 0:1024],
                                             convw[:, 4 * oi:4 * oi + 1], None, ALU.mult))
                for kk in range(1, 4):
                    chC.wait(vector)
                    chC.dep(vector.scalar_tensor_tensor(
                        cva[xpar][:], xcbuf[xpar][:, kk:kk + 1024],
                        convw[:, 4 * oi + kk:4 * oi + kk + 1], cva[xpar][:],
                        ALU.mult, ALU.add))
                chC.wait(vector)
                vector.tensor_scalar(cva[xpar][:, 0:1], cva[xpar][:, 0:1], 1.0, None,
                                     ALU.mult).then_inc(s_cv.h, 1)

    @b.vector
    def _(vector):
        nch = len(wchunks)
        for ci in range(nch):
            _emit_cast(vector, ci)
            if ci >= 1:
                _emit_evac(vector, 2 * (ci - 1))
                _emit_evac(vector, 2 * (ci - 1) + 1)
        _emit_evac(vector, 2 * (nch - 1))
        _emit_evac(vector, 2 * (nch - 1) + 1)
    s_pwc.inc(len(wchunks))
    s_pevX.inc(70)
    s_cv.inc(34)

    @b.tensor
    def _(tensor):
        for ui, (ot, tb) in enumerate(uorder):
            ci, sub = ot_chunk[ot]
            par = ci % 2
            pbk = 4 + ui % 2
            tensor.wait_ge(s_pwc.h, ci + 1)
            if ui >= 2:
                c2, i2 = cat[ui - 2]
                if c2 == "g":
                    tensor.wait_ge(s_pevG.h, i2 + 1)
                else:
                    tensor.wait_ge(s_pevX.h, i2 + 1)
            mm = None
            for k in range(KD):
                mm = tensor.matmul(ps[pbk][:],
                                   wpbf[par][:, k * 128: k * 128 + 128],
                                   xn[:, k * T + tb * 512: k * T + tb * 512 + 512],
                                   start=(k == 0), stop=(k == KD - 1))
            mm.then_inc(s_pmm.h, 1)
    s_pmm.inc(len(uorder))

    @b.scalar
    def _(scalar):
        for ui, (ot, tb) in enumerate(uorder):
            if ot >= 32:
                continue
            c, gi = cat[ui]
            pbk = 4 + ui % 2
            scalar.wait_ge(s_pmm.h, ui + 1)
            if gi >= 2:
                scalar.wait_ge(s_psp[gi % 2].h, 16 * (gi // 2))
            scalar.activation(sgb[gi % 2], ps[pbk][:], AF.Silu).then_inc(s_pevG.h, 1)
    s_pevG.inc(64)

    @b.scalar
    def _(scalar):
        for oi in range(34):
            spb = oi % 2
            scalar.wait_ge(s_cv.h, oi + 1)
            if oi < 32:
                if oi >= 2:
                    scalar.wait_ge(s_cvs[spb].h, 16 * (oi // 2))
                scalar.activation(cvo[spb], cva[spb][:], AF.Silu,
                                  bias=convb[:, oi:oi + 1]).then_inc(s_cvsl.h, 1)
            elif oi == 32:
                scalar.activation(B_T, cva[spb][:], AF.Silu,
                                  bias=convb[:, oi:oi + 1]).then_inc(s_cvsl.h, 1)
            else:
                scalar.activation(C_T, cva[spb][:], AF.Silu,
                                  bias=convb[:, oi:oi + 1]).then_inc(s_cvsl.h, 1)
    s_cvsl.inc(34)

    @b.gpsimd
    def _(gpsimd):
        for ui, (ot, tb) in enumerate(uorder):
            if ot >= 32:
                continue
            c, gi = cat[ui]
            gpsimd.wait_ge(s_pevG.h, gi + 1)
            gpsimd.dma_start(out=gateT.rearrange("(o p) t -> p o t", p=128)[:, ot, tb * 512:(tb + 1) * 512],
                             in_=sgb[gi % 2]).then_inc(s_psp[gi % 2].h, 16)
        for oi in range(32):
            spb = oi % 2
            gpsimd.wait_ge(s_cvsl.h, oi + 1)
            gpsimd.dma_start(out=xs_f.rearrange("(o p) t -> p o t", p=128)[:, oi, :],
                             in_=cvo[spb]).then_inc(s_cvs[spb].h, 16)
    s_psp[0].inc(16 * 32)
    s_psp[1].inc(16 * 32)
    s_cvs[0].inc(16 * 16)
    s_cvs[1].inc(16 * 16)

    for m in (s_pmm, s_pevG, s_pevX, s_psp[0], s_psp[1], s_cv, s_cvsl,
              s_cvs[0], s_cvs[1], s_pwc, s_pw):
        C.mark(m)
    C.barrier()

    s_xt = [C.sem(pref + "xt0"), C.sem(pref + "xt1")]
    s_xf = [C.sem(pref + "xf0"), C.sem(pref + "xf1")]
    s_dc = C.sem(pref + "dc")
    s_ds = C.sem(pref + "ds")
    s_pkT = C.sem(pref + "pkT")
    s_pk = C.sem(pref + "pk")
    s_lmm = C.sem(pref + "lmm")
    s_lev = C.sem(pref + "lev")
    s_lf = C.sem(pref + "lf")
    s_lfd = C.sem(pref + "lfd")
    s_cre = C.sem(pref + "cre")
    s_urU = C.sem(pref + "urU")
    s_gmm = C.sem(pref + "gmm")
    s_gev = C.sem(pref + "gev")
    s_btT = C.sem(pref + "btT")
    s_dD = C.sem(pref + "dD")
    s_e = C.sem(pref + "e")
    s_m = C.sem(pref + "m")
    s_ia = C.sem(pref + "ia")
    s_e1 = C.sem(pref + "e1")
    s_e2 = C.sem(pref + "e2")
    s_su = C.sem(pref + "su")
    s_sbf = C.sem(pref + "sbf")
    s_ysp = [C.sem(pref + "ysp0"), C.sem(pref + "ysp1")]
    chD = Chain(C, pref + "chD")
    chL = Chain(C, pref + "chL2")
    chX = Chain(C, pref + "chX")

    for q in range(NCH):
        par = q % 2
        cq = slice(q * 128, (q + 1) * 128)

        @b.scalar
        def _(scalar, q=q, par=par):
            if q >= 2:
                scalar.wait_ge(s_urU.h, 2 * (q - 1))
            scalar.dma_start(out=xsT[par],
                             in_=xs_f[:, q * 128:(q + 1) * 128],
                             transpose=True).then_inc(s_xt[par].h, 16)
        s_xt[par].inc(16)

        @b.sync
        def _(sync, q=q, par=par):
            if q >= 2:
                sync.wait_ge(s_ia.h, 8 * (q - 1))
            sync.dma_start(out=xsf2[par].rearrange("p (o t) -> p o t", o=32),
                           in_=xs_f.rearrange("(o p) t -> p o t", p=128)[:, :, q * 128:(q + 1) * 128]
                           ).then_inc(s_xf[par].h, 16)
        s_xf[par].inc(16)

        @b.vector
        def _(vector, q=q):
            chD.dep(vector.tensor_scalar(t64a[:], dt_hm[:, q * 128:(q + 1) * 128],
                                         dtb[:], None, ALU.add))
            chD.wait(vector)
            chD.dep(vector.tensor_scalar(t64c[:], t64a[:], 0.0, None, ALU.max))
            chD.wait(vector)
            vector.scalar_tensor_tensor(t64b[:], t64c[:], 2.0, t64a[:],
                                        ALU.mult, ALU.subtract).then_inc(s_dc.h, 1)
        s_dc.inc(1)

        @b.scalar
        def _(scalar, q=q):
            scalar.wait_ge(s_dc.h, s_dc.v)
            scalar.activation(t64b[:], t64b[:], AF.Exp,
                              scale=-1.0).then_inc(s_ds.h, 1)
            scalar.wait_ge(s_ds.h, s_ds.v + 1)
            scalar.activation(t64b[:], t64b[:], AF.Ln,
                              bias=onef[0:64, :]).then_inc(s_ds.h, 1)
        s_ds.inc(2)

        @b.vector
        def _(vector, q=q):
            vector.wait_ge(s_ds.h, s_ds.v)
            chD.wait(vector)
            chD.dep(vector.tensor_tensor(packb[0:64, :], t64c[:], t64b[:], ALU.add))
            chD.wait(vector)
            chD.dep(vector.tensor_scalar(t64a[:], packb[0:64, :], negA[:], None, ALU.mult))
            chD.wait(vector)
            vector.tensor_copy(packb[64:128, :], t64a[:]).then_inc(s_dc.h, 1)
        s_dc.inc(1)

        @b.tensor
        def _(tensor, q=q):
            tensor.wait_ge(s_dc.h, s_dc.v)
            if q >= 1:
                tensor.wait_ge(s_pk.h, q)
            tensor.transpose(ps[0][:, 0:128], packb[:], C.const_tiles["ident_f32"][:]
                             ).then_inc(s_pkT.h, 1)
        s_pkT.inc(1)

        @b.vector
        def _(vector, q=q):
            vector.wait_ge(s_pkT.h, s_pkT.v)
            vector.tensor_copy(packT[:], ps[0][:, 0:128]).then_inc(s_pk.h, 1)
        s_pk.inc(1)

        lmm_b = s_lmm.v
        lev_b = s_lev.v

        @b.tensor
        def _(tensor, q=q, lmm_b=lmm_b, lev_b=lev_b):
            tensor.wait_ge(s_pk.h, s_pk.v)
            if q >= 1:
                tensor.wait_ge(s_lev.h, lev_b)
            tensor.matmul(ps[1][:, 0:64], triu[:], packT[:, 64:128],
                          start=True, stop=True).then_inc(s_lmm.h, 1)
        s_lmm.inc(1)

        @b.vector
        def _(vector, q=q):
            vector.wait_ge(s_lmm.h, s_lmm.v)
            vector.tensor_copy(LTs[:, q * 64:(q + 1) * 64], ps[1][:, 0:64]).then_inc(s_lev.h, 1)
        s_lev.inc(1)

        @b.tensor
        def _(tensor, q=q):
            tensor.wait_ge(s_lev.h, s_lev.v)
            tensor.matmul(ps[1][0:1, 0:64], ones_col[:], packT[:, 64:128],
                          start=True, stop=True).then_inc(s_lmm.h, 1)
        s_lmm.inc(1)

        @b.vector
        def _(vector, q=q):
            vector.wait_ge(s_lmm.h, s_lmm.v)
            vector.tensor_copy(Ltr[:], ps[1][0:1, 0:64]).then_inc(s_lev.h, 1)
        s_lev.inc(1)

        @b.tensor
        def _(tensor, q=q):
            tensor.wait_ge(s_lev.h, s_lev.v)
            tensor.matmul(ps[1][0:64, 0:128], packT[:, 64:128], triu[:],
                          start=True, stop=True).then_inc(s_lmm.h, 1)
        s_lmm.inc(1)

        @b.vector
        def _(vector, q=q):
            vector.wait_ge(s_lmm.h, s_lmm.v)
            if q >= 1:
                vector.wait_ge(s_lf.h, 96 * q)
            chL.dep(vector.tensor_copy(Lhm[:], ps[1][0:64, 0:128]))
            chL.wait(vector)
            chL.dep(vector.tensor_copy(Ls[0][:], Lhm[:]))
            chL.wait(vector)
            chL.dep(vector.tensor_tensor(Ld1[:], Lhm[:], Ls[0][:], ALU.subtract))
            chL.wait(vector)
            chL.dep(vector.tensor_copy(Ls[1][:], Ld1[:]))
            chL.wait(vector)
            chL.dep(vector.tensor_tensor(Ld1[:], Ld1[:], Ls[1][:], ALU.subtract))
            chL.wait(vector)
            vector.tensor_copy(Ls[2][:], Ld1[:]).then_inc(s_lev.h, 1)
        s_lev.inc(1)

        @b.scalar
        def _(scalar, q=q):
            scalar.wait_ge(s_lev.h, s_lev.v)
            if q >= 1:
                scalar.wait_ge(s_dD.h, 16 * q)
            for i in range(3):
                scalar.dma_start(out=Lf[i].rearrange("(h t) -> h t", h=32),
                                 in_=Ls[i][0:32, :]).then_inc(s_lfd.h, 16)
            s_lfd.inc(48)
            scalar.wait_ge(s_lfd.h, s_lfd.v)
            for i in range(3):
                scalar.dma_start(out=Lflat3[i:i + 1, :],
                                 in_=Lf[i:i + 1, :]).then_inc(s_lf.h, 16)
        s_lf.inc(48)

        @b.scalar
        def _(scalar, q=q):
            if q >= 1:
                scalar.wait_ge(s_e1.h, 64 * q)
            scalar.activation(cT[:], LTs[:, q * 64:(q + 1) * 64], AF.Exp).then_inc(s_cre.h, 1)
        s_cre.inc(1)

        @b.tensor
        def _(tensor, q=q):
            tensor.wait_ge(s_lev.h, s_lev.v)
            tensor.matmul(ps[1][:, 0:64], ones_row[:], Ltr[:],
                          start=True, stop=True).then_inc(s_lmm.h, 1)
        s_lmm.inc(1)

        @b.vector
        def _(vector, q=q):
            vector.wait_ge(s_lmm.h, s_lmm.v)
            if q >= 1:
                vector.wait_ge(s_urU.h, 2 * q)
            vector.tensor_tensor(rT[:], ps[1][:, 0:64], LTs[:, q * 64:(q + 1) * 64],
                                 ALU.subtract).then_inc(s_lev.h, 1)
        s_lev.inc(1)

        @b.scalar
        def _(scalar, q=q):
            scalar.wait_ge(s_lev.h, s_lev.v)
            scalar.activation(rT[:], rT[:], AF.Exp).then_inc(s_cre.h, 1)
            scalar.activation(Ltr2[:], Ltr[:], AF.Exp).then_inc(s_cre.h, 1)
        s_cre.inc(2)

        @b.vector
        def _(vector, q=q):
            vector.tensor_copy(offlat[0:1, q * 64:(q + 1) * 64],
                               off[:]).then_inc(s_dc.h, 1)
            vector.wait_ge(s_lev.h, s_lev.v)
            vector.wait_ge(s_dc.h, s_dc.v + 1)
            vector.tensor_tensor(off[:], off[:], Ltr[:], ALU.add).then_inc(s_dc.h, 1)
        s_dc.inc(2)

        @b.tensor
        def _(tensor, q=q):
            tensor.wait_ge(s_cre.h, s_cre.v)
            tensor.wait_ge(s_lev.h, s_lev.v)
            tensor.matmul(ps[1][:, 0:64], ones_row[:], Ltr2[:],
                          start=True, stop=True).then_inc(s_lmm.h, 1)
        s_lmm.inc(1)

        @b.vector
        def _(vector, q=q):
            vector.wait_ge(s_lmm.h, s_lmm.v)
            if q >= 1:
                vector.wait_ge(s_su.h, 8 * q)
            vector.tensor_copy(dec[:], ps[1][:, 0:64]).then_inc(s_lev.h, 1)
        s_lev.inc(1)


        @b.vector
        def _(vector, q=q, par=par):
            vector.wait_ge(s_xt[par].h, 16 * (q // 2 + 1))
            vector.wait_ge(s_pk.h, s_pk.v)
            vector.wait_ge(s_cre.h, s_cre.v)
            if q >= 2:
                vector.wait_ge(s_ia.h, 8 * (q - 1))
            for h in range(64):
                uu = vector.tensor_scalar(U[:, h * 64:(h + 1) * 64],
                                          xsT[par][:, h * 64:(h + 1) * 64],
                                          packT[:, h:h + 1], None, ALU.mult)
            uu.then_inc(s_urU.h, 2)
        s_urU.inc(2)

        @b.tensor
        def _(tensor, q=q):
            if q >= 1:
                tensor.wait_ge(s_gev.h, 2 * q)
            tensor.matmul(ps[2][:, 0:128], B_T[:, cq], C_T[:, cq],
                          start=True, stop=True).then_inc(s_gmm.h, 1)
            tensor.wait_ge(s_pk.h, s_pk.v)
            tensor.transpose(ps[0][:, 0:64].bitcast(BF16), B_T[:, cq],
                             identb[:]).then_inc(s_btT.h, 1)
        s_gmm.inc(1)
        s_btT.inc(1)

        @b.vector
        def _(vector, q=q):
            vector.wait_ge(s_gmm.h, s_gmm.v)
            chX.dep(vector.tensor_tensor(Gmb, ps[2][:, 0:128], tril01[:], ALU.mult))
            chX.wait(vector)
            vector.tensor_copy(Grep[:, 0:128], Gmb)
            vector.tensor_copy(Grep[:, 128:256], Gmb)
            vector.tensor_copy(Grep[:, 256:384], Gmb)
            cp = vector.tensor_copy(Grep[:, 384:512], Gmb)
            cp.then_inc(s_gev.h, 1)
            vector.wait_ge(s_btT.h, s_btT.v)
            vector.tensor_copy(Btok, ps[0][:, 0:64].bitcast(BF16)).then_inc(s_gev.h, 1)
        s_gev.inc(2)

        for g2 in range(8):
            if g2 == 4:
                @b.scalar
                def _(scalar, q=q):
                    scalar.wait_ge(s_dD.h, 16 * q + 8)
                    for i in range(3):
                        scalar.dma_start(out=Lf[i].rearrange("(h t) -> h t", h=32),
                                         in_=Ls[i][32:64, :]).then_inc(s_lfd.h, 16)
                    s_lfd.inc(48)
                    scalar.wait_ge(s_lfd.h, s_lfd.v)
                    for i in range(3):
                        scalar.dma_start(out=Lflat3[i:i + 1, :],
                                         in_=Lf[i:i + 1, :]).then_inc(s_lf.h, 16)
                s_lf.inc(48)
            for half in range(2):
                bb = 2 * g2 + half
                dpar = bb % 2

                @b.tensor
                def _(tensor, q=q, bb=bb, dpar=dpar, g2=g2):
                    tensor.wait_ge(s_lf.h, 96 * q + (48 if bb < 8 else 96))
                    if 16 * q + bb >= 2:
                        tensor.wait_ge(s_e.h, 16 * q + bb - 1)
                    if bb == 0:
                        tensor.wait_ge(s_gev.h, s_gev.v)
                    if bb == 1:
                        tensor.wait_ge(s_lev.h, s_lev.v)
                    br = bb % 8
                    tensor.matmul(ps[dpar][:, 0:512], ones3[:],
                                  Lflat3[0:3, br * 512:(br + 1) * 512],
                                  start=True, stop=False)
                    for jj in range(4):
                        tensor.matmul(ps[dpar][:, jj * 128:(jj + 1) * 128],
                                      Lflat3[0:3, (4 * br + jj) * 128:(4 * br + jj) * 128 + 128],
                                      negones3[:], start=False, stop=False)
                    tensor.matmul(ps[dpar][:, 0:512], identb[:], M04[:],
                                  start=False, stop=True).then_inc(s_dD.h, 1)
                s_dD.inc(1)

                @b.scalar
                def _(scalar, q=q, bb=bb, dpar=dpar):
                    scalar.wait_ge(s_dD.h, s_dD.v)
                    if 16 * q + bb >= 2:
                        scalar.wait_ge(s_m.h, 16 * q + bb - 1)
                    scalar.activation(Ebuf[dpar], ps[dpar][:, 0:512], AF.Exp).then_inc(s_e.h, 1)
                s_e.inc(1)

                @b.vector
                def _(vector, q=q, bb=bb, dpar=dpar):
                    vector.wait_ge(s_e.h, s_e.v)
                    vector.wait_ge(s_gev.h, s_gev.v)
                    if 16 * q + bb >= 2:
                        vector.wait_ge(s_ia.h, 8 * q + (bb - 2) // 2 + (1 if bb >= 2 else 0))
                    vector.tensor_tensor(Mbuf[dpar], Ebuf[dpar], Grep, ALU.mult).then_inc(s_m.h, 1)
                s_m.inc(1)

            @b.vector
            def _(vector, q=q, g2=g2):
                gg = 8 * q + g2
                vector.wait_ge(s_urU.h, s_urU.v)
                if gg >= 2:
                    vector.wait_ge(s_ia.h, gg - 1)
                rr = None
                for h in range(8):
                    hh = 8 * g2 + h
                    rr = vector.tensor_scalar(rUg[g2 % 2][:, h * 64:(h + 1) * 64],
                                              U[:, hh * 64:(hh + 1) * 64],
                                              rT[:, hh:hh + 1], None, ALU.mult)
                if q > 0:
                    rr = vector.tensor_copy(Slbg[g2 % 2],
                                            S_loc[:, g2 * 512:(g2 + 1) * 512])
                rr.then_inc(s_sbf.h, 1)
            s_sbf.inc(1)

            @b.tensor
            def _(tensor, q=q, g2=g2, par=par):
                tensor.wait_ge(s_m.h, s_m.v)
                tensor.wait_ge(s_sbf.h, s_sbf.v)
                tensor.wait_ge(s_xf[par].h, 16 * (q // 2 + 1))
                tensor.wait_ge(s_gev.h, s_gev.v)
                pA = 2 + g2 % 2
                pB = 4 + g2 % 2
                pS = 6 + g2 % 2
                if 8 * q + g2 >= 2:
                    tensor.wait_ge(s_e2.h, 8 * q + g2 - 1)
                    tensor.wait_ge(s_e1.h, 8 * (8 * q + g2 - 1))
                    tensor.wait_ge(s_su.h, 8 * (8 * q + g2 - 1))
                for h in range(8):
                    hh = 8 * g2 + h
                    bt = (2 * g2 + h // 4) % 2
                    tensor.matmul(ps[pA][:, h * 64:(h + 1) * 64],
                                  Mbuf[bt][:, (h % 4) * 128:(h % 4) * 128 + 128],
                                  U[:, hh * 64:(hh + 1) * 64],
                                  start=(h == 0), stop=False)
                for i in range(4):
                    ii = 4 * g2 + i
                    tensor.matmul(ps[pA][:, i * 128:(i + 1) * 128],
                                  xsf2[par][:, ii * 128:(ii + 1) * 128],
                                  Ddb[:, ii * 128:(ii + 1) * 128],
                                  start=False, stop=(i == 3))
                if q > 0:
                    tensor.matmul(ps[pB][:, 0:512], C_T[:, cq],
                                  Slbg[g2 % 2],
                                  start=True, stop=True)
                tensor.matmul(ps[pS][:, 0:512], Btok,
                              rUg[g2 % 2],
                              start=True, stop=True).then_inc(s_ia.h, 1)
            s_ia.inc(1)

            @b.scalar
            def _(scalar, q=q, g2=g2):
                if q > 0:
                    scalar.wait_ge(s_ia.h, s_ia.v)
                    scalar.wait_ge(s_cre.h, s_cre.v)
                    pB = 4 + g2 % 2
                    if 8 * q + g2 >= 2:
                        scalar.wait_ge(s_e2.h, 8 * q + g2 - 1)
                    ee = None
                    for h in range(8):
                        hh = 8 * g2 + h
                        ee = scalar.activation(Ybb[h * 64:(h + 1) * 64] if False else
                                               Ybb[:, 0:0] if False else
                                               _col64(Ybb, h),
                                               ps[pB][:, h * 64:(h + 1) * 64],
                                               AF.Identity, scale=cT[:, hh:hh + 1])
                    ee.then_inc(s_e1.h, 8)
                else:
                    scalar.wait_ge(s_ia.h, s_ia.v)
                    scalar.nop().then_inc(s_e1.h, 8)
            s_e1.inc(8)

            @b.vector
            def _(vector, q=q, g2=g2):
                vector.wait_ge(s_ia.h, s_ia.v)
                vector.wait_ge(s_e1.h, s_e1.v)
                pA = 2 + g2 % 2
                pS = 6 + g2 % 2
                gg = 8 * q + g2
                if gg >= 2:
                    vector.wait_ge(s_ysp[g2 % 2].h, 16 * ((gg - 2) // 2 + 1))
                if q > 0:
                    vector.tensor_tensor(Ysbg[g2 % 2],
                                         ps[pA][:, 0:512], Ybb, ALU.add).then_inc(s_e2.h, 1)
                else:
                    vector.tensor_copy(Ysbg[g2 % 2],
                                       ps[pA][:, 0:512]).then_inc(s_e2.h, 1)
                uu = None
                if q > 0:
                    for h in range(8):
                        hh = 8 * g2 + h
                        uu = vector.scalar_tensor_tensor(
                            S_loc[:, hh * 64:(hh + 1) * 64],
                            S_loc[:, hh * 64:(hh + 1) * 64],
                            dec[:, hh:hh + 1],
                            ps[pS][:, h * 64:(h + 1) * 64],
                            ALU.mult, ALU.add)
                    uu.then_inc(s_su.h, 8)
                else:
                    for h in range(8):
                        hh = 8 * g2 + h
                        uu = vector.tensor_copy(S_loc[:, hh * 64:(hh + 1) * 64],
                                                ps[pS][:, h * 64:(h + 1) * 64])
                    uu.then_inc(s_su.h, 8)
            s_e2.inc(1)
            s_su.inc(8)

        @b.gpsimd
        def _(gpsimd, q=q, par=par):
            for g2 in range(8):
                gg = 8 * q + g2
                gpsimd.wait_ge(s_e2.h, gg + 1)
                gpsimd.dma_start(out=Ypart[q * 128:(q + 1) * 128, g2 * 512:(g2 + 1) * 512],
                                 in_=Ysbg[g2 % 2]).then_inc(s_ysp[g2 % 2].h, 16)
        s_ysp[0].inc(16 * 4)
        s_ysp[1].inc(16 * 4)

    s_pub = C.sem(pref + "pub")
    s_cc2 = C.sem(pref + "cc2")

    @b.gpsimd
    def _(gpsimd):
        gpsimd.wait_ge(s_su.h, s_su.v)
        gpsimd.wait_ge(s_dc.h, s_dc.v)
        gpsimd.dma_start(out=cc_s_in, in_=S_loc[:]).then_inc(s_pub.h, 16)
        gpsimd.dma_start(out=cc_o_in, in_=off[:]).then_inc(s_pub.h, 16)
        gpsimd.wait_ge(s_pub.h, 32)
        gpsimd.collective_compute("AllGather", ALU.bypass,
                                  replica_groups=[[0, 1, 2, 3], [4, 5, 6, 7]],
                                  ins=[cc_s_in], outs=[cc_s_out]).then_inc(s_cc2.h, 1)
        gpsimd.collective_compute("AllGather", ALU.bypass,
                                  replica_groups=[[0, 1, 2, 3], [4, 5, 6, 7]],
                                  ins=[cc_o_in], outs=[cc_o_out]).then_inc(s_cc2.h, 1)
    s_pub.inc(32)
    s_cc2.inc(2)
    for m in (s_cc2, s_pub, s_ysp[0], s_ysp[1], s_e2, s_su, s_ia, s_lf, s_lfd,
              s_dD, s_e, s_m, s_urU, s_sbf, s_lev, s_lmm, s_cre, s_dc, s_ds,
              s_xt[0], s_xt[1], s_xf[0], s_xf[1], s_gev, s_gmm, s_pk, s_pkT, s_btT):
        C.mark(m)
    C.barrier()

    s_cbo = C.sem(pref + "cbo")
    s_cbw = C.sem(pref + "cbw")
    s_cbg = C.sem(pref + "cbg")
    s_cbm = C.sem(pref + "cbm")
    s_cbe = C.sem(pref + "cbe")
    chB = Chain(C, pref + "chB")

    @b.sync
    def _(sync):
        sync.dma_start(out=ofg[:], in_=cc_o_out).then_inc(s_cbo.h, 16)
    s_cbo.inc(16)

    @b.tensor
    def _(tensor):
        tensor.wait_ge(s_cbo.h, 16)
        tensor.matmul(ps[0][0:4, 0:64], coefT[:], ofg[:], start=True, stop=True).then_inc(s_cbm.h, 1)
    s_cbm.inc(1)

    @b.scalar
    def _(scalar):
        scalar.wait_ge(s_cbm.h, 1)
        scalar.activation(Wt[:], ps[0][0:4, 0:64], AF.Exp).then_inc(s_cbe.h, 1)
    s_cbe.inc(1)

    @b.vector
    def _(vector):
        vector.wait_ge(s_cbe.h, 1)
        vector.tensor_scalar(Wt[:], Wt[:], masklt[:], None, ALU.mult).then_inc(s_cbe.h, 1)
    s_cbe.inc(1)

    @b.scalar
    def _(scalar):
        scalar.wait_ge(s_cbe.h, 2)
        for j in range(4):
            scalar.dma_start(out=Wflat[0:1, j * 64:(j + 1) * 64],
                             in_=Wt[j:j + 1, :]).then_inc(s_cbw.h, 16)
    s_cbw.inc(64)

    @b.gpsimd
    def _(gpsimd):
        for j in range(4):
            if j >= 1:
                gpsimd.wait_ge(s_cbe.h, 2 + j)
            gpsimd.dma_start(out=S_g[:], in_=cc_s_out[j * 128:(j + 1) * 128, :]).then_inc(s_cbg.h, 16)
    s_cbg.inc(64)

    @b.tensor
    def _(tensor):
        tensor.wait_ge(s_cbw.h, 64)
        for j in range(4):
            if j >= 1:
                tensor.wait_ge(chB.s.h, j)
            tensor.matmul(ps[1][:, 0:64], ones_row[:], Wflat[0:1, j * 64:(j + 1) * 64],
                          start=True, stop=True).then_inc(s_cbm.h, 1)
    s_cbm.inc(4)

    @b.vector
    def _(vector):
        for j in range(4):
            vector.wait_ge(s_cbm.h, 1 + j + 1)
            vector.wait_ge(s_cbg.h, 16 * (j + 1))
            chB.dep(vector.tensor_copy(dec[:], ps[1][:, 0:64]))
            chB.wait(vector)
            uu = None
            for h in range(64):
                if j == 0:
                    uu = vector.tensor_scalar(S_loc[:, h * 64:(h + 1) * 64],
                                              S_g[:, h * 64:(h + 1) * 64],
                                              dec[:, h:h + 1], None, ALU.mult)
                else:
                    uu = vector.scalar_tensor_tensor(S_loc[:, h * 64:(h + 1) * 64],
                                                     S_g[:, h * 64:(h + 1) * 64],
                                                     dec[:, h:h + 1],
                                                     S_loc[:, h * 64:(h + 1) * 64],
                                                     ALU.mult, ALU.add)
            uu.then_inc(s_cbe.h, 1)
        vector.wait_ge(s_cbe.h, s_cbe.v + 4)
        cp = None
        for i in range(8):
            cp = vector.tensor_copy(Sinbf[:, i * 512:(i + 1) * 512],
                                    S_loc[:, i * 512:(i + 1) * 512])
        cp.then_inc(s_cbe.h, 1)
    s_cbe.inc(5)
    C.mark(s_cbe)
    C.mark(s_cbm)
    C.mark(s_cbo)
    C.mark(s_cbw)
    C.mark(s_cbg)
    C.barrier()

    s_gb = [C.sem(pref + "gb0"), C.sem(pref + "gb1")]
    s_yb = [C.sem(pref + "yb0"), C.sem(pref + "yb1")]
    s_f1 = C.sem(pref + "f1")
    s_pc = C.sem(pref + "pc")
    s_yc = C.sem(pref + "yc")
    s_ya = C.sem(pref + "ya")
    s_rs2 = C.sem(pref + "rs2")
    s_sqw = C.sem(pref + "sqw")
    s_yn = C.sem(pref + "yn")
    s_yt = C.sem(pref + "yt")
    s_yte = C.sem(pref + "yte")
    s_ow = C.sem(pref + "ow")
    s_owc = C.sem(pref + "owc")
    s_oyd = [C.sem(pref + "oyd0"), C.sem(pref + "oyd1")]
    s_ysq = C.sem(pref + "ysq")
    s_omm = C.sem(pref + "omm")
    s_ore = C.sem(pref + "ore")
    chY = Chain(C, pref + "chY")
    chR = Chain(C, pref + "chR")

    for grp in range(2):
        for qi in range(4):
            q = grp * 4 + qi
            par = q % 2
            cq = slice(q * 128, (q + 1) * 128)

            @b.scalar
            def _(scalar, q=q, par=par):
                if q >= 1:
                    scalar.wait_ge(s_ya.h, 8 * q)
                scalar.dma_start(out=gateb[par],
                                 in_=gateT[:, q * 128:(q + 1) * 128],
                                 transpose=True).then_inc(s_gb[par].h, 16)
            s_gb[par].inc(16)

            @b.sync
            def _(sync, q=q, par=par):
                if q >= 1:
                    sync.wait_ge(s_ya.h, 8 * q)
                sync.dma_start(out=yb[par], in_=Ypart[q * 128:(q + 1) * 128, :]).then_inc(s_yb[par].h, 16)
            s_yb[par].inc(16)

            @b.tensor
            def _(tensor, q=q):
                if q % 4 == 0 and q >= 4:
                    tensor.wait_ge(s_ore.h, 16 * (q // 4))
                if q >= 1:
                    tensor.wait_ge(s_f1.h, 2 * q)
                tensor.matmul(ps[2][:, 0:64], ones_row[:], offlat[0:1, q * 64:(q + 1) * 64],
                              start=True, stop=True).then_inc(s_f1.h, 1)
            s_f1.inc(1)

            @b.vector
            def _(vector, q=q):
                vector.wait_ge(s_f1.h, s_f1.v)
                if q >= 1:
                    vector.wait_ge(s_yc.h, 8 * q)
                vector.tensor_tensor(fT[:], ps[2][:, 0:64], LTs[:, q * 64:(q + 1) * 64],
                                     ALU.add).then_inc(s_f1.h, 1)
            s_f1.inc(1)

            @b.scalar
            def _(scalar, q=q):
                scalar.wait_ge(s_f1.h, s_f1.v)
                scalar.activation(fT[:], fT[:], AF.Exp).then_inc(s_rs2.h, 1)
            s_rs2.inc(1)

            for g2 in range(8):
                @b.tensor
                def _(tensor, q=q, g2=g2, cq=cq):
                    pc = g2 % 2
                    if 8 * q + g2 >= 2:
                        tensor.wait_ge(s_yc.h, 8 * q + g2 - 1)
                    tensor.matmul(ps[pc][:, 0:512], C_T[:, cq],
                                  Sinbf[:, g2 * 512:(g2 + 1) * 512],
                                  start=True, stop=True).then_inc(s_pc.h, 1)
                s_pc.inc(1)

                @b.scalar
                def _(scalar, q=q, g2=g2):
                    pc = g2 % 2
                    scalar.wait_ge(s_pc.h, s_pc.v)
                    scalar.wait_ge(s_rs2.h, s_rs2.v)
                    if 8 * q + g2 >= 1:
                        scalar.wait_ge(s_ya.h, 8 * q + g2)
                    ee = None
                    for h in range(8):
                        hh = 8 * g2 + h
                        ee = scalar.activation(_col64(Ycb, h), ps[pc][:, h * 64:(h + 1) * 64],
                                               AF.Identity, scale=fT[:, hh:hh + 1])
                    ee.then_inc(s_yc.h, 1)
                s_yc.inc(1)

                @b.vector
                def _(vector, q=q, g2=g2, par=par):
                    vector.wait_ge(s_yc.h, s_yc.v)
                    vector.wait_ge(s_yb[par].h, 16 * (q // 2 + 1))
                    vector.wait_ge(s_gb[par].h, 16 * (q // 2 + 1))
                    sl = slice(g2 * 512, (g2 + 1) * 512)
                    chY.dep(vector.tensor_tensor(ycur[:, sl], yb[par][:, sl], Ycb, ALU.add))
                    chY.wait(vector)
                    vector.tensor_tensor(ycur[:, sl], ycur[:, sl], gateb[par][:, sl],
                                         ALU.mult).then_inc(s_ya.h, 1)
                s_ya.inc(1)

            @b.scalar
            def _(scalar, q=q):
                scalar.wait_ge(s_ya.h, 8 * (q + 1))
                if q >= 1:
                    scalar.wait_ge(s_yn.h, q)
                base = s_sqw.v
                for i in range(8):
                    if i >= 2:
                        scalar.wait_ge(s_sqw.h, base + i - 1)
                    scalar.activation(sc["sq"][i % 2][:, 0:512],
                                      ycur[:, i * 512:(i + 1) * 512], AF.Square,
                                      accum_out=ssq[:, i:i + 1]).then_inc(s_sqw.h, 1)
                    s_sqw.inc(1)
                scalar.wait_ge(s_sqw.h, s_sqw.v)
                scalar.nop().then_inc(s_rs2.h, 1)
            s_rs2.inc(1)

            @b.vector
            def _(vector, q=q):
                vector.wait_ge(s_rs2.h, s_rs2.v)
                chR.dep(vector.tensor_reduce(rsc[:], ssq[:].rearrange("p (a b) -> p a b", a=1),
                                             axis=AX.X, op=ALU.add))
                chR.wait(vector)
                chR.dep(vector.tensor_scalar(rsc[:], rsc[:], 1.0 / 4096.0, None, ALU.mult))
                chR.wait(vector)
                vector.tensor_scalar(rsc[:], rsc[:], 1e-5, None, ALU.add).then_inc(s_rs2.h, 1)
            s_rs2.inc(1)

            @b.scalar
            def _(scalar, q=q):
                scalar.wait_ge(s_rs2.h, s_rs2.v)
                scalar.activation(rsc[:], rsc[:], AF.Ln).then_inc(s_rs2.h, 1)
                scalar.wait_ge(s_rs2.h, s_rs2.v + 1)
                scalar.activation(rsc[:], rsc[:], AF.Exp, scale=-0.5).then_inc(s_rs2.h, 1)
            s_rs2.inc(2)

            @b.vector
            def _(vector, q=q, qi=qi):
                vector.wait_ge(s_rs2.h, s_rs2.v)
                if q >= 1:
                    vector.wait_ge(s_yt.h, 32 * q)
                yy = None
                for i in range(8):
                    yy = vector.tensor_scalar(ynsrc[:, i * 512:(i + 1) * 512],
                                              ycur[:, i * 512:(i + 1) * 512],
                                              rsc[:], None, ALU.mult)
                yy.then_inc(s_yn.h, 1)
            s_yn.inc(1)

            @b.tensor
            def _(tensor, q=q, qi=qi):
                tensor.wait_ge(s_yn.h, s_yn.v)
                for i in range(32):
                    g = (q * 32 + i)
                    if g >= 2:
                        tensor.wait_ge(s_yte.h, g - 1)
                    tensor.transpose(ps[3][:, 0:64].bitcast(BF16) if g % 2 == 0 else
                                     ps[6][:, 0:64].bitcast(BF16),
                                     ynsrc[:, i * 128:(i + 1) * 128],
                                     identb[:]).then_inc(s_yt.h, 1)
            s_yt.inc(32)

            @b.vector
            def _(vector, q=q, qi=qi):
                if q >= 1:
                    vector.wait_ge(s_ysq.h, 16 * q)
                for i in range(32):
                    g = (q * 32 + i)
                    vector.wait_ge(s_yt.h, g + 1)
                    src = ps[3][:, 0:64].bitcast(BF16) if g % 2 == 0 else ps[6][:, 0:64].bitcast(BF16)
                    vector.tensor_copy(ynfq[:, i * 128:(i + 1) * 128],
                                       src).then_inc(s_yte.h, 1)
            s_yte.inc(32)

            @b.gpsimd
            def _(gpsimd, q=q):
                gpsimd.wait_ge(s_yte.h, 32 * (q + 1))
                gpsimd.dma_start(out=ynscr.rearrange("(k p) t -> p k t", p=128)[:, :, q * 128:(q + 1) * 128],
                                 in_=ynfq[:].rearrange("p (k t) -> p k t", k=32)
                                 ).then_inc(s_ysq.h, 16)
            s_ysq.inc(16)

        @b.sync
        def _(sync, grp=grp):
            sync.wait_ge(s_ysq.h, 64 * (grp + 1))
            for dgrp in range(2):
                for kb in range(8):
                    gi = (grp * 2 + dgrp) * 8 + kb
                    par2 = gi % 2
                    if gi >= 2:
                        sync.wait_ge(s_omm.h, 4 * (gi - 1))
                    sync.dma_start(
                        out=ynb[par2][:].rearrange("p (k t) -> p k t", k=4),
                        in_=ynscr.rearrange("(k p) t -> p k t", p=128)[:, kb * 4:(kb + 1) * 4,
                                                                       grp * 512:(grp + 1) * 512]
                        ).then_inc(s_oyd[par2].h, 16)
        for gi in range((grp * 2) * 8, (grp * 2 + 2) * 8):
            s_oyd[gi % 2].inc(16)

        @b.scalar
        def _(scalar, grp=grp):
            pass

        @b.gpsimd
        def _(gpsimd, grp=grp):
            for dgrp in range(2):
                for k in range(32):
                    gi = (grp * 2 + dgrp) * 32 + k
                    par2 = gi % 2
                    if gi >= 1:
                        gpsimd.wait_ge(s_owc.h, gi)
                    gpsimd.dma_start(
                        out=w32[:, 0:1024],
                        in_=outWT[k * 128:(k + 1) * 128, dgrp * 1024:(dgrp + 1) * 1024]
                        ).then_inc(s_ow.h, 16)
                    gpsimd.wait_ge(s_ow.h, 16 * (gi + 1))
        for _i in range(64):
            s_ow.inc(16)

        @b.vector
        def _(vector, grp=grp):
            for dgrp in range(2):
                gd = grp * 2 + dgrp
                for k in range(32):
                    gi = gd * 32 + k
                    par2 = gi % 2
                    vector.wait_ge(s_ow.h, 16 * (gi + 1))
                    if gi >= 2:
                        vector.wait_ge(s_omm.h, gi - 1)
                    vector.tensor_scalar(wobk[par2], w32[:, 0:1024], gnw[:, k:k + 1],
                                         None, ALU.mult).then_inc(s_owc.h, 1)
                vector.wait_ge(s_omm.h, 32 * (gd + 1))
                for d8 in range(8):
                    dot = dgrp * 8 + d8
                    sl = slice(dot * T + grp * 512, dot * T + grp * 512 + 512)
                    vector.tensor_tensor(R[:, sl], R[:, sl], ps[d8][:, 0:512],
                                         ALU.add).then_inc(s_ore.h, 1)
        s_owc.inc(64)
        s_ore.inc(16)

        @b.tensor
        def _(tensor, grp=grp):
            for dgrp in range(2):
                gd = grp * 2 + dgrp
                if gd >= 1:
                    tensor.wait_ge(s_ore.h, 8 * gd)
                for k in range(32):
                    gi = gd * 32 + k
                    par2 = gi % 2
                    kpar = (gd * 8 + k // 4) % 2
                    tensor.wait_ge(s_owc.h, gi + 1)
                    tensor.wait_ge(s_oyd[kpar].h, 16 * ((gd * 8 + k // 4) // 2 + 1))
                    mm = None
                    for d8 in range(8):
                        mm = tensor.matmul(ps[d8][:, 0:512],
                                           wobk[par2][:, d8 * 128:(d8 + 1) * 128],
                                           ynb[kpar][:, (k % 4) * 512:(k % 4) * 512 + 512],
                                           start=(k == 0), stop=(k == 31))
                    mm.then_inc(s_omm.h, 4 if False else 1)
        s_omm.inc(64)

    for m in (s_ore, s_omm, s_owc, s_ow, s_yte, s_yt, s_yn, s_rs2, s_sqw, s_ya, s_yc, s_ysq,
              s_oyd[0], s_oyd[1], s_pc, s_f1, s_gb[0], s_gb[1], s_yb[0], s_yb[1]):
        C.mark(m)
    C.barrier()


def _col64(ap, h):
    return ap[:, h * 64:(h + 1) * 64]


def _w32k(ap, k):
    return ap[:, k * 128:(k + 1) * 128]


def _w32c(ap, k):
    return ap[:, k * 64:(k + 1) * 64]


def build_mamba_launch(debug=False):
    C = Ctx()
    nc = C.nc
    dbg = dict(kind="ExternalOutput") if debug else {}
    x_in = nc.dram_tensor("x_in", [DM, T], F32, kind="ExternalInput").ap()
    x_out = nc.dram_tensor("x_out", [DM, T], F32, kind="ExternalOutput").ap()
    inWT = nc.dram_tensor("inWT", [DM, 8576], F32, kind="ExternalInput").ap()
    outWT = nc.dram_tensor("outWT", [4096, DM], F32, kind="ExternalInput").ap()
    dram = {
        "Lf": nc.dram_tensor("Lf_scr", [3, 4096], BF16).ap(),
        "gateT": nc.dram_tensor("gateT", [4096, T], BF16, **dbg).ap(),
        "xs_f": nc.dram_tensor("xs_f_scr", [4096, T], BF16, **dbg).ap(),
        "Ypart": nc.dram_tensor("Ypart", [T, 4096], BF16, **dbg).ap(),
        "ynscr": nc.dram_tensor("ynscr", [4096, T], BF16, **dbg).ap(),
        "cc_s_in": nc.dram_tensor("cc_s_in", [128, 4096], F32).ap(),
        "cc_s_out": nc.dram_tensor("cc_s_out", [512, 4096], F32).ap(),
        "cc_o_in": nc.dram_tensor("cc_o_in", [1, 64], F32).ap(),
        "cc_o_out": nc.dram_tensor("cc_o_out", [4, 64], F32).ap(),
        "h_in": nc.dram_tensor("h_in", [128, 102], BF16).ap(),
        "h_out": nc.dram_tensor("h_out", [512, 102], BF16).ap(),
        "Ddiag": nc.dram_tensor("Ddiag", [4096, 128], F32, kind="ExternalInput").ap(),
    }
    R = C.sbuf("R", [128, KD * T], F32)
    C.start_block()
    sc = make_ln_scratch(C)
    C.const("normW", np.zeros((128, 16), np.float32), F32)
    C.const("gnw", np.zeros((128, 32), np.float32), F32)
    C.const("convw", np.zeros((128, 136), np.float32), F32)
    C.const("convb", np.zeros((128, 34), np.float32), F32)
    C.const("dtb", np.zeros((64, 1), np.float32), F32)
    C.const("alog", np.zeros((64, 1), np.float32), F32)
    C.const("ohprev", np.zeros((128, 4), np.float32), F32)
    C.const("coefT", np.zeros((4, 4), np.float32), F32)
    C.const("masklt", np.zeros((4, 1), np.float32), F32)
    C.const("triu_f32", np.triu(np.ones((128, 128), np.float32)), F32)
    C.const("tril01_bf16", np.triu(np.ones((128, 128), np.float32)), BF16)
    m0 = np.where(np.triu(np.ones((128, 128), np.float32)) > 0, 0.0, -1e38).astype(np.float32)
    C.const("M04_bf16", np.tile(m0, (1, 4)), BF16)
    C.const("ones3_bf16", np.ones((3, 128), np.float32), BF16)
    C.const("negones3_bf16", -np.ones((3, 128), np.float32), BF16)
    C.const("ident_f32", np.eye(128, dtype=np.float32), F32)
    C.const("ident_bf16", np.eye(128, dtype=np.float32), BF16)
    C.const("ones_col_bf16", np.ones((128, 1), np.float32), BF16)
    C.const("ones_col_f32x", np.ones((128, 1), np.float32), F32)
    A = C.sbuf("marena", [128, 6144 + 30976], BF16)
    xn = A[:, 6144 + 13952: 6144 + 13952 + KD * T]
    C.emit_const_loads()
    emit_stream_load(C, R, x_in)
    C.barrier()
    emit_ln(C, sc, R, xn, mode="mamba", h32=None)
    C.barrier()
    if debug:
        xn_d1 = nc.dram_tensor("xn_ln", [DM, T], BF16, kind="ExternalOutput").ap()
        s_dbg1 = C.sem("dbg1")

        @C.block.sync
        def _(sync):
            sync.dma_start(out=xn_d1.rearrange("(k p) t -> p k t", p=128),
                           in_=xn.rearrange("p (k t) -> p k t", k=KD)).then_inc(s_dbg1.h, 16)
        s_dbg1.inc(16)
        C.mark(s_dbg1)
        C.barrier()
    emit_rms_inplace(C, sc, xn)
    C.barrier()
    if debug:
        xn_d2 = nc.dram_tensor("xn_rms", [DM, T], BF16, kind="ExternalOutput").ap()
        m_d = nc.dram_tensor("m_dbg", [1, T], F32, kind="ExternalOutput").ap()
        rs_d = nc.dram_tensor("rs_dbg", [1, T], F32, kind="ExternalOutput").ap()
        brs_d = nc.dram_tensor("brs_dbg", [128, T], F32, kind="ExternalOutput").ap()
        s_dbg2 = C.sem("dbg2")

        @C.block.sync
        def _(sync):
            sync.dma_start(out=xn_d2.rearrange("(k p) t -> p k t", p=128),
                           in_=xn.rearrange("p (k t) -> p k t", k=KD)).then_inc(s_dbg2.h, 16)
            sync.dma_start(out=m_d, in_=sc["m"][:]).then_inc(s_dbg2.h, 16)
            sync.dma_start(out=rs_d, in_=sc["rs"][:]).then_inc(s_dbg2.h, 16)
            sync.dma_start(out=brs_d, in_=sc["Bm"][:]).then_inc(s_dbg2.h, 16)
        s_dbg2.inc(64)
        C.mark(s_dbg2)
        C.barrier()
    emit_mamba(C, sc, R, xn, None, A[:], inWT, outWT, dram)
    emit_stream_store(C, R, x_out)
    return C


def mamba_in_maps(inp, x_stream_percore):
    inW = f32c(np.asarray(inp["L0_attn_in_proj_w"]).T)
    inWT = np.concatenate([inW, np.zeros((DM, 64), np.float32)], axis=1)
    outWT = f32c(np.asarray(inp["L0_attn_out_proj_w"]).T)
    normW = f32c(np.asarray(inp["L0_attn_norm_weight"]).reshape(16, 128).T)
    gnw = f32c(np.asarray(inp["L0_attn_gate_norm_weight"]).reshape(32, 128).T)
    cw = np.asarray(inp["L0_attn_conv_w"])[:, 0, :]
    convw = f32c(cw.reshape(34, 128, 4).transpose(1, 0, 2).reshape(128, 136))
    convb = f32c(np.asarray(inp["L0_attn_conv_b"]).reshape(34, 128).T)
    dtb = f32c(np.asarray(inp["L0_attn_dt_bias"]).reshape(64, 1))
    alog = f32c(np.asarray(inp["L0_attn_A_log"]).reshape(64, 1))
    Drep = np.repeat(np.asarray(inp["L0_attn_D"]), 64)
    Dd = np.zeros((32, 128, 128), np.float32)
    for i in range(32):
        Dd[i] = np.diag(Drep[i * 128:(i + 1) * 128])
    Ddiag = f32c(Dd.reshape(4096, 128))
    maps = []
    for c in range(8):
        p = c % 4
        oh = np.zeros((128, 4), np.float32)
        if p >= 1:
            oh[:, p - 1] = 1.0
        coef = np.zeros((4, 4), np.float32)
        for j in range(4):
            for m2 in range(4):
                if j < m2 < p:
                    coef[j, m2] = 1.0
        mlt = np.zeros((4, 1), np.float32)
        mlt[:p, 0] = 1.0
        maps.append({
            "x_in": x_stream_percore[c],
            "inWT": inWT, "outWT": outWT, "Ddiag": Ddiag,
            "normW": normW, "gnw": gnw, "convw": convw, "convb": convb,
            "dtb": dtb, "alog": alog,
            "ohprev": oh, "coefT": np.ascontiguousarray(coef.T), "masklt": mlt,
        })
    return maps



SQD = 1.0 / (128.0 ** 0.5)
CSH = 20.0


def _mk_tables():
    sc = {}
    pv = {}
    dad = {}
    oev = {}
    n_sc = n_dad = n_oev = 0
    for h in range(4):
        for j in range(8):
            nkb = 4 * j + 4
            for kb in range(nkb):
                n_sc += 1
                sc[(h, j, kb)] = n_sc
                pv[(h, j, kb)] = n_sc
                if kb >= 4 * j:
                    n_dad += 1
                    dad[(h, j, kb)] = n_dad
            for qt in range(4):
                n_oev += 1
                oev[(h, j, qt)] = n_oev
    return sc, pv, dad, oev, n_sc, n_dad, n_oev


def emit_attn(C, sc_ln, R, xn, arena, wqT, wkT, wvT, woT, dram, pref="a"):
    b = C.block
    cin_x, ag_x = dram["cin_x"], dram["ag_x"]
    q_scr, k_scr, v_scr = dram["q_scr"], dram["k_scr"], dram["v_scr"]
    o_scr, o_ag = dram["o_scr"], dram["o_ag"]
    cosd, sind = dram["cosT"], dram["sinT"]
    ps = C.ps8
    identb = C.const_tiles["ident_bf16"]
    masks = C.const_tiles["attn_masks"]
    negC = C.const_tiles["negC"]
    oidx = C.const_tiles["oidx"]

    A = arena
    wqkv = A[:, 0:24576]
    xblk = A[:, 24576:40960]
    qkb = [A[:, 40960 + i * 512:40960 + (i + 1) * 512] for i in range(2)]
    rop = [A[:, 41984 + i * 512:41984 + (i + 1) * 512] for i in range(2)]
    qh = A[:, 0:4096]
    kh = A[:, 4096:8192]
    vh = A[:, 8192:12288]
    vaug = A[:, 12288:16416]
    pbuf = [A[:, 16416 + i * 512:16416 + (i + 1) * 512] for i in range(2)]
    otile = [A[:, 17440 + i * 128:17440 + (i + 1) * 128] for i in range(2)]
    attnT = A[:, 0:16384]
    gat = [A[:, 16384 + i * 512:16384 + (i + 1) * 512] for i in range(2)]
    wobf = [A[:, 17664 + i * 4096:17664 + (i + 1) * 4096] for i in range(2)]

    w32 = C.sbuf(pref + "w32", [128, 2048], F32)
    trig = C.sbuf(pref + "trig", [128, 2048], BF16)
    t64 = [C.sbuf(pref + f"t64_{i}", [64, 512], F32) for i in range(4)]
    sdig = [C.sbuf(pref + f"sdig{i}", [128, 512], F32) for i in range(2)]
    rcol = C.sbuf(pref + "rcol", [128, 1], F32)

    s_st = C.sem(pref + "st")
    s_ag = C.sem(pref + "ag")

    @b.gpsimd
    def _(gpsimd):
        gpsimd.dma_start(out=cin_x.rearrange("(k p) t -> p k t", p=128),
                         in_=xn[:].rearrange("p (k t) -> p k t", k=KD)).then_inc(s_st.h, 16)
        gpsimd.wait_ge(s_st.h, 16)
        gpsimd.collective_compute("AllGather", ALU.bypass,
                                  replica_groups=[[0, 1, 2, 3], [4, 5, 6, 7]],
                                  ins=[cin_x], outs=[ag_x]).then_inc(s_ag.h, 1)
    s_st.inc(16)
    s_ag.inc(1)
    C.mark(s_ag)
    C.mark(s_st)
    C.barrier()

    xg3 = ag_x.rearrange("(rk p) t -> p rk t", p=128)
    wsrcs = [wqT, wkT, wvT]
    scrs = [q_scr, k_scr, v_scr]
    s_wd = C.sem(pref + "wd")
    s_wc = C.sem(pref + "wc")
    s_xd = [C.sem(pref + "xd0"), C.sem(pref + "xd1")]
    s_mm = C.sem(pref + "mm")
    s_ev = C.sem(pref + "ev")
    s_sp = [C.sem(pref + "sp0"), C.sem(pref + "sp1")]
    s_td = [C.sem(pref + "td0"), C.sem(pref + "td1")]
    chP = Chain(C, pref + "chP")

    @b.sync
    def _(sync):
        for wi in range(3):
            for ch in range(4):
                gi = wi * 4 + ch
                if gi >= 1:
                    sync.wait_ge(s_wc.h, gi)
                sync.dma_start(out=w32[:].rearrange("p (k o) -> p k o", k=KD),
                               in_=wsrcs[wi].rearrange("(k p) o -> p k o", p=128)[:, :, ch * 128:(ch + 1) * 128]
                               ).then_inc(s_wd.h, 16)
        for tb in range(8):
            par = tb % 2
            if tb >= 2:
                sync.wait_ge(s_mm.h, 12 * (tb - 1))
            r, tl = divmod(tb, 2)
            sync.dma_start(
                out=xblk[:, par * 8192:(par + 1) * 8192].rearrange("p (k t) -> p k t", k=KD),
                in_=xg3[:, r * 16:(r + 1) * 16, tl * 512:(tl + 1) * 512]).then_inc(s_xd[par].h, 16)
    s_wd.inc(16 * 12)
    for tb in range(8):
        s_xd[tb % 2].inc(16)

    @b.scalar
    def _(scalar):
        for tb in range(8):
            par = tb % 2
            if tb >= 2:
                scalar.wait_ge(s_ev.h, 12 * (tb - 1))
            scalar.dma_start(out=trig[:, par * 1024:par * 1024 + 512],
                             in_=cosd[:, tb * 512:(tb + 1) * 512]).then_inc(s_td[par].h, 16)
            scalar.dma_start(out=trig[:, par * 1024 + 512:par * 1024 + 1024],
                             in_=sind[:, tb * 512:(tb + 1) * 512]).then_inc(s_td[par].h, 16)
    for tb in range(8):
        s_td[tb % 2].inc(32)

    @b.vector
    def _(vector):
        for wi in range(3):
            for ch in range(4):
                gi = wi * 4 + ch
                vector.wait_ge(s_wd.h, 16 * (gi + 1))
                vector.tensor_copy(
                    wqkv.rearrange("p (w k o) -> p w k o", w=3, k=KD)[:, wi, :, ch * 128:(ch + 1) * 128],
                    w32[:].rearrange("p (k o) -> p k o", k=KD)).then_inc(s_wc.h, 1)
        for tb in range(8):
            par = tb % 2
            for sel in range(3):
                for ot in range(4):
                    gidx = tb * 12 + sel * 4 + ot
                    pb = gidx % 2
                    vector.wait_ge(s_mm.h, gidx + 1)
                    if gidx >= 2:
                        vector.wait_ge(s_sp[pb].h, 16 * (gidx // 2))
                    if sel == 2:
                        vector.tensor_copy(rop[pb], ps[4 + pb][:]).then_inc(s_ev.h, 1)
                    else:
                        vector.wait_ge(s_td[par].h, 32 * (tb // 2 + 1))
                        chP.dep(vector.tensor_copy(qkb[pb], ps[4 + pb][:]))
                        chP.wait(vector)
                        c_lo = trig[0:64, par * 1024:par * 1024 + 512]
                        s_lo = trig[0:64, par * 1024 + 512:par * 1024 + 1024]
                        c_hi = trig[64:128, par * 1024:par * 1024 + 512]
                        s_hi = trig[64:128, par * 1024 + 512:par * 1024 + 1024]
                        chP.dep(vector.tensor_tensor(t64[0][:], _pview(qkb[pb], 0, 64), c_lo, ALU.mult))
                        chP.dep(vector.tensor_tensor(t64[1][:], _pview(qkb[pb], 64, 64), s_hi, ALU.mult))
                        chP.dep(vector.tensor_tensor(t64[2][:], _pview(qkb[pb], 0, 64), s_lo, ALU.mult))
                        chP.dep(vector.tensor_tensor(t64[3][:], _pview(qkb[pb], 64, 64), c_hi, ALU.mult))
                        chP.wait(vector)
                        chP.dep(vector.tensor_tensor(_pview(rop[pb], 0, 64), t64[0][:], t64[1][:], ALU.subtract))
                        vector.tensor_tensor(_pview(rop[pb], 64, 64), t64[2][:], t64[3][:],
                                             ALU.add).then_inc(s_ev.h, 1)
    s_wc.inc(12)
    s_ev.inc(96)

    @b.tensor
    def _(tensor):
        for tb in range(8):
            par = tb % 2
            for sel in range(3):
                for ot in range(4):
                    gidx = tb * 12 + sel * 4 + ot
                    pb = gidx % 2
                    if gidx == 0:
                        tensor.wait_ge(s_wc.h, 12)
                    tensor.wait_ge(s_xd[par].h, 16 * (tb // 2 + 1))
                    if gidx >= 2:
                        tensor.wait_ge(s_ev.h, gidx - 1)
                    for k in range(KD):
                        mm = tensor.matmul(
                            ps[4 + pb][:],
                            wqkv[:, sel * 8192 + k * 512 + ot * 128: sel * 8192 + k * 512 + ot * 128 + 128],
                            xblk[:, par * 8192 + k * 512: par * 8192 + k * 512 + 512],
                            start=(k == 0), stop=(k == KD - 1))
                        if k == KD - 1:
                            mm.then_inc(s_mm.h, 1)
    s_mm.inc(96)

    @b.gpsimd
    def _(gpsimd):
        for tb in range(8):
            for sel in range(3):
                for ot in range(4):
                    gidx = tb * 12 + sel * 4 + ot
                    pb = gidx % 2
                    gpsimd.wait_ge(s_ev.h, gidx + 1)
                    gpsimd.dma_start(out=scrs[sel][ot * 128:(ot + 1) * 128, tb * 512:(tb + 1) * 512],
                                     in_=rop[pb]).then_inc(s_sp[pb].h, 16)
    s_sp[0].inc(16 * 48)
    s_sp[1].inc(16 * 48)

    for m in (s_mm, s_ev, s_sp[0], s_sp[1], s_wc, s_wd, s_xd[0], s_xd[1], s_td[0], s_td[1]):
        C.mark(m)
    C.barrier()

    SC, PV, DAD, OEV, N_SC, N_DAD, N_OEV = _mk_tables()
    s_hl = C.sem(pref + "hl")
    s_vt = C.sem(pref + "vt")
    s_va = C.sem(pref + "va")
    s_sc1 = C.sem(pref + "sc1")
    s_px = C.sem(pref + "px")
    s_dad = C.sem(pref + "dad")
    s_pv = C.sem(pref + "pv")
    s_oev = C.sem(pref + "oev")
    s_osp = [C.sem(pref + "osp0"), C.sem(pref + "osp1")]
    s_ms = C.sem(pref + "ms")
    chO = Chain(C, pref + "chO")

    @b.vector
    def _(vector):
        vector.memset(vaug, 1.0).then_inc(s_ms.h, 1)
    s_ms.inc(1)

    @b.sync
    def _(sync):
        for h in range(4):
            if h >= 1:
                sync.wait_ge(s_oev.h, 32 * h)
            sync.dma_start(out=qh, in_=q_scr[h * 128:(h + 1) * 128, :]).then_inc(s_hl.h, 16)
            sync.dma_start(out=kh, in_=k_scr[h * 128:(h + 1) * 128, :]).then_inc(s_hl.h, 16)
            sync.dma_start(out=vh, in_=v_scr[h * 128:(h + 1) * 128, :]).then_inc(s_hl.h, 16)
    s_hl.inc(48 * 4)

    @b.tensor
    def _(tensor):
        for h in range(4):
            tensor.wait_ge(s_hl.h, 48 * (h + 1))
            for kb in range(32):
                g = h * 32 + kb
                if g >= 2:
                    tensor.wait_ge(s_va.h, g - 1)
                if kb == 0:
                    tensor.wait_ge(s_ms.h, 1)
                    if h >= 1:
                        tensor.wait_ge(s_oev.h, 32 * h)
                tensor.transpose(ps[2 + g % 2][:, 0:64].bitcast(BF16),
                                 vh[:, kb * 128:(kb + 1) * 128],
                                 identb[:]).then_inc(s_vt.h, 1)
            for j in range(8):
                nkb = 4 * j + 4
                for kb in range(nkb):
                    if SC[(h, j, kb)] >= 3:
                        tensor.wait_ge(s_px.h, SC[(h, j, kb)] - 2)
                    mm = tensor.matmul(ps[kb % 2][:],
                                       kh[:, kb * 128:(kb + 1) * 128],
                                       qh[:, j * 512:(j + 1) * 512],
                                       start=True, stop=True)
                    mm.then_inc(s_sc1.h, 1)
                    if kb >= 1:
                        _emit_pv(C, tensor, ps, pbuf, vaug, h, j, kb - 1, nkb,
                                 SC, OEV, s_px, s_va, s_oev, s_pv)
                _emit_pv(C, tensor, ps, pbuf, vaug, h, j, nkb - 1, nkb,
                         SC, OEV, s_px, s_va, s_oev, s_pv)
    s_sc1.inc(N_SC)
    s_pv.inc(N_SC)

    @b.scalar
    def _(scalar):
        for h in range(4):
            for j in range(8):
                nkb = 4 * j + 4
                for kb in range(nkb):
                    idx = SC[(h, j, kb)]
                    if idx >= 3:
                        scalar.wait_ge(s_pv.h, idx - 2)
                    if kb >= 4 * j:
                        scalar.wait_ge(s_dad.h, DAD[(h, j, kb)])
                        scalar.activation(pbuf[kb % 2], sdig[kb % 2][:], AF.Exp,
                                          scale=SQD, bias=negC[:]).then_inc(s_px.h, 1)
                    else:
                        scalar.wait_ge(s_sc1.h, idx)
                        scalar.activation(pbuf[kb % 2], ps[kb % 2][:], AF.Exp,
                                          scale=SQD, bias=negC[:]).then_inc(s_px.h, 1)
    s_px.inc(N_SC)

    @b.vector
    def _(vector):
        oev = 0
        for h in range(4):
            for kb in range(32):
                g = h * 32 + kb
                vector.wait_ge(s_vt.h, g + 1)
                vector.tensor_copy(vaug[:, kb * 129:kb * 129 + 128],
                                   ps[2 + g % 2][:, 0:64].bitcast(BF16)).then_inc(s_va.h, 1)
            for j in range(8):
                nkb = 4 * j + 4
                for kb in range(4 * j, nkb):
                    idx = SC[(h, j, kb)]
                    if idx >= 3:
                        vector.wait_ge(s_px.h, idx - 2)
                    vector.wait_ge(s_sc1.h, idx)
                    krel = kb - 4 * j
                    vector.tensor_tensor(sdig[kb % 2][:], ps[kb % 2][:],
                                         masks[:, krel * 512:(krel + 1) * 512],
                                         ALU.add).then_inc(s_dad.h, 1)
                vector.wait_ge(s_pv.h, PV[(h, j, nkb - 1)])
                for qt in range(4):
                    if oev >= 1:
                        vector.wait_ge(s_oev.h, oev)
                    chO.dep(vector.reciprocal(rcol[:], ps[4 + qt][:, 128:129]))
                    chO.wait(vector)
                    opb = oev % 2
                    vector.wait_ge(s_osp[opb].h, 16 * (oev // 2))
                    vector.tensor_scalar(otile[opb], ps[4 + qt][:, 0:128], rcol[:], None,
                                         ALU.mult).then_inc(s_oev.h, 1)
                    oev += 1
    s_va.inc(128)
    s_dad.inc(N_DAD)
    s_oev.inc(N_OEV)

    @b.gpsimd
    def _(gpsimd):
        oev = 0
        for h in range(4):
            for j in range(8):
                for qt in range(4):
                    gpsimd.wait_ge(s_oev.h, oev + 1)
                    pos = j * 512 + qt * 128
                    gpsimd.dma_start(out=o_scr[pos:pos + 128, h * 128:(h + 1) * 128],
                                     in_=otile[oev % 2]).then_inc(s_osp[oev % 2].h, 16)
                    oev += 1
    s_osp[0].inc(16 * 64)
    s_osp[1].inc(16 * 64)

    for m in (s_sc1, s_px, s_pv, s_oev, s_osp[0], s_osp[1], s_va, s_hl, s_dad):
        C.mark(m)
    C.barrier()

    s_ag2 = C.sem(pref + "ag2")

    @b.gpsimd
    def _(gpsimd):
        gpsimd.collective_compute("AllGather", ALU.bypass,
                                  replica_groups=[[0, 1, 2, 3], [4, 5, 6, 7]],
                                  ins=[o_scr], outs=[o_ag]).then_inc(s_ag2.h, 1)
    s_ag2.inc(1)
    C.mark(s_ag2)
    C.barrier()

    from concourse.bass import IndirectOffsetOnAxis
    s_gd = [C.sem(pref + "gd0"), C.sem(pref + "gd1")]
    s_gt2 = C.sem(pref + "gt2")
    s_ge2 = C.sem(pref + "ge2")

    @b.gpsimd
    def _(gpsimd):
        for i in range(32):
            par = i % 2
            if i >= 2:
                gpsimd.wait_ge(s_gt2.h, 4 * (i - 1))
            gpsimd.indirect_dma_start(
                gat[par], None, o_ag,
                IndirectOffsetOnAxis(ap=oidx[:, i:i + 1], axis=0)).then_inc(s_gd[par].h, 16)
    s_gd[0].inc(16 * 16)
    s_gd[1].inc(16 * 16)

    @b.tensor
    def _(tensor):
        for i in range(32):
            r, tt = divmod(i, 8)
            par = i % 2
            tensor.wait_ge(s_gd[par].h, 16 * (i // 2 + 1))
            for hl in range(4):
                g = i * 4 + hl
                if g >= 2:
                    tensor.wait_ge(s_ge2.h, g - 1)
                tensor.transpose(ps[6 + g % 2][:, 0:64].bitcast(BF16),
                                 _fslice(gat[par], hl * 128, 128),
                                 identb[:]).then_inc(s_gt2.h, 1)
    s_gt2.inc(128)

    @b.vector
    def _(vector):
        for i in range(32):
            r, tt = divmod(i, 8)
            for hl in range(4):
                g = i * 4 + hl
                h = 4 * r + hl
                vector.wait_ge(s_gt2.h, g + 1)
                vector.tensor_copy(attnT[:, h * 1024 + tt * 128: h * 1024 + tt * 128 + 128],
                                   ps[6 + g % 2][:, 0:64].bitcast(BF16)).then_inc(s_ge2.h, 1)
    s_ge2.inc(128)

    for m in (s_gt2, s_ge2, s_gd[0], s_gd[1]):
        C.mark(m)
    C.barrier()

    s_od = C.sem(pref + "od")
    s_oc = C.sem(pref + "oc")
    s_omm = C.sem(pref + "omm")
    s_ora = C.sem(pref + "ora")

    @b.sync
    def _(sync):
        for ch in range(16):
            if ch >= 1:
                sync.wait_ge(s_oc.h, ch)
            sync.dma_start(out=w32[:].rearrange("p (k o) -> p k o", k=KD),
                           in_=woT.rearrange("(k p) o -> p k o", p=128)[:, :, ch * 128:(ch + 1) * 128]
                           ).then_inc(s_od.h, 16)
    s_od.inc(16 * 16)

    @b.scalar
    def _(scalar):
        for ch in range(16):
            scalar.wait_ge(s_od.h, 16 * (ch + 1))
            bb = (ch // 2) % 2
            if ch >= 4:
                scalar.wait_ge(s_omm.h, 2 * (ch - 3))
            scalar.activation(
                wobf[bb].rearrange("p (k o) -> p k o", o=256)[:, :, (ch % 2) * 128:(ch % 2) * 128 + 128],
                w32[:].rearrange("p (k o) -> p k o", k=KD),
                AF.Identity).then_inc(s_oc.h, 1)
    s_oc.inc(16)

    @b.tensor
    def _(tensor):
        for u in range(32):
            ob, st = divmod(u, 4)
            sub, tb = divmod(st, 2)
            bb = ob % 2
            pb = 4 + u % 2
            tensor.wait_ge(s_oc.h, 2 * (ob + 1))
            if u >= 2:
                tensor.wait_ge(s_ora.h, u - 1)
            for k in range(KD):
                mm = tensor.matmul(
                    ps[pb][:],
                    _fslice(wobf[bb], k * 256 + sub * 128, 128),
                    attnT[:, k * 1024 + tb * 512: k * 1024 + tb * 512 + 512],
                    start=(k == 0), stop=(k == KD - 1))
                if k == KD - 1:
                    mm.then_inc(s_omm.h, 1)
    s_omm.inc(32)

    @b.vector
    def _(vector):
        for u in range(32):
            ob, st = divmod(u, 4)
            sub, tb = divmod(st, 2)
            pb = 4 + u % 2
            ot = ob * 2 + sub
            vector.wait_ge(s_omm.h, u + 1)
            sl = slice(ot * T + tb * 512, ot * T + tb * 512 + 512)
            vector.tensor_tensor(R[:, sl], R[:, sl], ps[pb][:], ALU.add).then_inc(s_ora.h, 1)
    s_ora.inc(32)

    for m in (s_omm, s_ora, s_oc, s_od):
        C.mark(m)
    C.barrier()


def _pview(ap, p0, np_):
    return ap[p0:p0 + np_, :]


def _fslice(ap, c0, w):
    return ap[:, c0:c0 + w]


def _emit_pv(C, tensor, ps, pbuf, vaug, h, j, kb, nkb, SC, OEV, s_px, s_va, s_oev, s_pv):
    idx = SC[(h, j, kb)]
    tensor.wait_ge(s_px.h, idx)
    tensor.wait_ge(s_va.h, h * 32 + kb + 1)
    for qt in range(4):
        if kb == 0 and (h + j) > 0:
            tensor.wait_ge(s_oev.h, OEV[(h, j, qt)] - 4)
        mm = tensor.matmul(ps[4 + qt][:, 0:129],
                           _fslice(pbuf[kb % 2], qt * 128, 128),
                           _fslice(vaug, kb * 129, 129),
                           start=(kb == 0), stop=(kb == nkb - 1))
        if qt == 3:
            mm.then_inc(s_pv.h, 1)


def build_attn_launch():
    C = Ctx()
    nc = C.nc
    x_in = nc.dram_tensor("x_in", [DM, T], F32, kind="ExternalInput").ap()
    x_out = nc.dram_tensor("x_out", [DM, T], F32, kind="ExternalOutput").ap()
    wqT = nc.dram_tensor("wqT", [DM, 512], F32, kind="ExternalInput").ap()
    wkT = nc.dram_tensor("wkT", [DM, 512], F32, kind="ExternalInput").ap()
    wvT = nc.dram_tensor("wvT", [DM, 512], F32, kind="ExternalInput").ap()
    woT = nc.dram_tensor("woT", [DM, DM], F32, kind="ExternalInput").ap()
    cosT = nc.dram_tensor("cosT", [128, 4096], BF16, kind="ExternalInput").ap()
    sinT = nc.dram_tensor("sinT", [128, 4096], BF16, kind="ExternalInput").ap()
    dram = {
        "cin_x": nc.dram_tensor("cin_x", [DM, T], BF16).ap(),
        "ag_x": nc.dram_tensor("ag_x", [4 * DM, T], BF16).ap(),
        "q_scr": nc.dram_tensor("q_scr", [512, 4096], BF16).ap(),
        "k_scr": nc.dram_tensor("k_scr", [512, 4096], BF16).ap(),
        "v_scr": nc.dram_tensor("v_scr", [512, 4096], BF16).ap(),
        "o_scr": nc.dram_tensor("o_scr", [4096, 512], BF16).ap(),
        "o_ag": nc.dram_tensor("o_ag", [4 * 4096, 512], BF16).ap(),
        "cosT": cosT, "sinT": sinT,
    }
    R = C.sbuf("R", [128, KD * T], F32)
    C.start_block()
    sc = make_ln_scratch(C)
    C.const("ident_bf16", np.eye(128, dtype=np.float32), BF16)
    C.const("attn_masks", np.zeros((128, 2048), np.float32), BF16)
    C.const("negC", np.full((128, 1), -CSH, np.float32), F32)
    C.const("oidx", np.zeros((128, 32), np.int32), I32)
    arena = C.sbuf("arena", [128, 43008], BF16)
    xn = arena[:, 0:KD * T]
    C.emit_const_loads()
    emit_stream_load(C, R, x_in)
    C.barrier()
    emit_ln(C, sc, R, xn, mode="plain")
    C.barrier()
    emit_attn(C, sc, R, xn, arena, wqT, wkT, wvT, woT, dram)
    emit_stream_store(C, R, x_out)
    return C


def rope_perm():
    p = np.zeros(2048, np.int64)
    for h in range(16):
        base = h * 128
        p[base:base + 64] = base + np.arange(0, 128, 2)
        p[base + 64:base + 128] = base + np.arange(1, 128, 2)
    return p


def attn_in_maps(inp, x_stream_percore):
    perm = rope_perm()
    WqTp = f32c(np.asarray(inp["L1_attn_Wq"]).T[:, perm])
    WkTp = f32c(np.asarray(inp["L1_attn_Wk"]).T[:, perm])
    WvT = f32c(np.asarray(inp["L1_attn_Wv"]).T)
    WoT = f32c(np.asarray(inp["L1_attn_Wo"]).T)
    idx = np.arange(0, 128, 2, dtype=np.float32)
    inv = 10000.0 ** (-idx / 128.0)
    ang = np.arange(4096, dtype=np.float32)[:, None] * inv[None, :]
    cosT = bfc(np.concatenate([np.cos(ang).T, np.cos(ang).T], axis=0))
    sinT = bfc(np.concatenate([np.sin(ang).T, np.sin(ang).T], axis=0))
    m = np.zeros((128, 2048), np.float32)
    for krel in range(4):
        for kl in range(128):
            for ql in range(512):
                if krel * 128 + kl > ql:
                    m[kl, krel * 512 + ql] = -1e38
    masks = bfc(m)
    maps = []
    for c in range(8):
        p = c % 4
        oidx = np.zeros((128, 32), np.int32)
        for r in range(4):
            for tt in range(8):
                oidx[:, r * 8 + tt] = r * 4096 + p * 1024 + tt * 128 + np.arange(128)
        maps.append({
            "x_in": x_stream_percore[c],
            "wqT": np.ascontiguousarray(WqTp[:, p * 512:(p + 1) * 512]),
            "wkT": np.ascontiguousarray(WkTp[:, p * 512:(p + 1) * 512]),
            "wvT": np.ascontiguousarray(WvT[:, p * 512:(p + 1) * 512]),
            "woT": WoT,
            "cosT": cosT, "sinT": sinT,
            "attn_masks": masks,
            "negC": np.full((128, 1), -CSH, np.float32),
            "oidx": oidx,
            "ident_bf16": np.eye(128, dtype=np.float32).astype(ml_dtypes.bfloat16),
        })
    return maps



def emit_moe(C, sc, R, xn_xblk, w1d, w3d, w2d, sw1d, sw3d, sw2d, dram, pref="e"):
    b = C.block
    cin_x, cout_x, cin_l, cout_l = dram["cin_x"], dram["cout_x"], dram["cin_l"], dram["cout_l"]
    hidscr, partial, rs_out, hshscr, sh_out = (dram["hidscr"], dram["partial"],
                                               dram["rs_out"], dram["hshscr"], dram["sh_out"])
    gwt = C.const_tiles["gwt128"]
    ident = C.const_tiles["ident_f32"]
    onehot = C.const_tiles["onehot64"]
    ones_col = sc["ones_col"]
    Bm, Brs = sc["Bm"], sc["Brs"]
    ps = C.ps8
    xblk = xn_xblk

    logitsT = C.sbuf(pref + "logitsT", [8, T], F32)
    sg = C.sbuf(pref + "sg", [8, 1], F32)
    t8 = C.sbuf(pref + "t8", [8, 512], F32)
    mw32 = C.sbuf(pref + "mw32", [128, 2048], F32)
    w1h = C.sbuf(pref + "w1h", [128, 16 * 768], BF16)
    w3h = C.sbuf(pref + "w3h", [128, 16 * 768], BF16)
    gs = [C.sbuf(pref + f"gs{i}", [128, 512], BF16) for i in range(2)]
    hb = [C.sbuf(pref + f"hb{i}", [128, 512], BF16) for i in range(2)]
    hidT = C.sbuf(pref + "hidT", [128, 2 * 22 * 128], BF16)
    pout = [C.sbuf(pref + f"pout{i}", [128, 512], F32) for i in range(2)]
    lgt = C.sbuf(pref + "lgt", [128, 64], F32)
    gsc = C.sbuf(pref + "gsc", [128, 384], F32)
    w_all = C.sbuf(pref + "w_all", [128, 64], F32)

    s_lmm = C.sem(pref + "lmm")
    s_lev = C.sem(pref + "lev")
    chL = Chain(C, pref + "chL")

    @b.tensor
    def _(tensor):
        for k in range(KD):
            tensor.matmul(ps[4][0:8, 0:1], gwt[:, 8 * k:8 * k + 8], ones_col[:],
                          start=(k == 0), stop=(k == KD - 1))
        for tb in range(2):
            for k in range(KD):
                mm = tensor.matmul(ps[2 + tb][0:8, :], gwt[:, 8 * k:8 * k + 8],
                                   R[:, k * T + tb * 512: k * T + tb * 512 + 512],
                                   start=(k == 0), stop=(k == KD - 1))
                if k == KD - 1:
                    mm.then_inc(s_lmm.h, 1)
    s_lmm.inc(2)

    @b.vector
    def _(vector):
        vector.wait_ge(s_lmm.h, 2)
        chL.dep(vector.tensor_copy(sg[:], ps[4][0:8, 0:1]))
        for tb in range(2):
            sl = slice(tb * 512, (tb + 1) * 512)
            chL.wait(vector)
            chL.dep(vector.tensor_scalar(t8[:], Bm[0:8, sl], sg[:], None, ALU.mult))
            chL.wait(vector)
            chL.dep(vector.tensor_tensor(t8[:], ps[2 + tb][0:8, :], t8[:], ALU.subtract))
            chL.wait(vector)
            ll = vector.tensor_tensor(logitsT[:, sl], t8[:], Brs[0:8, sl], ALU.mult)
            if tb == 1:
                ll.then_inc(s_lev.h, 1)
            else:
                chL.dep(ll)
    s_lev.inc(1)
    C.mark(s_lev)
    C.mark(s_lmm)
    C.barrier()

    s_cc = C.sem(pref + "cc")
    s_ccd = C.sem(pref + "ccd")
    s_cw = C.sem(pref + "cw")
    s_cwc = C.sem(pref + "cwc")

    @b.gpsimd
    def _(gpsimd):
        gpsimd.dma_start(out=cin_l, in_=logitsT[:]).then_inc(s_ccd.h, 16)
        gpsimd.wait_ge(s_ccd.h, 16)
        gpsimd.collective_compute("AllGather", ALU.bypass,
                                  replica_groups=[[0, 1, 2, 3, 4, 5, 6, 7]],
                                  ins=[cin_x], outs=[cout_x]).then_inc(s_cc.h, 1)
        gpsimd.collective_compute("AllGather", ALU.bypass,
                                  replica_groups=[[0, 1, 2, 3, 4, 5, 6, 7]],
                                  ins=[cin_l], outs=[cout_l]).then_inc(s_cc.h, 1)
    s_cc.inc(2)
    s_ccd.inc(16)

    wsrc = [w1d, w3d]
    wdst = [w1h, w3h]
    oh0_chunks = []
    oh1_chunks = []
    for wi in range(2):
        for oh in range(2):
            HW = [768, 640][oh]
            off = 0
            while off < HW:
                w = min(128, HW - off)
                (oh0_chunks if oh == 0 else oh1_chunks).append((wi, oh, off, w))
                off += w
    n_oh0 = len(oh0_chunks)
    n_oh1 = len(oh1_chunks)

    @b.sync
    def _(sync):
        for ci, (wi, oh, off, w) in enumerate(oh0_chunks):
            if ci >= 1:
                sync.wait_ge(s_cwc.h, ci)
            sync.dma_start(out=mw32[:, 0:16 * w].rearrange("p (k o) -> p k o", k=KD),
                           in_=wsrc[wi].rearrange("(k p) o -> p k o", p=128)[:, :, off:off + w]
                           ).then_inc(s_cw.h, 16)
    s_cw.inc(16 * n_oh0)

    @b.vector
    def _(vector):
        for ci, (wi, oh, off, w) in enumerate(oh0_chunks):
            vector.wait_ge(s_cw.h, 16 * (ci + 1))
            vector.tensor_copy(
                wdst[wi][:].rearrange("p (k o) -> p k o", o=768)[:, :, off:off + w],
                mw32[:, 0:16 * w].rearrange("p (k o) -> p k o", k=KD)).then_inc(s_cwc.h, 1)
    s_cwc.inc(n_oh0)

    C.mark(s_cc)
    C.mark(s_cwc)
    C.mark(s_cw)
    C.barrier()

    s_gl = C.sem(pref + "gl")
    s_gt = C.sem(pref + "gt")
    s_ge = C.sem(pref + "ge")
    s_gd1 = C.sem(pref + "gd1")
    s_gs = C.sem(pref + "gs")
    s_gw = C.sem(pref + "gw")
    chG = Chain(C, pref + "chG")
    lgsb = logitsT

    @b.gpsimd
    def _(gpsimd):
        for r in range(8):
            if r >= 1:
                gpsimd.wait_ge(s_gt.h, 8 * r)
            gpsimd.dma_start(out=lgsb[:], in_=cout_l[8 * r:8 * r + 8, :]).then_inc(s_gl.h, 16)
    s_gl.inc(16 * 8)

    @b.tensor
    def _(tensor):
        for r in range(8):
            tensor.wait_ge(s_gl.h, 16 * (r + 1))
            for t8i in range(8):
                g = r * 8 + t8i
                if g >= 2:
                    tensor.wait_ge(s_ge.h, g - 1)
                tensor.transpose(ps[g % 2][0:128, 0:8],
                                 lgsb[0:8, t8i * 128:(t8i + 1) * 128],
                                 ident[0:8, 0:8]).then_inc(s_gt.h, 1)
    s_gt.inc(64)

    @b.vector
    def _(vector):
        for r in range(8):
            for t8i in range(8):
                g = r * 8 + t8i
                vector.wait_ge(s_gt.h, g + 1)
                vector.tensor_copy(lgt[:, t8i * 8:(t8i + 1) * 8],
                                   ps[g % 2][0:128, 0:8]).then_inc(s_ge.h, 1)
            l3 = lgt[:].rearrange("p (g e) -> p g e", g=8)
            m1 = gsc[:, 0:8]
            m1b = m1.rearrange("p (g e) -> p g e", e=1).broadcast_to([128, 8, 8])
            m2 = gsc[:, 8:16]
            m2b = m2.rearrange("p (g e) -> p g e", e=1).broadcast_to([128, 8, 8])
            den = gsc[:, 16:24]
            denb = den.rearrange("p (g e) -> p g e", e=1).broadcast_to([128, 8, 8])
            d1 = gsc[:, 64:128]
            u = gsc[:, 128:192]
            lm = gsc[:, 192:256]
            sel = gsc[:, 256:320]
            d3 = lambda x: x.rearrange("p (g e) -> p g e", e=8)
            vector.wait_ge(s_ge.h, 8 * (r + 1))
            chG.wait(vector)
            chG.dep(vector.tensor_reduce(m1, l3, axis=AX.X, op=ALU.max))
            chG.wait(vector)
            vector.tensor_tensor(d3(d1), l3, m1b, ALU.subtract).then_inc(s_gd1.h, 1)
            chG.dep(vector.tensor_tensor(d3(lm), l3, m1b, ALU.is_ge))
            chG.wait(vector)
            chG.dep(vector.scalar_tensor_tensor(lm[:], lm[:], -1e38, lgt[:], ALU.mult, ALU.add))
            chG.wait(vector)
            chG.dep(vector.tensor_reduce(m2, d3(lm), axis=AX.X, op=ALU.max))
            chG.wait(vector)
            chG.dep(vector.tensor_tensor(d3(sel), l3, m2b, ALU.is_ge))
            vector.wait_ge(s_gs.h, r + 1)
            chG.wait(vector)
            chG.dep(vector.tensor_tensor(u[:], u[:], sel[:], ALU.mult))
            chG.wait(vector)
            chG.dep(vector.tensor_reduce(den, d3(u), axis=AX.X, op=ALU.add))
            chG.wait(vector)
            chG.dep(vector.reciprocal(den, den))
            chG.wait(vector)
            chG.dep(vector.tensor_tensor(d3(u), d3(u), denb, ALU.mult))
            chG.wait(vector)
            chG.dep(vector.tensor_tensor(u[:], u[:], onehot[:], ALU.mult))
            chG.wait(vector)
            last = vector.tensor_reduce(w_all[:, r * 8:(r + 1) * 8], d3(u), axis=AX.X, op=ALU.add)
            if r == 7:
                last.then_inc(s_gw.h, 1)
            else:
                chG.dep(last)
    s_gd1.inc(8)
    s_gw.inc(1)

    @b.scalar
    def _(scalar):
        for r in range(8):
            scalar.wait_ge(s_gd1.h, r + 1)
            scalar.activation(gsc[:, 128:192], gsc[:, 64:128], AF.Exp).then_inc(s_gs.h, 1)
    s_gs.inc(8)

    C.mark(s_gw)
    C.mark(s_gs)
    C.mark(s_gl)
    C.barrier()

    xg3 = cout_x.rearrange("(rk p) t -> p rk t", p=128)
    hid3 = hidscr.rearrange("(o p) t -> p o t", p=128)
    OH_TILES = [(0, 6), (6, 11)]
    s_ax = [C.sem(pref + "ax0"), C.sem(pref + "ax1")]
    s_amm = C.sem(pref + "amm")
    s_asl = C.sem(pref + "asl")
    s_ahe = C.sem(pref + "ahe")
    s_asp = [C.sem(pref + "asp0"), C.sem(pref + "asp1")]
    s_cwc2 = C.sem(pref + "cwc2")

    units = []
    for oh in range(2):
        for tb in range(16):
            for ot in range(*OH_TILES[oh]):
                units.append((oh, tb, ot, len(units)))
    NUA = len(units)
    blocks = [(oh, tb) for oh in range(2) for tb in range(16)]
    U_through = []
    acc = 0
    for (oh, tb) in blocks:
        acc += OH_TILES[oh][1] - OH_TILES[oh][0]
        U_through.append(acc)
    n_oh0_units = U_through[15]

    @b.sync
    def _(sync):
        def xdma(bi):
            oh, tb = blocks[bi]
            r, tl = divmod(tb, 2)
            par = bi % 2
            if bi >= 2:
                sync.wait_ge(s_amm.h, 2 * U_through[bi - 2])
            sync.dma_start(
                out=xblk[:, par * 8192:(par + 1) * 8192].rearrange("p (k t) -> p k t", k=KD),
                in_=xg3[:, r * 16:(r + 1) * 16, tl * 512:(tl + 1) * 512]).then_inc(s_ax[par].h, 16)
        for bi in range(16):
            xdma(bi)
        for ci, (wi, oh, off, w) in enumerate(oh1_chunks):
            if ci == 0:
                sync.wait_ge(s_amm.h, 2 * n_oh0_units)
            else:
                sync.wait_ge(s_cwc2.h, ci)
            col0 = 768 + off
            sync.dma_start(out=mw32[:, 0:16 * w].rearrange("p (k o) -> p k o", k=KD),
                           in_=wsrc[wi].rearrange("(k p) o -> p k o", p=128)[:, :, col0:col0 + w]
                           ).then_inc(s_cw.h, 16)
        for bi in range(16, 32):
            xdma(bi)
    for bi in range(len(blocks)):
        s_ax[bi % 2].inc(16)
    s_cw.inc(16 * n_oh1)

    @b.vector
    def _(vector):
        def hev(u):
            pb = u % 2
            vector.wait_ge(s_amm.h, 2 * u + 2)
            vector.wait_ge(s_asl.h, u + 1)
            if u >= 2:
                vector.wait_ge(s_asp[u % 2].h, 16 * (u // 2))
            vector.tensor_tensor(hb[pb][:], ps[2 + pb][:], gs[pb][:],
                                 ALU.mult).then_inc(s_ahe.h, 1)
        for u in range(n_oh0_units):
            hev(u)
        cw_base = 16 * n_oh0
        for ci, (wi, oh, off, w) in enumerate(oh1_chunks):
            vector.wait_ge(s_cw.h, cw_base + 16 * (ci + 1))
            vector.tensor_copy(
                wdst[wi][:].rearrange("p (k o) -> p k o", o=768)[:, :, off:off + w],
                mw32[:, 0:16 * w].rearrange("p (k o) -> p k o", k=KD)).then_inc(s_cwc2.h, 1)
        for u in range(n_oh0_units, NUA):
            hev(u)
    s_cwc2.inc(n_oh1)
    s_ahe.inc(NUA)

    @b.tensor
    def _(tensor):
        for (oh, tb, ot, u) in units:
            bi = oh * 16 + tb
            par = bi % 2
            pb = u % 2
            oi = ot - OH_TILES[oh][0]
            tensor.wait_ge(s_ax[par].h, 16 * (bi // 2 + 1))
            if oh == 1:
                tensor.wait_ge(s_cwc2.h, n_oh1)
            if u >= 2:
                tensor.wait_ge(s_asl.h, u - 1)
                tensor.wait_ge(s_ahe.h, u - 1)
            for k in range(KD):
                mm = tensor.matmul(ps[pb][:], w1h[:, k * 768 + oi * 128: k * 768 + oi * 128 + 128],
                                   xblk[:, par * 8192 + k * 512: par * 8192 + k * 512 + 512],
                                   start=(k == 0), stop=(k == KD - 1))
                if k == KD - 1:
                    mm.then_inc(s_amm.h, 1)
            for k in range(KD):
                mm = tensor.matmul(ps[2 + pb][:], w3h[:, k * 768 + oi * 128: k * 768 + oi * 128 + 128],
                                   xblk[:, par * 8192 + k * 512: par * 8192 + k * 512 + 512],
                                   start=(k == 0), stop=(k == KD - 1))
                if k == KD - 1:
                    mm.then_inc(s_amm.h, 1)
    s_amm.inc(2 * NUA)

    @b.scalar
    def _(scalar):
        for (oh, tb, ot, u) in units:
            pb = u % 2
            scalar.wait_ge(s_amm.h, 2 * u + 1)
            if u >= 2:
                scalar.wait_ge(s_ahe.h, u - 1)
            scalar.activation(gs[pb][:], ps[pb][:], AF.Silu).then_inc(s_asl.h, 1)
    s_asl.inc(NUA)

    @b.gpsimd
    def _(gpsimd):
        for (oh, tb, ot, u) in units:
            pb = u % 2
            gpsimd.wait_ge(s_ahe.h, u + 1)
            gpsimd.dma_start(out=hid3[:, ot, tb * 512:(tb + 1) * 512],
                             in_=hb[pb][:]).then_inc(s_asp[u % 2].h, 16)
    s_asp[0].inc(16 * ((NUA + 1) // 2))
    s_asp[1].inc(16 * (NUA // 2))

    for m in (s_amm, s_asl, s_ahe, s_asp[0], s_asp[1], s_cwc2, s_cw, s_ax[0], s_ax[1]):
        C.mark(m)
    C.barrier()

    s_b_wd = C.sem(pref + "bwd")
    s_b_wc = C.sem(pref + "bwc")
    s_b_h = [C.sem(pref + "bh0"), C.sem(pref + "bh1")]
    s_bmm = C.sem(pref + "bmm")
    s_bev = C.sem(pref + "bev")
    s_bp = [C.sem(pref + "bp0"), C.sem(pref + "bp1")]
    w2bf = w3h
    w23 = w2d.rearrange("(k p) o -> p k o", p=128)

    @b.sync
    def _(sync):
        for dh in range(2):
            for ch in range(8):
                gi = dh * 8 + ch
                if gi >= 1:
                    sync.wait_ge(s_b_wc.h, gi)
                if dh == 1 and ch == 0:
                    sync.wait_ge(s_bmm.h, 2 * 64)
                sync.dma_start(out=mw32[:, 0:11 * 128].rearrange("p (k o) -> p k o", k=11),
                               in_=w23[:, :, dh * 1024 + ch * 128: dh * 1024 + ch * 128 + 128]
                               ).then_inc(s_b_wd.h, 16)
    s_b_wd.inc(16 * 16)

    @b.scalar
    def _(scalar):
        for gi in range(16):
            scalar.wait_ge(s_b_wd.h, 16 * (gi + 1))
            ch = gi % 8
            scalar.activation(
                w2bf[:, 0:11 * 1024].rearrange("p (k o) -> p k o", o=1024)[:, :, ch * 128:(ch + 1) * 128],
                mw32[:, 0:11 * 128].rearrange("p (k o) -> p k o", k=11),
                AF.Identity).then_inc(s_b_wc.h, 1)
    s_b_wc.inc(16)

    @b.gpsimd
    def _(gpsimd):
        for dh in range(2):
            for tt in range(64):
                j = dh * 64 + tt
                par = j % 2
                if j >= 2:
                    gpsimd.wait_ge(s_bmm.h, 2 * (j - 1))
                gpsimd.dma_start(
                    out=hidT[:, par * 22 * 128:par * 22 * 128 + 11 * 128].rearrange("p (k t) -> p k t", k=11),
                    in_=hid3[:, :, tt * 128:(tt + 1) * 128]).then_inc(s_b_h[par].h, 16)
                if j >= 1:
                    jj = j - 1
                    tt2 = jj % 64
                    dh2 = jj // 64
                    for dg2 in range(2):
                        gpsimd.wait_ge(s_bev.h, 2 * jj + dg2 + 1)
                        gpsimd.dma_start(
                            out=partial[tt2 * 128:(tt2 + 1) * 128,
                                        dh2 * 1024 + dg2 * 512: dh2 * 1024 + dg2 * 512 + 512],
                            in_=pout[dg2][:]).then_inc(s_bp[dg2].h, 16)
        jj = 127
        for dg2 in range(2):
            gpsimd.wait_ge(s_bev.h, 2 * jj + dg2 + 1)
            gpsimd.dma_start(
                out=partial[(jj % 64) * 128:((jj % 64) + 1) * 128,
                            (jj // 64) * 1024 + dg2 * 512: (jj // 64) * 1024 + dg2 * 512 + 512],
                in_=pout[dg2][:]).then_inc(s_bp[dg2].h, 16)
    s_b_h[0].inc(16 * 64)
    s_b_h[1].inc(16 * 64)
    s_bp[0].inc(16 * 128)
    s_bp[1].inc(16 * 128)

    @b.tensor
    def _(tensor):
        for dh in range(2):
            tensor.wait_ge(s_b_wc.h, 8 * (dh + 1))
            for tt in range(64):
                j = dh * 64 + tt
                par = j % 2
                tensor.wait_ge(s_b_h[par].h, 16 * (j // 2 + 1))
                if j >= 2:
                    tensor.wait_ge(s_bev.h, 2 * (j - 1))
                for dg2 in range(2):
                    bank = (j % 2) * 2 + dg2
                    for k in range(11):
                        mm = tensor.matmul(
                            ps[bank][:],
                            hidT[:, par * 22 * 128 + k * 128: par * 22 * 128 + k * 128 + 128],
                            w2bf[:, k * 1024 + dg2 * 512: k * 1024 + dg2 * 512 + 512],
                            start=(k == 0), stop=(k == 10))
                    mm.then_inc(s_bmm.h, 1)
    s_bmm.inc(2 * 128)

    @b.vector
    def _(vector):
        for dh in range(2):
            for tt in range(64):
                j = dh * 64 + tt
                for dg2 in range(2):
                    bank = (j % 2) * 2 + dg2
                    vector.wait_ge(s_bmm.h, 2 * j + dg2 + 1)
                    vector.wait_ge(s_bp[dg2].h, 16 * j)
                    vector.tensor_scalar(pout[dg2][:], ps[bank][:], w_all[:, tt:tt + 1], None,
                                         ALU.mult).then_inc(s_bev.h, 1)
    s_bev.inc(2 * 128)

    for m in (s_bmm, s_bev, s_bp[0], s_bp[1], s_b_wc, s_b_wd, s_b_h[0], s_b_h[1]):
        C.mark(m)
    C.barrier()

    swsrc = [sw1d, sw3d]
    hsh3 = hshscr.rearrange("(o p) t -> p o t", p=128)
    och_tiles = [(0, 6), (6, 12), (12, 18), (18, 22)]
    s_sw = C.sem(pref + "ssw")
    s_swc = C.sem(pref + "sswc")
    s_sx = [C.sem(pref + "ssx0"), C.sem(pref + "ssx1")]
    s_smm = C.sem(pref + "ssmm")
    s_ssl = C.sem(pref + "sssl")
    s_she = C.sem(pref + "sshe")
    s_ssp = [C.sem(pref + "ssp0"), C.sem(pref + "ssp1")]

    sw_chunks = []
    for och in range(4):
        o0, o1 = och_tiles[och]
        HW = (o1 - o0) * 128
        for wi in range(2):
            off = 0
            while off < HW:
                w = min(128, HW - off)
                sw_chunks.append((wi, och, off, w, len(sw_chunks)))
                off += w
    sunits = []
    for och in range(4):
        for tb in range(2):
            for ot in range(*och_tiles[och]):
                sunits.append((och, tb, ot, len(sunits)))
    NUS = len(sunits)
    chunks_through_och = []
    acc = 0
    for och in range(4):
        acc += len([c for c in sw_chunks if c[1] == och])
        chunks_through_och.append(acc)
    su_through_och = []
    acc = 0
    for och in range(4):
        acc += 2 * (och_tiles[och][1] - och_tiles[och][0])
        su_through_och.append(acc)

    @b.sync
    def _(sync):
        for och in range(4):
            for (wi, och_, off, w, ci) in [c for c in sw_chunks if c[1] == och]:
                if ci >= 1:
                    sync.wait_ge(s_swc.h, ci)
                if och >= 1 and off == 0 and wi == 0:
                    sync.wait_ge(s_smm.h, 2 * su_through_och[och - 1])
                o0 = och_tiles[och][0]
                col0 = o0 * 128 + off
                sync.dma_start(out=mw32[:, 0:16 * w].rearrange("p (k o) -> p k o", k=KD),
                               in_=swsrc[wi].rearrange("(k p) o -> p k o", p=128)[:, :, col0:col0 + w]
                               ).then_inc(s_sw.h, 16)
            for tb in range(2):
                si = och * 2 + tb
                par = si % 2
                if si >= 2:
                    prev_units = su_through_och[och - 1] if tb == 0 else \
                        su_through_och[och] - (och_tiles[och][1] - och_tiles[och][0])
                    sync.wait_ge(s_smm.h, 2 * prev_units)
                sync.dma_start(
                    out=xblk[:, par * 8192:(par + 1) * 8192].rearrange("p (k t) -> p k t", k=KD),
                    in_=cin_x.rearrange("(k p) t -> p k t", p=128)[:, :, tb * 512:(tb + 1) * 512]
                    ).then_inc(s_sx[par].h, 16)
    s_sw.inc(16 * len(sw_chunks))
    for si in range(8):
        s_sx[si % 2].inc(16)

    @b.vector
    def _(vector):
        ui = 0
        for och in range(4):
            for (wi, och_, off, w, ci) in [c for c in sw_chunks if c[1] == och]:
                vector.wait_ge(s_sw.h, 16 * (ci + 1))
                vector.tensor_copy(
                    wdst[wi][:].rearrange("p (k o) -> p k o", o=768)[:, :, off:off + w],
                    mw32[:, 0:16 * w].rearrange("p (k o) -> p k o", k=KD)).then_inc(s_swc.h, 1)
            while ui < su_through_och[och]:
                u = ui
                pb = u % 2
                vector.wait_ge(s_smm.h, 2 * u + 2)
                vector.wait_ge(s_ssl.h, u + 1)
                if u >= 2:
                    vector.wait_ge(s_ssp[u % 2].h, 16 * (u // 2))
                vector.tensor_tensor(hb[pb][:], ps[2 + pb][:], gs[pb][:],
                                     ALU.mult).then_inc(s_she.h, 1)
                ui += 1
    s_swc.inc(len(sw_chunks))
    s_she.inc(NUS)

    @b.tensor
    def _(tensor):
        for (och, tb, ot, u) in sunits:
            si = och * 2 + tb
            par = si % 2
            pb = u % 2
            oi = ot - och_tiles[och][0]
            tensor.wait_ge(s_swc.h, chunks_through_och[och])
            tensor.wait_ge(s_sx[par].h, 16 * (si // 2 + 1))
            if u >= 2:
                tensor.wait_ge(s_ssl.h, u - 1)
                tensor.wait_ge(s_she.h, u - 1)
            for k in range(KD):
                mm = tensor.matmul(ps[pb][:], w1h[:, k * 768 + oi * 128: k * 768 + oi * 128 + 128],
                                   xblk[:, par * 8192 + k * 512: par * 8192 + k * 512 + 512],
                                   start=(k == 0), stop=(k == KD - 1))
                if k == KD - 1:
                    mm.then_inc(s_smm.h, 1)
            for k in range(KD):
                mm = tensor.matmul(ps[2 + pb][:], w3h[:, k * 768 + oi * 128: k * 768 + oi * 128 + 128],
                                   xblk[:, par * 8192 + k * 512: par * 8192 + k * 512 + 512],
                                   start=(k == 0), stop=(k == KD - 1))
                if k == KD - 1:
                    mm.then_inc(s_smm.h, 1)
    s_smm.inc(2 * NUS)

    @b.scalar
    def _(scalar):
        for (och, tb, ot, u) in sunits:
            pb = u % 2
            scalar.wait_ge(s_smm.h, 2 * u + 1)
            if u >= 2:
                scalar.wait_ge(s_she.h, u - 1)
            scalar.activation(gs[pb][:], ps[pb][:], AF.Silu).then_inc(s_ssl.h, 1)
    s_ssl.inc(NUS)

    @b.gpsimd
    def _(gpsimd):
        for (och, tb, ot, u) in sunits:
            pb = u % 2
            gpsimd.wait_ge(s_she.h, u + 1)
            gpsimd.dma_start(out=hsh3[:, ot, tb * 512:(tb + 1) * 512],
                             in_=hb[pb][:]).then_inc(s_ssp[u % 2].h, 16)
    s_ssp[0].inc(16 * ((NUS + 1) // 2))
    s_ssp[1].inc(16 * (NUS // 2))

    for m in (s_smm, s_ssl, s_she, s_ssp[0], s_ssp[1], s_swc, s_sw, s_sx[0], s_sx[1]):
        C.mark(m)
    C.barrier()

    s_d2w = C.sem(pref + "d2w")
    s_d2c = C.sem(pref + "d2c")
    s_d2h = [C.sem(pref + "d2h0"), C.sem(pref + "d2h1")]
    s_d2m = C.sem(pref + "d2m")
    s_d2e = C.sem(pref + "d2e")
    s_d2p = [C.sem(pref + "d2p0"), C.sem(pref + "d2p1")]
    sw23 = sw2d.rearrange("(k p) o -> p k o", p=128)
    sw2bf = w3h

    @b.sync
    def _(sync):
        for dgrp in range(4):
            for ch in range(8):
                gi = dgrp * 8 + ch
                if gi >= 1:
                    sync.wait_ge(s_d2c.h, gi)
                if ch == 0 and dgrp >= 1:
                    sync.wait_ge(s_d2m.h, 8 * dgrp)
                sync.dma_start(out=mw32[:, 0:22 * 64].rearrange("p (k o) -> p k o", k=22),
                               in_=sw23[:, :, dgrp * 512 + ch * 64: dgrp * 512 + ch * 64 + 64]
                               ).then_inc(s_d2w.h, 16)
    s_d2w.inc(16 * 32)

    @b.scalar
    def _(scalar):
        for gi in range(32):
            dgrp, ch = divmod(gi, 8)
            scalar.wait_ge(s_d2w.h, 16 * (gi + 1))
            scalar.activation(
                sw2bf[:, 0:22 * 512].rearrange("p (k o) -> p k o", o=512)[:, :, ch * 64:(ch + 1) * 64],
                mw32[:, 0:22 * 64].rearrange("p (k o) -> p k o", k=22),
                AF.Identity).then_inc(s_d2c.h, 1)
    s_d2c.inc(32)

    @b.gpsimd
    def _(gpsimd):
        for dgrp in range(4):
            for tt in range(8):
                j = dgrp * 8 + tt
                par = j % 2
                if j >= 2:
                    gpsimd.wait_ge(s_d2m.h, j - 1)
                gpsimd.dma_start(
                    out=hidT[:, par * 22 * 128:(par + 1) * 22 * 128].rearrange("p (k t) -> p k t", k=22),
                    in_=hsh3[:, :, tt * 128:(tt + 1) * 128]).then_inc(s_d2h[par].h, 16)
                if j >= 1:
                    jj = j - 1
                    gpsimd.wait_ge(s_d2e.h, jj + 1)
                    gpsimd.dma_start(out=sh_out[(jj % 8) * 128:((jj % 8) + 1) * 128,
                                                (jj // 8) * 512:((jj // 8) + 1) * 512],
                                     in_=pout[jj % 2][:]).then_inc(s_d2p[jj % 2].h, 16)
        jj = 31
        gpsimd.wait_ge(s_d2e.h, jj + 1)
        gpsimd.dma_start(out=sh_out[(jj % 8) * 128:((jj % 8) + 1) * 128,
                                    (jj // 8) * 512:((jj // 8) + 1) * 512],
                         in_=pout[jj % 2][:]).then_inc(s_d2p[jj % 2].h, 16)
    s_d2h[0].inc(16 * 16)
    s_d2h[1].inc(16 * 16)
    s_d2p[0].inc(16 * 16)
    s_d2p[1].inc(16 * 16)

    @b.tensor
    def _(tensor):
        for dgrp in range(4):
            tensor.wait_ge(s_d2c.h, 8 * (dgrp + 1))
            for tt in range(8):
                j = dgrp * 8 + tt
                par = j % 2
                bank = 4 + j % 2
                tensor.wait_ge(s_d2h[par].h, 16 * (j // 2 + 1))
                if j >= 2:
                    tensor.wait_ge(s_d2e.h, j - 1)
                mm = None
                for k in range(22):
                    mm = tensor.matmul(
                        ps[bank][:],
                        hidT[:, par * 22 * 128 + k * 128: par * 22 * 128 + k * 128 + 128],
                        sw2bf[:, k * 512:(k + 1) * 512],
                        start=(k == 0), stop=(k == 21))
                mm.then_inc(s_d2m.h, 1)
    s_d2m.inc(32)

    @b.vector
    def _(vector):
        for j in range(32):
            vector.wait_ge(s_d2m.h, j + 1)
            vector.wait_ge(s_d2p[j % 2].h, 16 * (j // 2))
            bank = 4 + j % 2
            vector.tensor_copy(pout[j % 2][:], ps[bank][:]).then_inc(s_d2e.h, 1)
    s_d2e.inc(32)

    for m in (s_d2m, s_d2e, s_d2p[0], s_d2p[1], s_d2c, s_d2w, s_d2h[0], s_d2h[1]):
        C.mark(m)
    C.barrier()

    s_rs = C.sem(pref + "rs")

    @b.gpsimd
    def _(gpsimd):
        gpsimd.collective_compute("ReduceScatter", ALU.add,
                                  replica_groups=[[0, 1, 2, 3, 4, 5, 6, 7]],
                                  ins=[partial], outs=[rs_out]).then_inc(s_rs.h, 1)
    s_rs.inc(1)
    C.mark(s_rs)
    C.barrier()

    s_fd = [C.sem(pref + "fd0"), C.sem(pref + "fd1")]
    s_ftp = C.sem(pref + "ftp")
    s_fa = C.sem(pref + "fa")

    @b.sync
    def _(sync):
        for i in range(2 * KD):
            si, k = divmod(i, KD)
            src = rs_out if si == 0 else sh_out
            par = i % 2
            if i >= 2:
                sync.wait_ge(s_ftp.h, 8 * (i - 1))
            sync.dma_start(
                out=mw32[:, par * 1024:(par + 1) * 1024].rearrange("p (tt d) -> p tt d", tt=8),
                in_=src.rearrange("(tt p) d -> p tt d", p=128)[:, :, k * 128:(k + 1) * 128]
                ).then_inc(s_fd[par].h, 16)
    s_fd[0].inc(16 * KD)
    s_fd[1].inc(16 * KD)

    @b.tensor
    def _(tensor):
        for i in range(2 * KD):
            si, k = divmod(i, KD)
            par = i % 2
            tensor.wait_ge(s_fd[par].h, 16 * (i // 2 + 1))
            for tt in range(8):
                j = i * 8 + tt
                if j >= 2:
                    tensor.wait_ge(s_fa.h, j - 1)
                tensor.transpose(ps[6 + j % 2][:, 0:128],
                                 mw32[:, par * 1024 + tt * 128: par * 1024 + tt * 128 + 128],
                                 ident[:]).then_inc(s_ftp.h, 1)
    s_ftp.inc(16 * KD)

    @b.vector
    def _(vector):
        for i in range(2 * KD):
            si, k = divmod(i, KD)
            for tt in range(8):
                j = i * 8 + tt
                vector.wait_ge(s_ftp.h, j + 1)
                sl = slice(k * T + tt * 128, k * T + tt * 128 + 128)
                vector.tensor_tensor(R[:, sl], R[:, sl], ps[6 + j % 2][:, 0:128],
                                     ALU.add).then_inc(s_fa.h, 1)
    s_fa.inc(16 * KD)
    C.mark(s_fa)
    C.barrier()


def build_moe_launch():
    C = Ctx()
    nc = C.nc
    x_in = nc.dram_tensor("x_in", [DM, T], F32, kind="ExternalInput").ap()
    x_out = nc.dram_tensor("x_out", [DM, T], F32, kind="ExternalOutput").ap()
    w1d = nc.dram_tensor("moe_w1", [DM, 1408], F32, kind="ExternalInput").ap()
    w3d = nc.dram_tensor("moe_w3", [DM, 1408], F32, kind="ExternalInput").ap()
    w2d = nc.dram_tensor("moe_w2", [1408, DM], F32, kind="ExternalInput").ap()
    sw1d = nc.dram_tensor("sw1", [DM, 2816], F32, kind="ExternalInput").ap()
    sw3d = nc.dram_tensor("sw3", [DM, 2816], F32, kind="ExternalInput").ap()
    sw2d = nc.dram_tensor("sw2", [2816, DM], F32, kind="ExternalInput").ap()
    dram = {
        "cin_x": nc.dram_tensor("cin_x", [DM, T], BF16).ap(),
        "cout_x": nc.dram_tensor("cout_x", [8 * DM, T], BF16).ap(),
        "cin_l": nc.dram_tensor("cin_l", [8, T], F32).ap(),
        "cout_l": nc.dram_tensor("cout_l", [64, T], F32).ap(),
        "hidscr": nc.dram_tensor("hidscr", [1408, 8 * T], BF16).ap(),
        "partial": nc.dram_tensor("partial", [8 * T, DM], F32).ap(),
        "rs_out": nc.dram_tensor("rs_out", [T, DM], F32).ap(),
        "hshscr": nc.dram_tensor("hshscr", [2816, T], BF16).ap(),
        "sh_out": nc.dram_tensor("sh_out", [T, DM], F32).ap(),
    }
    R = C.sbuf("R", [128, KD * T], F32)
    C.start_block()
    sc = make_ln_scratch(C)
    C.const("gwt128", np.zeros((128, 128), np.float32), F32)
    C.const("ident_f32", np.eye(128, dtype=np.float32), F32)
    C.const("onehot64", np.zeros((128, 64), np.float32), F32)
    xblk = C.sbuf("exblk", [128, 2 * 16 * 512], BF16)
    xn = xblk
    C.emit_const_loads()
    emit_stream_load(C, R, x_in)
    C.barrier()
    emit_ln(C, sc, R, xn[:, 0:KD * T], mode="plain")
    C.barrier()
    s_xs = C.sem("xnstage")
    b = C.block

    @b.gpsimd
    def _(gpsimd):
        gpsimd.dma_start(out=dram["cin_x"].rearrange("(k p) t -> p k t", p=128),
                         in_=xn[:, 0:KD * T].rearrange("p (k t) -> p k t", k=KD)).then_inc(s_xs.h, 16)
    s_xs.inc(16)
    C.mark(s_xs)
    C.barrier()
    emit_moe(C, sc, R, xblk, w1d, w3d, w2d, sw1d, sw3d, sw2d, dram)
    emit_stream_store(C, R, x_out)
    return C


def moe_in_maps(inp, x_stream_percore):
    maps = []
    gw = f32c(np.asarray(inp["L1_ffn_gate_w"]).T)
    gwt128 = np.ascontiguousarray(gw.reshape(16, 128, 8).transpose(1, 0, 2).reshape(128, 128))
    for c in range(8):
        onehot = np.zeros((128, 64), np.float32)
        for t8i in range(8):
            onehot[:, t8i * 8 + c] = 1.0
        m = {
            "x_in": x_stream_percore[c],
            "moe_w1": f32c(inp["L1_ffn_w1"][c]),
            "moe_w3": f32c(inp["L1_ffn_w3"][c]),
            "moe_w2": f32c(inp["L1_ffn_w2"][c]),
            "sw1": f32c(inp["L1_ffn_sw1"]),
            "sw3": f32c(inp["L1_ffn_sw3"]),
            "sw2": f32c(inp["L1_ffn_sw2"]),
            "gwt128": gwt128,
            "ident_f32": np.eye(128, dtype=np.float32),
            "onehot64": onehot,
        }
        maps.append(m)
    return maps



def build_fused():
    C = Ctx()
    nc = C.nc
    x_in = nc.dram_tensor("x_in", [DM, T], F32, kind="ExternalInput").ap()
    x_out = nc.dram_tensor("x_out", [DM, T], F32, kind="ExternalOutput").ap()

    inWT = nc.dram_tensor("inWT", [DM, 8576], F32, kind="ExternalInput").ap()
    outWT = nc.dram_tensor("outWT", [4096, DM], F32, kind="ExternalInput").ap()
    m_dram = {
        "Lf": nc.dram_tensor("Lf_scr", [3, 4096], BF16).ap(),
        "gateT": nc.dram_tensor("gateT", [4096, T], BF16).ap(),
        "xs_f": nc.dram_tensor("xs_f_scr", [4096, T], BF16).ap(),
        "Ypart": nc.dram_tensor("Ypart", [T, 4096], BF16).ap(),
        "ynscr": nc.dram_tensor("ynscr", [4096, T], BF16).ap(),
        "cc_s_in": nc.dram_tensor("cc_s_in", [128, 4096], F32).ap(),
        "cc_s_out": nc.dram_tensor("cc_s_out", [512, 4096], F32).ap(),
        "cc_o_in": nc.dram_tensor("cc_o_in", [1, 64], F32).ap(),
        "cc_o_out": nc.dram_tensor("cc_o_out", [4, 64], F32).ap(),
        "h_in": nc.dram_tensor("h_in", [128, 102], BF16).ap(),
        "h_out": nc.dram_tensor("h_out", [512, 102], BF16).ap(),
        "Ddiag": nc.dram_tensor("Ddiag", [4096, 128], F32, kind="ExternalInput").ap(),
    }
    wgT = nc.dram_tensor("wgT", [DM, 8192], F32, kind="ExternalInput").ap()
    wuT = nc.dram_tensor("wuT", [DM, 8192], F32, kind="ExternalInput").ap()
    wdT = nc.dram_tensor("wdT", [8192, DM], F32, kind="ExternalInput").ap()
    hscr = nc.dram_tensor("hscr", [8192, T], BF16)
    wqT = nc.dram_tensor("wqT", [DM, 512], F32, kind="ExternalInput").ap()
    wkT = nc.dram_tensor("wkT", [DM, 512], F32, kind="ExternalInput").ap()
    wvT = nc.dram_tensor("wvT", [DM, 512], F32, kind="ExternalInput").ap()
    woT = nc.dram_tensor("woT", [DM, DM], F32, kind="ExternalInput").ap()
    cosT = nc.dram_tensor("cosT", [128, 4096], BF16, kind="ExternalInput").ap()
    sinT = nc.dram_tensor("sinT", [128, 4096], BF16, kind="ExternalInput").ap()
    a_dram = {
        "cin_x": nc.dram_tensor("a_cin_x", [DM, T], BF16).ap(),
        "ag_x": nc.dram_tensor("ag_x", [4 * DM, T], BF16).ap(),
        "q_scr": nc.dram_tensor("q_scr", [512, 4096], BF16).ap(),
        "k_scr": nc.dram_tensor("k_scr", [512, 4096], BF16).ap(),
        "v_scr": nc.dram_tensor("v_scr", [512, 4096], BF16).ap(),
        "o_scr": nc.dram_tensor("o_scr", [4096, 512], BF16).ap(),
        "o_ag": nc.dram_tensor("o_ag", [4 * 4096, 512], BF16).ap(),
        "cosT": cosT, "sinT": sinT,
    }
    w1d = nc.dram_tensor("moe_w1", [DM, 1408], F32, kind="ExternalInput").ap()
    w3d = nc.dram_tensor("moe_w3", [DM, 1408], F32, kind="ExternalInput").ap()
    w2d = nc.dram_tensor("moe_w2", [1408, DM], F32, kind="ExternalInput").ap()
    sw1d = nc.dram_tensor("sw1", [DM, 2816], F32, kind="ExternalInput").ap()
    sw3d = nc.dram_tensor("sw3", [DM, 2816], F32, kind="ExternalInput").ap()
    sw2d = nc.dram_tensor("sw2", [2816, DM], F32, kind="ExternalInput").ap()
    e_dram = {
        "cin_x": nc.dram_tensor("e_cin_x", [DM, T], BF16).ap(),
        "cout_x": nc.dram_tensor("cout_x", [8 * DM, T], BF16).ap(),
        "cin_l": nc.dram_tensor("cin_l", [8, T], F32).ap(),
        "cout_l": nc.dram_tensor("cout_l", [64, T], F32).ap(),
        "hidscr": nc.dram_tensor("hidscr", [1408, 8 * T], BF16).ap(),
        "partial": nc.dram_tensor("partial", [8 * T, DM], F32).ap(),
        "rs_out": nc.dram_tensor("rs_out", [T, DM], F32).ap(),
        "hshscr": nc.dram_tensor("hshscr", [2816, T], BF16).ap(),
        "sh_out": nc.dram_tensor("sh_out", [T, DM], F32).ap(),
    }

    R = C.sbuf("R", [128, KD * T], F32)
    C.start_block()
    sc = make_ln_scratch(C)
    C.const("ident_f32", np.eye(128, dtype=np.float32), F32)
    C.const("ident_bf16", np.eye(128, dtype=np.float32), BF16)
    C.const("ones_col_bf16", np.ones((128, 1), np.float32), BF16)
    C.const("ones_col_f32x", np.ones((128, 1), np.float32), F32)
    C.emit_const_loads()
    emit_stream_load(C, R, x_in)
    C.barrier()

    with C.scope():
        C.const("normW", np.zeros((128, 16), np.float32), F32)
        C.const("gnw", np.zeros((128, 32), np.float32), F32)
        C.const("convw", np.zeros((128, 136), np.float32), F32)
        C.const("convb", np.zeros((128, 34), np.float32), F32)
        C.const("dtb", np.zeros((64, 1), np.float32), F32)
        C.const("alog", np.zeros((64, 1), np.float32), F32)
        C.const("ohprev", np.zeros((128, 4), np.float32), F32)
        C.const("coefT", np.zeros((4, 4), np.float32), F32)
        C.const("masklt", np.zeros((4, 1), np.float32), F32)
        C.const("triu_f32", np.triu(np.ones((128, 128), np.float32)), F32)
        C.const("tril01_bf16", np.triu(np.ones((128, 128), np.float32)), BF16)
        m0 = np.where(np.triu(np.ones((128, 128), np.float32)) > 0, 0.0, -1e38).astype(np.float32)
        C.const("M04_bf16", np.tile(m0, (1, 4)), BF16)
        C.const("ones3_bf16", np.ones((3, 128), np.float32), BF16)
        C.const("negones3_bf16", -np.ones((3, 128), np.float32), BF16)
        C.emit_const_loads()
        A = C.sbuf("marena", [128, 6144 + 30976], BF16)
        xn = A[:, 6144 + 13952: 6144 + 13952 + KD * T]
        emit_ln(C, sc, R, xn, mode="mamba", h32=None, tag="B0")
        C.barrier()
        emit_rms_inplace(C, sc, xn)
        C.barrier()
        emit_mamba(C, sc, R, xn, None, A[:], inWT, outWT, m_dram)
        C.barrier()

    with C.scope():
        xn = C.sbuf("f_xn", [128, KD * T], BF16)
        emit_ln(C, sc, R, xn, mode="plain", tag="B1")
        C.barrier()
        emit_ffn(C, R, xn, wgT, wuT, wdT, hscr.ap())
        C.barrier()

    with C.scope():
        C.const("attn_masks", np.zeros((128, 2048), np.float32), BF16)
        C.const("negC", np.full((128, 1), -CSH, np.float32), F32)
        C.const("oidx", np.zeros((128, 32), np.int32), I32)
        C.emit_const_loads()
        arena = C.sbuf("arena", [128, 43008], BF16)
        xn = arena[:, 0:KD * T]
        emit_ln(C, sc, R, xn, mode="plain", tag="B2")
        C.barrier()
        emit_attn(C, sc, R, xn, arena, wqT, wkT, wvT, woT, a_dram)
        C.barrier()

    with C.scope():
        C.const("gwt128", np.zeros((128, 128), np.float32), F32)
        C.const("onehot64", np.zeros((128, 64), np.float32), F32)
        C.emit_const_loads()
        xblk = C.sbuf("exblk", [128, 2 * 16 * 512], BF16)
        xn = xblk
        emit_ln(C, sc, R, xn[:, 0:KD * T], mode="plain", tag="B3")
        C.barrier()
        s_xs = C.sem("xnstage")
        b = C.block

        @b.gpsimd
        def _(gpsimd):
            gpsimd.dma_start(out=e_dram["cin_x"].rearrange("(k p) t -> p k t", p=128),
                             in_=xn[:, 0:KD * T].rearrange("p (k t) -> p k t", k=KD)
                             ).then_inc(s_xs.h, 16)
        s_xs.inc(16)
        C.mark(s_xs)
        C.barrier()
        emit_moe(C, sc, R, xblk, w1d, w3d, w2d, sw1d, sw3d, sw2d, e_dram)
        C.barrier()

    emit_stream_store(C, R, x_out)
    return C


def fused_in_maps(inp, x_stream_percore):
    mm = mamba_in_maps(inp, x_stream_percore)
    fm = ffn_in_maps(inp, x_stream_percore)
    am = attn_in_maps(inp, x_stream_percore)
    em = moe_in_maps(inp, x_stream_percore)
    maps = []
    for c in range(8):
        m = {}
        m.update(fm[c])
        m.update(am[c])
        m.update(em[c])
        m.update(mm[c])
        maps.append(m)
    return maps





def kernel(**inputs):
    x0 = np.asarray(inputs["L0_attn_hidden_states"], dtype=np.float32)
    C = build_fused()
    maps = fused_in_maps(inputs, shard_stream(x0))
    for m in maps:
        for n, a in C.const_arrays.items():
            if n not in m:
                m[n] = a
    res = run_bass_kernel_spmd(C.nc, maps, core_ids=list(range(8)))
    return unshard_stream([res.results[c]["x_out"] for c in range(8)])


# version 05 (final)
# speedup vs baseline: 1.0335x; 1.0335x over previous
import contextlib
import numpy as np
import ml_dtypes
import concourse.bass as bass
import concourse.mybir as mybir
from concourse.bass_utils import run_bass_kernel_spmd

F32 = mybir.dt.float32
BF16 = mybir.dt.bfloat16
I32 = mybir.dt.int32
AF = mybir.ActivationFunctionType
ALU = mybir.AluOpType
AX = mybir.AxisListType

T = 1024
KD = 16
DM = 2048


class Sem:
    def __init__(self, h):
        self.h = h
        self.v = 0

    def inc(self, n=1):
        self.v += n
        return self.v


class Chain:
    def __init__(self, C, name):
        self.s = C.sem(name)

    def dep(self, inst):
        inst.then_inc(self.s.h, 1)
        self.s.inc()
        return inst

    def wait(self, eng):
        eng.wait_ge(self.s.h, self.s.v)


class Ctx:
    def __init__(self):
        self.nc = bass.Bass("TRN2", target_bir_lowering=False, debug=False)
        self.es = contextlib.ExitStack()
        self.root_es = self.es
        self.block = None
        self.const_tiles = {}
        self.const_dram = {}
        self.const_arrays = {}

    def start_block(self):
        self.block = self.es.enter_context(self.nc.Block())
        self.ps8 = [self.psum(f"ps8_{i}", [128, 512]) for i in range(8)]
        self.marks = []

    def mark(self, sem, v=None):
        self.marks.append((sem, sem.v if v is None else v))

    def sem(self, name):
        if getattr(self, "_in_scope", False):
            h = self.nc.alloc_semaphore(name=name)
            self._scope_sems.append(h)
            return Sem(h)
        return Sem(self.es.enter_context(self.nc.semaphore(name)))

    def sbuf(self, name, shape, dt):
        return self.es.enter_context(self.nc.sbuf_tensor(name, shape, dt))

    def psum(self, name, shape, dt=F32):
        return self.es.enter_context(self.nc.psum_tensor(name, shape, dt))

    @contextlib.contextmanager
    def scope(self):
        old = self.es
        self.es = contextlib.ExitStack()
        self._scope_sems = []
        self._scope_n = getattr(self, "_scope_n", 0) + 1
        self._in_scope = True
        try:
            yield
        finally:
            self._in_scope = False
            handles = self._scope_sems
            self._scope_sems = []
            if handles:
                s_clr = Sem(self.root_es.enter_context(
                    self.nc.semaphore(f"scopeclr{self._scope_n}")))
                b = self.block

                hs = sorted(handles, key=lambda h: h.num)
                runs = [[hs[0]]]
                for h in hs[1:]:
                    if h.num == runs[-1][-1].num + 1 and len(runs[-1]) < 10:
                        runs[-1].append(h)
                    else:
                        runs.append([h])

                @b.gpsimd
                def _(g):
                    last = None
                    for run in runs:
                        r = range(run[0].num, run[-1].num + 1)
                        g.dma_reset(r)
                        last = g.sem_clear(r)
                    last.then_inc(s_clr.h, 1)
                s_clr.inc(1)
                self.mark(s_clr)
                self.barrier()
                for h in handles:
                    self.nc.release_semaphore(h)
            self.es.close()
            self.es = old

    def const(self, name, arr, dt):
        if name in self.const_tiles:
            return self.const_tiles[name]
        import ml_dtypes
        _np = {F32: np.float32, BF16: ml_dtypes.bfloat16,
               mybir.dt.int32: np.int32}.get(dt)
        arr = np.ascontiguousarray(np.asarray(arr).astype(_np))
        d = self.nc.dram_tensor(name, list(arr.shape), dt, kind="ExternalInput").ap()
        t = self.es.enter_context(self.nc.sbuf_tensor(name + "_sb", list(arr.shape), dt))
        self.const_dram[name] = d
        self.const_tiles[name] = t
        self.const_arrays[name] = arr
        return t

    def emit_const_loads(self):
        if not hasattr(self, "_consts_loaded"):
            self._consts_loaded = set()
            self._constload_n = 0
        todo = [n for n in self.const_dram if n not in self._consts_loaded]
        if not todo:
            return None
        self._consts_loaded.update(todo)
        self._constload_n += 1
        s = self.sem(f"constload{self._constload_n}")
        b = self.block

        @b.sync
        def _(sync):
            for name in todo:
                sync.dma_start(out=self.const_tiles[name][:],
                               in_=self.const_dram[name]).then_inc(s.h, 16)
        s.inc(16 * len(todo))

        self.mark(s)
        return s

    def barrier(self):
        marks = [(s, v) for (s, v) in self.marks]
        self.marks = []
        b = self.block
        for dec in [b.tensor, b.vector, b.scalar, b.sync, b.gpsimd]:
            @dec
            def _(eng):
                for sem, v in marks:
                    eng.wait_ge(sem.h, v)


def bfc(x):
    return np.ascontiguousarray(np.asarray(x).astype(ml_dtypes.bfloat16))


def f32c(x):
    return np.ascontiguousarray(np.asarray(x), dtype=np.float32)


def make_ln_scratch(C):
    sc = {}
    sc["m"] = C.sbuf("ln_m", [1, 1024], F32)
    sc["v"] = C.sbuf("ln_v", [1, 1024], F32)
    sc["rs"] = C.sbuf("ln_rs", [1, 1024], F32)
    sc["Bm"] = C.sbuf("ln_Bm", [128, 1024], F32)
    sc["Brs"] = C.sbuf("ln_Brs", [128, 1024], F32)
    sc["tmp"] = [C.sbuf(f"ln_tmp{i}", [128, 512], F32) for i in range(2)]
    sc["sq"] = sc["tmp"]
    sc["psS1"] = C.ps8[4][0:1, :]
    sc["psS2"] = C.ps8[5][0:1, :]
    sc["psBm"] = C.ps8[6]
    sc["psBrs"] = C.ps8[7]
    sc["ones_col"] = C.const("ones_col_f32", np.ones((128, 1), np.float32), F32)
    sc["ones_row"] = C.const("ones_row_f32", np.ones((1, 128), np.float32), F32)
    sc["eps"] = C.const("eps_ln", np.full((128, 1), 1e-5, np.float32), F32)
    return sc


def emit_ln(C, sc, R, xn, mode="plain", h32=None, tag=""):
    b = C.block
    s_sq = C.sem("lnsq" + tag)
    s_mm = C.sem("lnmm" + tag)
    s_r1 = C.sem("lnr1" + tag)
    s_ev = C.sem("lnev" + tag)
    sq, tmp = sc["sq"], sc["tmp"]
    psS1, psS2 = sc["psS1"], sc["psS2"]
    psBm, psBrs = sc["psBm"], sc["psBrs"]
    m_sb, v_sb, rs_sb = sc["m"], sc["v"], sc["rs"]
    Bm, Brs = sc["Bm"], sc["Brs"]
    sq_base = s_sq.v
    mm_base = s_mm.v
    ev_base = s_ev.v

    for tb in range(2):
        @b.scalar
        def _(scalar, tb=tb):
            for k in range(KD):
                g = tb * KD + k
                if g >= 2:
                    scalar.wait_ge(s_mm.h, mm_base + 2 * (g - 1))
                scalar.activation(sq[g % 2][:],
                                  R[:, k * T + tb * 512: k * T + tb * 512 + 512],
                                  AF.Square).then_inc(s_sq.h, 1)
        s_sq.inc(KD)

        @b.tensor
        def _(tensor, tb=tb):
            if tb == 1:
                tensor.wait_ge(s_ev.h, ev_base + 2)
            for k in range(KD):
                g = tb * KD + k
                tensor.matmul(psS1, sc["ones_col"][:],
                              R[:, k * T + tb * 512: k * T + tb * 512 + 512],
                              start=(k == 0), stop=(k == KD - 1))
                tensor.wait_ge(s_sq.h, sq_base + tb * (KD + 3) + k + 1)
                tensor.matmul(psS2, sc["ones_col"][:], sq[g % 2][:],
                              start=(k == 0), stop=(k == KD - 1)).then_inc(s_mm.h, 2)
        s_mm.inc(2 * KD)

        sl = slice(tb * 512, (tb + 1) * 512)

        @b.vector
        def _(vector, sl=sl):
            vector.wait_ge(s_mm.h, s_mm.v)
            vector.tensor_scalar(m_sb[:, sl], psS1, 1.0 / DM, None,
                                 ALU.mult).then_inc(s_ev.h, 1)
        s_ev.inc()

        @b.scalar
        def _(scalar, sl=sl):
            scalar.wait_ge(s_ev.h, s_ev.v)
            scalar.activation(v_sb[:, sl], m_sb[:, sl], AF.Square).then_inc(s_sq.h, 1)
        s_sq.inc()

        @b.vector
        def _(vector, sl=sl):
            vector.wait_ge(s_sq.h, s_sq.v)
            vector.scalar_tensor_tensor(v_sb[:, sl], psS2, 1.0 / DM, v_sb[:, sl],
                                        ALU.mult, ALU.subtract).then_inc(s_ev.h, 1)
        s_ev.inc()

        @b.scalar
        def _(scalar, sl=sl):
            scalar.wait_ge(s_ev.h, s_ev.v)
            scalar.activation(rs_sb[:, sl], v_sb[:, sl], AF.Ln, bias=sc["eps"][0:1, :]).then_inc(s_sq.h, 1)
            scalar.wait_ge(s_sq.h, s_sq.v + 1)
            scalar.activation(rs_sb[:, sl], rs_sb[:, sl], AF.Exp, scale=-0.5).then_inc(s_sq.h, 1)
        s_sq.inc(2)

        @b.tensor
        def _(tensor, sl=sl, tb=tb):
            tensor.wait_ge(s_sq.h, s_sq.v)
            if tb == 1:
                tensor.wait_ge(s_ev.h, ev_base + 3)
            tensor.matmul(psBm[:], sc["ones_row"][:], m_sb[:, sl],
                          start=True, stop=True)
            tensor.matmul(psBrs[:], sc["ones_row"][:], rs_sb[:, sl],
                          start=True, stop=True).then_inc(s_r1.h, 1)
        s_r1.inc()

        @b.vector
        def _(vector, sl=sl):
            vector.wait_ge(s_r1.h, s_r1.v)
            vector.tensor_copy(Bm[:, sl], psBm[:])
            vector.tensor_copy(Brs[:, sl], psBrs[:]).then_inc(s_ev.h, 1)
        s_ev.inc()

    ch = Chain(C, "lnch" + tag)

    @b.vector
    def _(vector):
        vector.wait_ge(s_ev.h, s_ev.v)
        for k in range(KD):
            for tb in range(2):
                i = k * 2 + tb
                sl = slice(k * T + tb * 512, k * T + tb * 512 + 512)
                bsl = slice(tb * 512, (tb + 1) * 512)
                tt = tmp[i % 2]
                ch.wait(vector)
                ch.dep(vector.tensor_tensor(tt[:], R[:, sl], Bm[:, bsl], ALU.subtract))
                ch.wait(vector)
                if mode == "plain":
                    last = vector.tensor_tensor(xn[:, sl], tt[:], Brs[:, bsl], ALU.mult)
                elif h32 is None:
                    hb = tmp[1 - i % 2]
                    ch.dep(vector.tensor_tensor(hb[:], tt[:], Brs[:, bsl], ALU.mult))
                    ch.wait(vector)
                    vector.tensor_copy(xn[:, sl], hb[:])
                    last = vector.tensor_tensor(R[:, sl], R[:, sl], hb[:], ALU.add)
                    if not (k == KD - 1 and tb == 1):
                        ch.dep(last)
                else:
                    ch.dep(vector.tensor_tensor(h32[:, sl], tt[:], Brs[:, bsl], ALU.mult))
                    ch.wait(vector)
                    last = vector.tensor_copy(xn[:, sl], h32[:, sl])
                    if mode == "mamba":
                        last = vector.tensor_tensor(R[:, sl], R[:, sl], h32[:, sl], ALU.add)
                if k == KD - 1 and tb == 1:
                    last.then_inc(s_ev.h, 1)
                    s_ev.inc()
    C.mark(s_ev)
    C.mark(s_r1)
    C.mark(s_sq)
    C.mark(s_mm)


def emit_ffn(C, R, xn, wgT, wuT, wdT, hscr, pref="f"):
    b = C.block
    OB = 32
    NU = 4 * OB
    wg32 = C.sbuf(pref + "wg32", [128, KD * 256], F32)
    wu32 = C.sbuf(pref + "wu32", [128, KD * 256], F32)
    wgbf = C.sbuf(pref + "wgbf", [128, 2 * KD * 256], BF16)
    wubf = C.sbuf(pref + "wubf", [128, 2 * KD * 256], BF16)
    gs = [C.sbuf(pref + f"gs{i}", [128, 512], BF16) for i in range(2)]
    hb = [C.sbuf(pref + f"hb{i}", [128, 512], BF16) for i in range(2)]
    ps = C.ps8

    s_wdma = C.sem(pref + "wdma")
    s_cast = C.sem(pref + "cast")
    s_mm = C.sem(pref + "mm")
    s_silu = C.sem(pref + "silu")
    s_hev = C.sem(pref + "hev")
    s_hsp = [C.sem(pref + "hsp0"), C.sem(pref + "hsp1")]

    wg3 = wgT.rearrange("(k p) o -> p k o", p=128)
    wu3 = wuT.rearrange("(k p) o -> p k o", p=128)
    h3 = hscr.rearrange("(o p) t -> p o t", p=128)

    @b.sync
    def _(sync):
        for ob in range(OB):
            if ob >= 1:
                sync.wait_ge(s_cast.h, 2 * ob)
            sync.dma_start(out=wg32[:].rearrange("p (k o) -> p k o", k=KD),
                           in_=wg3[:, :, ob * 256:(ob + 1) * 256]).then_inc(s_wdma.h, 16)
            sync.dma_start(out=wu32[:].rearrange("p (k o) -> p k o", k=KD),
                           in_=wu3[:, :, ob * 256:(ob + 1) * 256]).then_inc(s_wdma.h, 16)
    s_wdma.inc(32 * OB)

    @b.vector
    def _(vector):
        def cast(ob):
            bb = ob % 2
            vector.wait_ge(s_wdma.h, 32 * (ob + 1))
            if ob >= 2:
                vector.wait_ge(s_mm.h, 8 * (ob - 1))
            vector.tensor_copy(wgbf[:, bb * KD * 256:(bb + 1) * KD * 256], wg32[:])
            vector.tensor_copy(wubf[:, bb * KD * 256:(bb + 1) * KD * 256],
                               wu32[:]).then_inc(s_cast.h, 2)

        def hev(u):
            pb = u % 2
            vector.wait_ge(s_mm.h, 2 * u + 2)
            vector.wait_ge(s_silu.h, u + 1)
            if u >= 2:
                vector.wait_ge(s_hsp[u % 2].h, 16 * (u // 2))
            vector.tensor_tensor(hb[pb][:], ps[2 + pb][:], gs[pb][:],
                                 ALU.mult).then_inc(s_hev.h, 1)

        cast(0)
        for ob in range(1, OB):
            cast(ob)
            for u in range(4 * (ob - 1), 4 * ob):
                hev(u)
        for u in range(4 * (OB - 1), NU):
            hev(u)
    s_cast.inc(2 * OB)
    s_hev.inc(NU)

    @b.tensor
    def _(tensor):
        for u in range(NU):
            ot, tb = divmod(u, 2)
            ob, sub = divmod(ot, 2)
            bb = ob % 2
            pb = u % 2
            tensor.wait_ge(s_cast.h, 2 * (ob + 1))
            if u >= 2:
                tensor.wait_ge(s_silu.h, u - 1)
                tensor.wait_ge(s_hev.h, u - 1)
            woff = bb * KD * 256 + sub * 128
            for k in range(KD):
                mm = tensor.matmul(ps[pb][:], wgbf[:, woff + k * 256: woff + k * 256 + 128],
                                   xn[:, k * T + tb * 512: k * T + tb * 512 + 512],
                                   start=(k == 0), stop=(k == KD - 1))
                if k == KD - 1:
                    mm.then_inc(s_mm.h, 1)
            for k in range(KD):
                mm = tensor.matmul(ps[2 + pb][:], wubf[:, woff + k * 256: woff + k * 256 + 128],
                                   xn[:, k * T + tb * 512: k * T + tb * 512 + 512],
                                   start=(k == 0), stop=(k == KD - 1))
                if k == KD - 1:
                    mm.then_inc(s_mm.h, 1)
    s_mm.inc(2 * NU)

    @b.scalar
    def _(scalar):
        for u in range(NU):
            pb = u % 2
            scalar.wait_ge(s_mm.h, 2 * u + 1)
            if u >= 2:
                scalar.wait_ge(s_hev.h, u - 1)
            scalar.activation(gs[pb][:], ps[pb][:], AF.Silu).then_inc(s_silu.h, 1)
    s_silu.inc(NU)

    @b.gpsimd
    def _(gpsimd):
        for u in range(NU):
            ot, tb = divmod(u, 2)
            pb = u % 2
            gpsimd.wait_ge(s_hev.h, u + 1)
            gpsimd.dma_start(out=h3[:, ot, tb * 512:(tb + 1) * 512],
                             in_=hb[pb][:]).then_inc(s_hsp[u % 2].h, 16)
    s_hsp[0].inc(16 * NU // 2)
    s_hsp[1].inc(16 * NU // 2)

    for m in (s_mm, s_silu, s_hev, s_hsp[0], s_hsp[1], s_wdma, s_cast):
        C.mark(m)
    C.barrier()

    KH = 64
    wd32 = wg32
    wdbf = wgbf
    hbuf = wubf
    s_ddma = C.sem(pref + "ddma")
    s_dcast = C.sem(pref + "dcast")
    s_hdma = [C.sem(pref + "hdma0"), C.sem(pref + "hdma1")]
    s_dmm2 = C.sem(pref + "dmm2")

    def dmm2_cover_wd(g):
        return 16 * (g // 8) + 2 * (g % 8) + 2
    s_dev = C.sem(pref + "dev")

    wd3 = wdT.rearrange("(k p) o -> p k o", p=128)

    @b.sync
    def _(sync):
        for gd in range(32):
            dg, kb = divmod(gd, 8)
            if gd >= 1:
                sync.wait_ge(s_dcast.h, gd)
            sync.dma_start(out=wd32[:].rearrange("p (k o) -> p k o", k=8),
                           in_=wd3[:, kb * 8:(kb + 1) * 8, dg * 512:(dg + 1) * 512]
                           ).then_inc(s_ddma.h, 16)
    s_ddma.inc(16 * 32)

    @b.scalar
    def _(scalar):
        for gd in range(32):
            bb = gd % 2
            scalar.wait_ge(s_ddma.h, 16 * (gd + 1))
            if gd >= 2:
                scalar.wait_ge(s_dmm2.h, dmm2_cover_wd(gd - 2))
            scalar.activation(wdbf[:, bb * 4096:(bb + 1) * 4096], wd32[:],
                              AF.Identity).then_inc(s_dcast.h, 1)
    s_dcast.inc(32)

    @b.gpsimd
    def _(gpsimd):
        for hk in range(64):
            dg, hb16 = divmod(hk, 16)
            bb = hk % 2
            if hk >= 2:
                gpsimd.wait_ge(s_dmm2.h, hk - 1)
            gpsimd.dma_start(out=hbuf[:, bb * 4096:(bb + 1) * 4096].rearrange("p (k t) -> p k t", k=4),
                             in_=h3[:, hb16 * 4:hb16 * 4 + 4, :]).then_inc(s_hdma[hk % 2].h, 16)
    s_hdma[0].inc(16 * 32)
    s_hdma[1].inc(16 * 32)

    @b.tensor
    def _(tensor):
        for dg in range(4):
            if dg >= 1:
                tensor.wait_ge(s_dev.h, 8 * dg)
            for hb16 in range(16):
                hk = dg * 16 + hb16
                bb = hk % 2
                tensor.wait_ge(s_hdma[hk % 2].h, 16 * (hk // 2 + 1))
                mm = None
                for kk in range(4):
                    k = hb16 * 4 + kk
                    gd = dg * 8 + k // 8
                    if (k % 8) == 0:
                        tensor.wait_ge(s_dcast.h, gd + 1)
                    wb = gd % 2
                    for d4 in range(4):
                        for tb in range(2):
                            mm = tensor.matmul(
                                ps[d4 * 2 + tb][:],
                                wdbf[:, wb * 4096 + (k % 8) * 512 + d4 * 128:
                                     wb * 4096 + (k % 8) * 512 + d4 * 128 + 128],
                                hbuf[:, bb * 4096 + kk * 1024 + tb * 512:
                                     bb * 4096 + kk * 1024 + tb * 512 + 512],
                                start=(k == 0), stop=(k == KH - 1))
                mm.then_inc(s_dmm2.h, 1)
                s_dmm2.inc()

    @b.vector
    def _(vector):
        for dg in range(4):
            vector.wait_ge(s_dmm2.h, 16 * (dg + 1))
            for d4 in range(4):
                for tb in range(2):
                    dt_i = dg * 4 + d4
                    sl = slice(dt_i * T + tb * 512, dt_i * T + tb * 512 + 512)
                    vector.tensor_tensor(R[:, sl], R[:, sl], ps[d4 * 2 + tb][:],
                                         ALU.add).then_inc(s_dev.h, 1)
    s_dev.inc(32)

    for m in (s_dmm2, s_dev, s_dcast, s_ddma, s_hdma[0], s_hdma[1]):
        C.mark(m)
    C.barrier()


def emit_stream_load(C, R, x_in):
    b = C.block
    s = C.sem("xload")

    @b.sync
    def _(sync):
        sync.dma_start(out=R[:].rearrange("p (k t) -> p k t", k=KD),
                       in_=x_in.rearrange("(k p) t -> p k t", p=128)).then_inc(s.h, 16)
    s.inc(16)
    C.mark(s)


def emit_stream_store(C, R, x_out):
    b = C.block
    s = C.sem("xstore")

    @b.sync
    def _(sync):
        sync.dma_start(out=x_out.rearrange("(k p) t -> p k t", p=128),
                       in_=R[:].rearrange("p (k t) -> p k t", k=KD)).then_inc(s.h, 16)
        sync.wait_ge(s.h, 16)
    s.inc(16)


def build_ffn_launch():
    C = Ctx()
    nc = C.nc
    x_in = nc.dram_tensor("x_in", [DM, T], F32, kind="ExternalInput").ap()
    x_out = nc.dram_tensor("x_out", [DM, T], F32, kind="ExternalOutput").ap()
    wgT = nc.dram_tensor("wgT", [DM, 8192], F32, kind="ExternalInput").ap()
    wuT = nc.dram_tensor("wuT", [DM, 8192], F32, kind="ExternalInput").ap()
    wdT = nc.dram_tensor("wdT", [8192, DM], F32, kind="ExternalInput").ap()
    hscr = nc.dram_tensor("hscr", [8192, T], BF16)

    R = C.sbuf("R", [128, KD * T], F32)
    xn = C.sbuf("xn", [128, KD * T], BF16)
    C.start_block()
    sc = make_ln_scratch(C)
    C.emit_const_loads()
    emit_stream_load(C, R, x_in)
    C.barrier()
    emit_ln(C, sc, R, xn, mode="plain")
    C.barrier()
    emit_ffn(C, R, xn, wgT, wuT, wdT, hscr.ap())
    emit_stream_store(C, R, x_out)
    return C


def ffn_in_maps(inp, x_stream_percore):
    wg = f32c(inp["L0_ffn_w_gate"].T)
    wu = f32c(inp["L0_ffn_w_up"].T)
    wd = f32c(inp["L0_ffn_w_down"].T)
    maps = []
    for c in range(8):
        m = {"x_in": x_stream_percore[c], "wgT": wg, "wuT": wu, "wdT": wd}
        maps.append(m)
    return maps


def shard_stream(x):
    out = []
    for c in range(8):
        b, p = c // 4, c % 4
        out.append(np.ascontiguousarray(x[b].T[:, 1024 * p:1024 * (p + 1)], dtype=np.float32))
    return out


def unshard_stream(slices):
    x = np.zeros((2, 4096, 2048), np.float32)
    for c in range(8):
        b, p = c // 4, c % 4
        x[b, 1024 * p:1024 * (p + 1), :] = slices[c].T
    return x



NCH = 8


def emit_rms_inplace(C, sc, xn, tag="r"):
    b = C.block
    normW = C.const_tiles["normW"]
    onesb = C.const_tiles["ones_col_bf16"]
    s_sq = C.sem("rmssq" + tag)
    s_mm = C.sem("rmsmm" + tag)
    s_ev = C.sem("rmsev" + tag)
    s_r1 = C.sem("rmsr1" + tag)
    s_ap = C.sem("rmsap" + tag)
    sq = sc["sq"]
    psS1, psBm = sc["psS1"], sc["psBm"]
    m_sb, rs_sb = sc["m"], sc["rs"]
    Brs = sc["Bm"]
    sqb = [sq[i][:].bitcast(BF16) for i in range(2)]

    for tb in range(2):
        sq_base = s_sq.v
        mm_base = s_mm.v

        @b.scalar
        def _(scalar, tb=tb, sq_base=sq_base, mm_base=mm_base):
            for k in range(KD):
                g = tb * KD + k
                if g >= 2:
                    scalar.wait_ge(s_mm.h, mm_base + (k - 1) if tb == 0 else
                                   (mm_base + (k - 1) if k >= 2 else mm_base))
                scalar.activation(sqb[g % 2][:, 0:512],
                                  xn[:, k * T + tb * 512: k * T + tb * 512 + 512],
                                  AF.Square).then_inc(s_sq.h, 1)
        s_sq.inc(KD)

        @b.tensor
        def _(tensor, tb=tb, sq_base=sq_base):
            if tb == 1:
                tensor.wait_ge(s_ev.h, s_ev.v)
            for k in range(KD):
                g = tb * KD + k
                tensor.wait_ge(s_sq.h, sq_base + k + 1)
                tensor.matmul(psS1, onesb[:], sqb[g % 2][:, 0:512],
                              start=(k == 0), stop=(k == KD - 1)).then_inc(s_mm.h, 1)
        s_mm.inc(KD)

        sl = slice(tb * 512, (tb + 1) * 512)

        @b.vector
        def _(vector, sl=sl):
            vector.wait_ge(s_mm.h, s_mm.v)
            vector.tensor_scalar(m_sb[:, sl], psS1, 1.0 / 2048.0, None,
                                 ALU.mult).then_inc(s_ev.h, 1)
        s_ev.inc()

        @b.scalar
        def _(scalar, sl=sl):
            scalar.wait_ge(s_ev.h, s_ev.v)
            scalar.activation(rs_sb[:, sl], m_sb[:, sl], AF.Ln,
                              bias=sc["eps"][0:1, :]).then_inc(s_sq.h, 1)
            scalar.wait_ge(s_sq.h, s_sq.v + 1)
            scalar.activation(rs_sb[:, sl], rs_sb[:, sl], AF.Exp,
                              scale=-0.5).then_inc(s_sq.h, 1)
        s_sq.inc(2)

        @b.tensor
        def _(tensor, sl=sl, tb=tb):
            tensor.wait_ge(s_sq.h, s_sq.v)
            if tb == 1:
                tensor.wait_ge(s_ev.h, s_ev.v)
            tensor.matmul(psBm[:], sc["ones_row"][:], rs_sb[:, sl],
                          start=True, stop=True).then_inc(s_r1.h, 1)
        s_r1.inc()

        @b.vector
        def _(vector, sl=sl):
            vector.wait_ge(s_r1.h, s_r1.v)
            vector.tensor_copy(Brs[:, sl], psBm[:]).then_inc(s_ev.h, 1)
        s_ev.inc()

    @b.vector
    def _(vector):
        vector.wait_ge(s_ev.h, s_ev.v)
        for k in range(KD):
            for tb in range(2):
                sl = slice(k * T + tb * 512, k * T + tb * 512 + 512)
                bsl = slice(tb * 512, (tb + 1) * 512)
                last = vector.scalar_tensor_tensor(xn[:, sl], xn[:, sl],
                                                   normW[:, k:k + 1], Brs[:, bsl],
                                                   ALU.mult, ALU.mult)
        last.then_inc(s_ap.h, 1)
    s_ap.inc(1)
    C.mark(s_ap)


def emit_mamba(C, sc, R, xn, fscr, A, inWT, outWT, dram, pref="m"):
    b = C.block
    gateT, xs_f, Ypart = dram["gateT"], dram["xs_f"], dram["Ypart"]
    ynscr = dram["ynscr"]
    cc_s_in, cc_s_out = dram["cc_s_in"], dram["cc_s_out"]
    cc_o_in, cc_o_out = dram["cc_o_in"], dram["cc_o_out"]
    h_in, h_out = dram["h_in"], dram["h_out"]
    Lf = dram["Lf"]
    Ddiag_d = dram["Ddiag"]
    ps = C.ps8
    triu = C.const_tiles["triu_f32"]
    tril01 = C.const_tiles["tril01_bf16"]
    M04 = C.const_tiles["M04_bf16"]
    ones3 = C.const_tiles["ones3_bf16"]
    negones3 = C.const_tiles["negones3_bf16"]
    identb = C.const_tiles["ident_bf16"]
    convw = C.const_tiles["convw"]
    convb = C.const_tiles["convb"]
    dtb = C.const_tiles["dtb"]
    alog = C.const_tiles["alog"]
    ohprev = C.const_tiles["ohprev"]
    coefT = C.const_tiles["coefT"]
    masklt = C.const_tiles["masklt"]
    gnw = C.const_tiles["gnw"]
    onef = C.const_tiles["ones_col_f32x"]
    ones_row = sc["ones_row"]
    ones_col = sc["ones_col"]

    B_T = A[:, 0:1024]
    C_T = A[:, 1024:2048]
    Ddb = A[:, 2048:6144]
    pb0 = 6144
    wpbf = [A[:, pb0 + i * 2048: pb0 + (i + 1) * 2048] for i in range(2)]
    hwb = wpbf
    xcbuf = [A[:, pb0 + 4096 + i * 1056: pb0 + 4096 + (i + 1) * 1056] for i in range(2)]
    sgb = [A[:, pb0 + 6208 + i * 512: pb0 + 6208 + (i + 1) * 512] for i in range(2)]
    halo = A[:, pb0 + 7232: pb0 + 7334]
    halo_gs = [A[:, pb0 + 7360 + i * 102: pb0 + 7360 + (i + 1) * 102] for i in range(4)]
    cvo = [A[:, pb0 + 7808 + i * 1024: pb0 + 7808 + (i + 1) * 1024] for i in range(2)]
    cva = [A[:, pb0 + 9856 + i * 2048: pb0 + 9856 + (i + 1) * 2048].bitcast(F32)
           for i in range(2)]
    xsT = [A[:, pb0 + i * 4096: pb0 + (i + 1) * 4096] for i in range(2)]
    xsf2 = [A[:, pb0 + 8192 + i * 4096: pb0 + 8192 + (i + 1) * 4096] for i in range(2)]
    U = A[:, pb0 + 16384: pb0 + 20480]
    rUg = [A[:, pb0 + 20480 + i * 512: pb0 + 20480 + (i + 1) * 512] for i in range(2)]
    Ysbg = [A[:, pb0 + 21504 + i * 512: pb0 + 21504 + (i + 1) * 512] for i in range(2)]
    Slbg = [A[:, pb0 + 22528 + i * 512: pb0 + 22528 + (i + 1) * 512] for i in range(2)]
    Ebuf = [A[:, pb0 + 23552 + i * 512: pb0 + 23552 + (i + 1) * 512] for i in range(2)]
    Mbuf = [A[:, pb0 + 24576 + i * 512: pb0 + 24576 + (i + 1) * 512] for i in range(2)]
    Gmb = A[:, pb0 + 25600: pb0 + 25728]
    Grep = A[:, pb0 + 25728: pb0 + 26240]
    Btok = A[:, pb0 + 26240: pb0 + 26368]
    Ybb = A[:, pb0 + 26368: pb0 + 26880]
    Lflat3 = A[0:3, pb0 + 26880: pb0 + 30976]
    yb = [A[:, pb0 + 0: pb0 + 4096]] * 2
    gateb = [A[:, pb0 + 4096: pb0 + 8192]] * 2
    Sinbf = A[:, pb0 + 8192: pb0 + 12288]
    ynfq = A[:, pb0 + 12288: pb0 + 16384]
    Ycb = A[:, pb0 + 16384: pb0 + 16896]
    ycur = A[:, pb0 + 16896: pb0 + 20992]
    ynb = [A[:, pb0 + 20992 + i * 2048: pb0 + 20992 + (i + 1) * 2048] for i in range(2)]
    wobk = [A[:, pb0 + 25088 + i * 1024: pb0 + 25088 + (i + 1) * 1024] for i in range(2)]
    ynsrc = A[:, pb0 + 26880: pb0 + 30976]
    S_g_ap = A[:, pb0 + 12288: pb0 + 20480].bitcast(F32)

    w32 = C.sbuf(pref + "w32", [128, 2048], F32)
    S_loc = C.sbuf(pref + "S_loc", [128, 4096], F32)
    dt_hm = C.sbuf(pref + "dt_hm", [64, 1024], F32)
    LTs = C.sbuf(pref + "LTs", [128, 512], F32)
    offlat = C.sbuf(pref + "offlat", [1, 512], F32)
    off = C.sbuf(pref + "off", [1, 64], F32)
    Ltr = C.sbuf(pref + "Ltr", [1, 64], F32)
    Ltr2 = C.sbuf(pref + "Ltr2", [1, 64], F32)
    packb = C.sbuf(pref + "packb", [128, 128], F32)
    packT = C.sbuf(pref + "packT", [128, 128], F32)
    Lhm = C.sbuf(pref + "Lhm", [64, 128], F32)
    Ld1 = C.sbuf(pref + "Ld1", [64, 128], F32)
    Ls = [C.sbuf(pref + f"Ls{i}", [64, 128], BF16) for i in range(3)]
    cT = C.sbuf(pref + "cT", [128, 64], F32)
    rT = C.sbuf(pref + "rT", [128, 64], F32)
    dec = C.sbuf(pref + "dec", [128, 64], F32)
    fT = cT
    t64a = C.sbuf(pref + "t64a", [64, 128], F32)
    t64b = C.sbuf(pref + "t64b", [64, 128], F32)
    t64c = C.sbuf(pref + "t64c", [64, 128], F32)
    t64d = C.sbuf(pref + "t64d", [64, 128], F32)
    negA = C.sbuf(pref + "negA", [64, 1], F32)
    Wt = C.sbuf(pref + "Wt", [4, 64], F32)
    Wflat = C.sbuf(pref + "Wflat", [1, 256], F32)
    ofg = C.sbuf(pref + "ofg", [4, 64], F32)
    ssq = C.sbuf(pref + "ssq", [128, 8], F32)
    rsc = C.sbuf(pref + "rsc", [128, 1], F32)
    S_g = S_g_ap

    s_dd = C.sem(pref + "dd")
    s_ddc = C.sem(pref + "ddc")
    s_na = C.sem(pref + "na")

    @b.sync
    def _(sync):
        sync.dma_start(out=w32[:].rearrange("p (a o) -> p a o", a=16),
                       in_=Ddiag_d.rearrange("(a p) o -> p a o", p=128)[:, 0:16, :]).then_inc(s_dd.h, 16)
        sync.wait_ge(s_ddc.h, 1)
        sync.dma_start(out=w32[:].rearrange("p (a o) -> p a o", a=16),
                       in_=Ddiag_d.rearrange("(a p) o -> p a o", p=128)[:, 16:32, :]).then_inc(s_dd.h, 16)
    s_dd.inc(32)

    @b.vector
    def _(vector):
        vector.wait_ge(s_dd.h, 16)
        vector.tensor_copy(Ddb[:, 0:2048], w32[:]).then_inc(s_ddc.h, 1)
        vector.wait_ge(s_dd.h, 32)
        vector.tensor_copy(Ddb[:, 2048:4096], w32[:]).then_inc(s_ddc.h, 1)
        vector.wait_ge(s_na.h, 1)
        vector.tensor_scalar(negA[:], negA[:], -1.0, None, ALU.mult).then_inc(s_na.h, 1)
        vector.memset(off[:], 0.0)
        vector.memset(S_loc[:], 0.0).then_inc(s_na.h, 1)
    s_ddc.inc(2)

    @b.scalar
    def _(scalar):
        scalar.activation(negA[:], alog[:], AF.Exp).then_inc(s_na.h, 1)
    s_na.inc(3)

    s_hw = C.sem(pref + "hw")
    s_hwc = C.sem(pref + "hwc")
    s_hmm = C.sem(pref + "hmm")
    s_hev = C.sem(pref + "hev")
    s_hcc = C.sem(pref + "hcc")
    s_hcd = C.sem(pref + "hcd")
    chV = Chain(C, pref + "chV")

    @b.sync
    def _(sync):
        for oi in range(34):
            if oi == 0:
                sync.wait_ge(s_ddc.h, 2)
            if oi >= 1:
                sync.wait_ge(s_hwc.h, oi)
            sync.dma_start(out=w32[:].rearrange("p (k o) -> p k o", k=KD)[:, :, 0:128],
                           in_=inWT.rearrange("(k p) o -> p k o", p=128)[:, :, 4096 + oi * 128:4096 + oi * 128 + 128]
                           ).then_inc(s_hw.h, 16)
    s_hw.inc(16 * 34)

    @b.vector
    def _(vector):
        for oi in range(34):
            par = oi % 2
            vector.wait_ge(s_hw.h, 16 * (oi + 1))
            if oi >= 2:
                vector.wait_ge(s_hmm.h, oi - 1)
            vector.tensor_copy(hwb[par], w32[:, 0:2048]).then_inc(s_hwc.h, 1)
    s_hwc.inc(34)

    @b.tensor
    def _(tensor):
        for oi in range(34):
            par = oi % 2
            tensor.wait_ge(s_hwc.h, oi + 1)
            mm = None
            for k in range(KD):
                mm = tensor.matmul(ps[1][:, 3 * oi:3 * oi + 3],
                                   hwb[par][:, k * 128:(k + 1) * 128],
                                   xn[:, k * T + 1021: k * T + 1024],
                                   start=(k == 0), stop=(k == KD - 1))
            mm.then_inc(s_hmm.h, 1)
    s_hmm.inc(34)

    @b.vector
    def _(vector):
        vector.wait_ge(s_hmm.h, 34)
        vector.tensor_copy(halo, ps[1][:, 0:102]).then_inc(s_hev.h, 1)
    s_hev.inc(1)

    @b.gpsimd
    def _(gpsimd):
        gpsimd.wait_ge(s_hev.h, 1)
        gpsimd.dma_start(out=h_in, in_=halo).then_inc(s_hcd.h, 16)
        gpsimd.wait_ge(s_hcd.h, 16)
        gpsimd.collective_compute("AllGather", ALU.bypass,
                                  replica_groups=[[0, 1, 2, 3], [4, 5, 6, 7]],
                                  ins=[h_in], outs=[h_out]).then_inc(s_hcc.h, 1)
        gpsimd.wait_ge(s_hcc.h, 1)
        for j in range(4):
            gpsimd.dma_start(out=halo_gs[j], in_=h_out[j * 128:(j + 1) * 128, :]).then_inc(s_hcd.h, 16)
    s_hcc.inc(1)
    s_hcd.inc(16 * 5)

    @b.vector
    def _(vector):
        vector.wait_ge(s_hcd.h, 16 * 5)
        chV.dep(vector.tensor_scalar(halo, halo_gs[0], ohprev[:, 0:1], None, ALU.mult))
        for j in range(1, 4):
            chV.wait(vector)
            chV.dep(vector.scalar_tensor_tensor(halo, halo_gs[j], ohprev[:, j:j + 1],
                                                halo, ALU.mult, ALU.add))
        chV.wait(vector)
        vector.tensor_copy(halo_gs[0], halo).then_inc(s_hev.h, 1)
    s_hev.inc(1)
    haloC = halo_gs[0]

    for m in (s_hev, s_hcc, s_hcd, s_hmm, s_hwc, s_ddc, s_na, s_hw, s_dd):
        C.mark(m)
    C.barrier()

    ot_seq = list(range(32)) + [66] + list(range(32, 66))
    uorder = [(ot, tb) for ot in ot_seq for tb in range(2)]
    wchunks = [(o,) for o in ot_seq]
    ot_chunk = {o: (ci, 0) for ci, (o,) in enumerate(wchunks)}
    cat = {}
    gidx = xidx = 0
    for ui, (ot, tb) in enumerate(uorder):
        if ot < 32:
            cat[ui] = ("g", gidx)
            gidx += 1
        else:
            cat[ui] = ("x", xidx)
            xidx += 1

    s_pw = C.sem(pref + "pw")
    s_pwc = C.sem(pref + "pwc")
    s_pmm = C.sem(pref + "pmm")
    s_pevG = C.sem(pref + "pevG")
    s_pevX = C.sem(pref + "pevX")
    s_psp = [C.sem(pref + "psp0"), C.sem(pref + "psp1")]
    s_cv = C.sem(pref + "cv")
    s_cvsl = C.sem(pref + "cvsl")
    s_cvs = [C.sem(pref + "cvs0"), C.sem(pref + "cvs1")]
    chC = Chain(C, pref + "chC")

    @b.sync
    def _(sync):
        for ci, (ot,) in enumerate(wchunks):
            if ci >= 1:
                sync.wait_ge(s_pwc.h, ci)
            sync.dma_start(
                out=w32[:].rearrange("p (k o) -> p k o", k=KD),
                in_=inWT.rearrange("(k p) o -> p k o", p=128)[:, :, ot * 128:(ot + 1) * 128]
                ).then_inc(s_pw.h, 16)
    s_pw.inc(16 * len(wchunks))

    s_pevX_count = [0]

    def _emit_cast(vector, ci):
        par = ci % 2
        vector.wait_ge(s_pw.h, 16 * (ci + 1))
        if ci >= 2:
            vector.wait_ge(s_pmm.h, 2 * (ci - 1))
        vector.tensor_copy(wpbf[par][:, 0:2048],
                           w32[:, 0:2048]).then_inc(s_pwc.h, 1)

    def _emit_evac(vector, ui):
        ot, tb = uorder[ui]
        if ot < 32:
            return
        c, xi = cat[ui]
        pbk = 4 + ui % 2
        vector.wait_ge(s_pmm.h, ui + 1)
        if ot == 66:
            vector.tensor_copy(dt_hm[:, tb * 512:(tb + 1) * 512],
                               ps[pbk][0:64, :]).then_inc(s_pevX.h, 1)
            s_pevX_count[0] += 1
        else:
            oi = ot - 32
            xpar = oi % 2
            if oi >= 2 and tb == 0:
                vector.wait_ge(s_cvsl.h, oi - 1)
            vector.tensor_copy(xcbuf[xpar][:, 3 + tb * 512: 3 + tb * 512 + 512],
                               ps[pbk][:]).then_inc(s_pevX.h, 1)
            s_pevX_count[0] += 1
            if tb == 1:
                vector.wait_ge(s_pevX.h, s_pevX_count[0])
                chC.dep(vector.tensor_copy(xcbuf[xpar][:, 0:3],
                                           haloC[:, 3 * oi:3 * oi + 3]))
                chC.wait(vector)
                chC.dep(vector.tensor_scalar(cva[xpar][:], xcbuf[xpar][:, 0:1024],
                                             convw[:, 4 * oi:4 * oi + 1], None, ALU.mult))
                for kk in range(1, 4):
                    chC.wait(vector)
                    chC.dep(vector.scalar_tensor_tensor(
                        cva[xpar][:], xcbuf[xpar][:, kk:kk + 1024],
                        convw[:, 4 * oi + kk:4 * oi + kk + 1], cva[xpar][:],
                        ALU.mult, ALU.add))
                chC.wait(vector)
                vector.tensor_scalar(cva[xpar][:, 0:1], cva[xpar][:, 0:1], 1.0, None,
                                     ALU.mult).then_inc(s_cv.h, 1)

    @b.vector
    def _(vector):
        nch = len(wchunks)
        for ci in range(nch):
            _emit_cast(vector, ci)
            if ci >= 1:
                _emit_evac(vector, 2 * (ci - 1))
                _emit_evac(vector, 2 * (ci - 1) + 1)
        _emit_evac(vector, 2 * (nch - 1))
        _emit_evac(vector, 2 * (nch - 1) + 1)
    s_pwc.inc(len(wchunks))
    s_pevX.inc(70)
    s_cv.inc(34)

    @b.tensor
    def _(tensor):
        for ui, (ot, tb) in enumerate(uorder):
            ci, sub = ot_chunk[ot]
            par = ci % 2
            pbk = 4 + ui % 2
            tensor.wait_ge(s_pwc.h, ci + 1)
            if ui >= 2:
                c2, i2 = cat[ui - 2]
                if c2 == "g":
                    tensor.wait_ge(s_pevG.h, i2 + 1)
                else:
                    tensor.wait_ge(s_pevX.h, i2 + 1)
            mm = None
            for k in range(KD):
                mm = tensor.matmul(ps[pbk][:],
                                   wpbf[par][:, k * 128: k * 128 + 128],
                                   xn[:, k * T + tb * 512: k * T + tb * 512 + 512],
                                   start=(k == 0), stop=(k == KD - 1))
            mm.then_inc(s_pmm.h, 1)
    s_pmm.inc(len(uorder))

    @b.scalar
    def _(scalar):
        for ui, (ot, tb) in enumerate(uorder):
            if ot >= 32:
                continue
            c, gi = cat[ui]
            pbk = 4 + ui % 2
            scalar.wait_ge(s_pmm.h, ui + 1)
            if gi >= 2:
                scalar.wait_ge(s_psp[gi % 2].h, 16 * (gi // 2))
            scalar.activation(sgb[gi % 2], ps[pbk][:], AF.Silu).then_inc(s_pevG.h, 1)
    s_pevG.inc(64)

    @b.scalar
    def _(scalar):
        for oi in range(34):
            spb = oi % 2
            scalar.wait_ge(s_cv.h, oi + 1)
            if oi < 32:
                if oi >= 2:
                    scalar.wait_ge(s_cvs[spb].h, 16 * (oi // 2))
                scalar.activation(cvo[spb], cva[spb][:], AF.Silu,
                                  bias=convb[:, oi:oi + 1]).then_inc(s_cvsl.h, 1)
            elif oi == 32:
                scalar.activation(B_T, cva[spb][:], AF.Silu,
                                  bias=convb[:, oi:oi + 1]).then_inc(s_cvsl.h, 1)
            else:
                scalar.activation(C_T, cva[spb][:], AF.Silu,
                                  bias=convb[:, oi:oi + 1]).then_inc(s_cvsl.h, 1)
    s_cvsl.inc(34)

    @b.gpsimd
    def _(gpsimd):
        for ui, (ot, tb) in enumerate(uorder):
            if ot >= 32:
                continue
            c, gi = cat[ui]
            gpsimd.wait_ge(s_pevG.h, gi + 1)
            gpsimd.dma_start(out=gateT.rearrange("(o p) t -> p o t", p=128)[:, ot, tb * 512:(tb + 1) * 512],
                             in_=sgb[gi % 2]).then_inc(s_psp[gi % 2].h, 16)
        for oi in range(32):
            spb = oi % 2
            gpsimd.wait_ge(s_cvsl.h, oi + 1)
            gpsimd.dma_start(out=xs_f.rearrange("(o p) t -> p o t", p=128)[:, oi, :],
                             in_=cvo[spb]).then_inc(s_cvs[spb].h, 16)
    s_psp[0].inc(16 * 32)
    s_psp[1].inc(16 * 32)
    s_cvs[0].inc(16 * 16)
    s_cvs[1].inc(16 * 16)

    for m in (s_pmm, s_pevG, s_pevX, s_psp[0], s_psp[1], s_cv, s_cvsl,
              s_cvs[0], s_cvs[1], s_pwc, s_pw):
        C.mark(m)
    C.barrier()

    s_xt = [C.sem(pref + "xt0"), C.sem(pref + "xt1")]
    s_xf = [C.sem(pref + "xf0"), C.sem(pref + "xf1")]
    s_dc = C.sem(pref + "dc")
    s_ds = C.sem(pref + "ds")
    s_pkT = C.sem(pref + "pkT")
    s_pk = C.sem(pref + "pk")
    s_lmm = C.sem(pref + "lmm")
    s_lev = C.sem(pref + "lev")
    s_lf = C.sem(pref + "lf")
    s_lfd = C.sem(pref + "lfd")
    s_cre = C.sem(pref + "cre")
    s_urU = C.sem(pref + "urU")
    s_gmm = C.sem(pref + "gmm")
    s_gev = C.sem(pref + "gev")
    s_btT = C.sem(pref + "btT")
    s_dD = C.sem(pref + "dD")
    s_e = C.sem(pref + "e")
    s_m = C.sem(pref + "m")
    s_ia = C.sem(pref + "ia")
    s_e1 = C.sem(pref + "e1")
    s_e2 = C.sem(pref + "e2")
    s_su = C.sem(pref + "su")
    s_sbf = C.sem(pref + "sbf")
    s_ysp = [C.sem(pref + "ysp0"), C.sem(pref + "ysp1")]
    chD = Chain(C, pref + "chD")
    chL = Chain(C, pref + "chL2")
    chX = Chain(C, pref + "chX")
    chE = Chain(C, pref + "chE")

    for q in range(NCH):
        par = q % 2
        cq = slice(q * 128, (q + 1) * 128)

        @b.scalar
        def _(scalar, q=q, par=par):
            if q >= 2:
                scalar.wait_ge(s_urU.h, 2 * (q - 1))
            scalar.dma_start(out=xsT[par],
                             in_=xs_f[:, q * 128:(q + 1) * 128],
                             transpose=True).then_inc(s_xt[par].h, 16)
        s_xt[par].inc(16)

        @b.sync
        def _(sync, q=q, par=par):
            if q >= 2:
                sync.wait_ge(s_ia.h, 8 * (q - 1))
            sync.dma_start(out=xsf2[par].rearrange("p (o t) -> p o t", o=32),
                           in_=xs_f.rearrange("(o p) t -> p o t", p=128)[:, :, q * 128:(q + 1) * 128]
                           ).then_inc(s_xf[par].h, 16)
        s_xf[par].inc(16)

        @b.vector
        def _(vector, q=q):
            chD.dep(vector.tensor_scalar(t64a[:], dt_hm[:, q * 128:(q + 1) * 128],
                                         dtb[:], None, ALU.add))
            chD.wait(vector)
            chD.dep(vector.tensor_scalar(t64c[:], t64a[:], 0.0, None, ALU.max))
            chD.wait(vector)
            vector.scalar_tensor_tensor(t64b[:], t64c[:], 2.0, t64a[:],
                                        ALU.mult, ALU.subtract).then_inc(s_dc.h, 1)
        s_dc.inc(1)

        @b.scalar
        def _(scalar, q=q):
            scalar.wait_ge(s_dc.h, s_dc.v)
            scalar.activation(t64b[:], t64b[:], AF.Exp,
                              scale=-1.0).then_inc(s_ds.h, 1)
            scalar.wait_ge(s_ds.h, s_ds.v + 1)
            scalar.activation(t64b[:], t64b[:], AF.Ln,
                              bias=onef[0:64, :]).then_inc(s_ds.h, 1)
        s_ds.inc(2)

        @b.vector
        def _(vector, q=q):
            vector.wait_ge(s_ds.h, s_ds.v)
            chD.wait(vector)
            chD.dep(vector.tensor_tensor(packb[0:64, :], t64c[:], t64b[:], ALU.add))
            chD.wait(vector)
            chD.dep(vector.tensor_scalar(t64a[:], packb[0:64, :], negA[:], None, ALU.mult))
            chD.wait(vector)
            vector.tensor_copy(packb[64:128, :], t64a[:]).then_inc(s_dc.h, 1)
        s_dc.inc(1)

        @b.tensor
        def _(tensor, q=q):
            tensor.wait_ge(s_dc.h, s_dc.v)
            if q >= 1:
                tensor.wait_ge(s_pk.h, q)
            tensor.transpose(ps[0][:, 0:128], packb[:], C.const_tiles["ident_f32"][:]
                             ).then_inc(s_pkT.h, 1)
        s_pkT.inc(1)

        @b.vector
        def _(vector, q=q):
            vector.wait_ge(s_pkT.h, s_pkT.v)
            vector.tensor_copy(packT[:], ps[0][:, 0:128]).then_inc(s_pk.h, 1)
        s_pk.inc(1)

        lmm_b = s_lmm.v
        lev_b = s_lev.v

        @b.tensor
        def _(tensor, q=q, lmm_b=lmm_b, lev_b=lev_b):
            tensor.wait_ge(s_pk.h, s_pk.v)
            if q >= 1:
                tensor.wait_ge(s_lev.h, lev_b)
            tensor.matmul(ps[1][:, 0:64], triu[:], packT[:, 64:128],
                          start=True, stop=True).then_inc(s_lmm.h, 1)
        s_lmm.inc(1)

        @b.vector
        def _(vector, q=q):
            vector.wait_ge(s_lmm.h, s_lmm.v)
            vector.tensor_copy(LTs[:, q * 64:(q + 1) * 64], ps[1][:, 0:64]).then_inc(s_lev.h, 1)
        s_lev.inc(1)

        @b.tensor
        def _(tensor, q=q):
            tensor.wait_ge(s_lev.h, s_lev.v)
            tensor.matmul(ps[1][0:1, 0:64], ones_col[:], packT[:, 64:128],
                          start=True, stop=True).then_inc(s_lmm.h, 1)
        s_lmm.inc(1)

        @b.vector
        def _(vector, q=q):
            vector.wait_ge(s_lmm.h, s_lmm.v)
            vector.tensor_copy(Ltr[:], ps[1][0:1, 0:64]).then_inc(s_lev.h, 1)
        s_lev.inc(1)

        @b.tensor
        def _(tensor, q=q):
            tensor.wait_ge(s_lev.h, s_lev.v)
            tensor.matmul(ps[1][0:64, 0:128], packT[:, 64:128], triu[:],
                          start=True, stop=True).then_inc(s_lmm.h, 1)
        s_lmm.inc(1)

        @b.vector
        def _(vector, q=q):
            vector.wait_ge(s_lmm.h, s_lmm.v)
            if q >= 1:
                vector.wait_ge(s_lf.h, 96 * q)
            chL.dep(vector.tensor_copy(Lhm[:], ps[1][0:64, 0:128]))
            chL.wait(vector)
            chL.dep(vector.tensor_copy(Ls[0][:], Lhm[:]))
            chL.wait(vector)
            chL.dep(vector.tensor_tensor(Ld1[:], Lhm[:], Ls[0][:], ALU.subtract))
            chL.wait(vector)
            chL.dep(vector.tensor_copy(Ls[1][:], Ld1[:]))
            chL.wait(vector)
            chL.dep(vector.tensor_tensor(Ld1[:], Ld1[:], Ls[1][:], ALU.subtract))
            chL.wait(vector)
            vector.tensor_copy(Ls[2][:], Ld1[:]).then_inc(s_lev.h, 1)
        s_lev.inc(1)

        @b.scalar
        def _(scalar, q=q):
            scalar.wait_ge(s_lev.h, s_lev.v)
            if q >= 1:
                scalar.wait_ge(s_dD.h, 16 * q)
            for i in range(3):
                scalar.dma_start(out=Lf[i].rearrange("(h t) -> h t", h=32),
                                 in_=Ls[i][0:32, :]).then_inc(s_lfd.h, 16)
            s_lfd.inc(48)
            scalar.wait_ge(s_lfd.h, s_lfd.v)
            for i in range(3):
                scalar.dma_start(out=Lflat3[i:i + 1, :],
                                 in_=Lf[i:i + 1, :]).then_inc(s_lf.h, 16)
        s_lf.inc(48)

        @b.scalar
        def _(scalar, q=q):
            if q >= 1:
                scalar.wait_ge(s_e2.h, 8 * q)
            scalar.activation(cT[:], LTs[:, q * 64:(q + 1) * 64], AF.Exp).then_inc(s_cre.h, 1)
        s_cre.inc(1)

        @b.tensor
        def _(tensor, q=q):
            tensor.wait_ge(s_lev.h, s_lev.v)
            tensor.matmul(ps[1][:, 0:64], ones_row[:], Ltr[:],
                          start=True, stop=True).then_inc(s_lmm.h, 1)
        s_lmm.inc(1)

        @b.vector
        def _(vector, q=q):
            vector.wait_ge(s_lmm.h, s_lmm.v)
            if q >= 1:
                vector.wait_ge(s_urU.h, 2 * q)
            vector.tensor_tensor(rT[:], ps[1][:, 0:64], LTs[:, q * 64:(q + 1) * 64],
                                 ALU.subtract).then_inc(s_lev.h, 1)
        s_lev.inc(1)

        @b.scalar
        def _(scalar, q=q):
            scalar.wait_ge(s_lev.h, s_lev.v)
            scalar.activation(rT[:], rT[:], AF.Exp).then_inc(s_cre.h, 1)
            scalar.activation(Ltr2[:], Ltr[:], AF.Exp).then_inc(s_cre.h, 1)
        s_cre.inc(2)

        @b.vector
        def _(vector, q=q):
            vector.tensor_copy(offlat[0:1, q * 64:(q + 1) * 64],
                               off[:]).then_inc(s_dc.h, 1)
            vector.wait_ge(s_lev.h, s_lev.v)
            vector.wait_ge(s_dc.h, s_dc.v + 1)
            vector.tensor_tensor(off[:], off[:], Ltr[:], ALU.add).then_inc(s_dc.h, 1)
        s_dc.inc(2)

        @b.tensor
        def _(tensor, q=q):
            tensor.wait_ge(s_cre.h, s_cre.v)
            tensor.wait_ge(s_lev.h, s_lev.v)
            tensor.matmul(ps[1][:, 0:64], ones_row[:], Ltr2[:],
                          start=True, stop=True).then_inc(s_lmm.h, 1)
        s_lmm.inc(1)

        @b.vector
        def _(vector, q=q):
            vector.wait_ge(s_lmm.h, s_lmm.v)
            if q >= 1:
                vector.wait_ge(s_su.h, 8 * q)
            vector.tensor_copy(dec[:], ps[1][:, 0:64]).then_inc(s_lev.h, 1)
        s_lev.inc(1)


        @b.vector
        def _(vector, q=q, par=par):
            vector.wait_ge(s_xt[par].h, 16 * (q // 2 + 1))
            vector.wait_ge(s_pk.h, s_pk.v)
            vector.wait_ge(s_cre.h, s_cre.v)
            if q >= 2:
                vector.wait_ge(s_ia.h, 8 * (q - 1))
            for h in range(64):
                uu = vector.tensor_scalar(U[:, h * 64:(h + 1) * 64],
                                          xsT[par][:, h * 64:(h + 1) * 64],
                                          packT[:, h:h + 1], None, ALU.mult)
            uu.then_inc(s_urU.h, 2)
        s_urU.inc(2)

        @b.tensor
        def _(tensor, q=q):
            if q >= 1:
                tensor.wait_ge(s_gev.h, 2 * q)
            tensor.matmul(ps[2][:, 0:128], B_T[:, cq], C_T[:, cq],
                          start=True, stop=True).then_inc(s_gmm.h, 1)
            tensor.wait_ge(s_pk.h, s_pk.v)
            tensor.transpose(ps[0][:, 0:64].bitcast(BF16), B_T[:, cq],
                             identb[:]).then_inc(s_btT.h, 1)
        s_gmm.inc(1)
        s_btT.inc(1)

        @b.vector
        def _(vector, q=q):
            vector.wait_ge(s_gmm.h, s_gmm.v)
            chX.dep(vector.tensor_tensor(Gmb, ps[2][:, 0:128], tril01[:], ALU.mult))
            chX.wait(vector)
            vector.tensor_copy(Grep[:, 0:128], Gmb)
            vector.tensor_copy(Grep[:, 128:256], Gmb)
            vector.tensor_copy(Grep[:, 256:384], Gmb)
            cp = vector.tensor_copy(Grep[:, 384:512], Gmb)
            cp.then_inc(s_gev.h, 1)
            vector.wait_ge(s_btT.h, s_btT.v)
            vector.tensor_copy(Btok, ps[0][:, 0:64].bitcast(BF16)).then_inc(s_gev.h, 1)
        s_gev.inc(2)

        for g2 in range(8):
            if g2 == 4:
                @b.scalar
                def _(scalar, q=q):
                    scalar.wait_ge(s_dD.h, 16 * q + 8)
                    for i in range(3):
                        scalar.dma_start(out=Lf[i].rearrange("(h t) -> h t", h=32),
                                         in_=Ls[i][32:64, :]).then_inc(s_lfd.h, 16)
                    s_lfd.inc(48)
                    scalar.wait_ge(s_lfd.h, s_lfd.v)
                    for i in range(3):
                        scalar.dma_start(out=Lflat3[i:i + 1, :],
                                         in_=Lf[i:i + 1, :]).then_inc(s_lf.h, 16)
                s_lf.inc(48)
            for half in range(2):
                bb = 2 * g2 + half
                dpar = bb % 2

                @b.tensor
                def _(tensor, q=q, bb=bb, dpar=dpar, g2=g2):
                    tensor.wait_ge(s_lf.h, 96 * q + (48 if bb < 8 else 96))
                    if 16 * q + bb >= 2:
                        tensor.wait_ge(s_e.h, 16 * q + bb - 1)
                    if bb == 0:
                        tensor.wait_ge(s_gev.h, s_gev.v)
                    if bb == 1:
                        tensor.wait_ge(s_lev.h, s_lev.v)
                    br = bb % 8
                    tensor.matmul(ps[dpar][:, 0:512], ones3[:],
                                  Lflat3[0:3, br * 512:(br + 1) * 512],
                                  start=True, stop=False)
                    for jj in range(4):
                        tensor.matmul(ps[dpar][:, jj * 128:(jj + 1) * 128],
                                      Lflat3[0:3, (4 * br + jj) * 128:(4 * br + jj) * 128 + 128],
                                      negones3[:], start=False, stop=False)
                    tensor.matmul(ps[dpar][:, 0:512], identb[:], M04[:],
                                  start=False, stop=True).then_inc(s_dD.h, 1)
                s_dD.inc(1)

                @b.scalar
                def _(scalar, q=q, bb=bb, dpar=dpar):
                    scalar.wait_ge(s_dD.h, s_dD.v)
                    if 16 * q + bb >= 2:
                        scalar.wait_ge(s_m.h, 16 * q + bb - 1)
                    scalar.activation(Ebuf[dpar], ps[dpar][:, 0:512], AF.Exp).then_inc(s_e.h, 1)
                s_e.inc(1)

                @b.vector
                def _(vector, q=q, bb=bb, dpar=dpar):
                    vector.wait_ge(s_e.h, s_e.v)
                    vector.wait_ge(s_gev.h, s_gev.v)
                    if 16 * q + bb >= 2:
                        vector.wait_ge(s_ia.h, 8 * q + (bb - 2) // 2 + (1 if bb >= 2 else 0))
                    vector.tensor_tensor(Mbuf[dpar], Ebuf[dpar], Grep, ALU.mult).then_inc(s_m.h, 1)
                s_m.inc(1)

            @b.vector
            def _(vector, q=q, g2=g2):
                gg = 8 * q + g2
                vector.wait_ge(s_urU.h, s_urU.v)
                if gg >= 2:
                    vector.wait_ge(s_ia.h, gg - 1)
                rr = None
                for h in range(8):
                    hh = 8 * g2 + h
                    rr = vector.tensor_scalar(rUg[g2 % 2][:, h * 64:(h + 1) * 64],
                                              U[:, hh * 64:(hh + 1) * 64],
                                              rT[:, hh:hh + 1], None, ALU.mult)
                if q > 0:
                    rr = vector.tensor_copy(Slbg[g2 % 2],
                                            S_loc[:, g2 * 512:(g2 + 1) * 512])
                rr.then_inc(s_sbf.h, 1)
            s_sbf.inc(1)

            @b.tensor
            def _(tensor, q=q, g2=g2, par=par):
                tensor.wait_ge(s_m.h, s_m.v)
                tensor.wait_ge(s_sbf.h, s_sbf.v)
                tensor.wait_ge(s_xf[par].h, 16 * (q // 2 + 1))
                tensor.wait_ge(s_gev.h, s_gev.v)
                pA = 2 + g2 % 2
                pB = 4 + g2 % 2
                pS = 6 + g2 % 2
                if 8 * q + g2 >= 2:
                    tensor.wait_ge(s_e2.h, 8 * q + g2 - 1)
                    tensor.wait_ge(s_su.h, 8 * (8 * q + g2 - 1))
                for h in range(8):
                    hh = 8 * g2 + h
                    bt = (2 * g2 + h // 4) % 2
                    tensor.matmul(ps[pA][:, h * 64:(h + 1) * 64],
                                  Mbuf[bt][:, (h % 4) * 128:(h % 4) * 128 + 128],
                                  U[:, hh * 64:(hh + 1) * 64],
                                  start=(h == 0), stop=False)
                for i in range(4):
                    ii = 4 * g2 + i
                    tensor.matmul(ps[pA][:, i * 128:(i + 1) * 128],
                                  xsf2[par][:, ii * 128:(ii + 1) * 128],
                                  Ddb[:, ii * 128:(ii + 1) * 128],
                                  start=False, stop=(i == 3))
                if q > 0:
                    tensor.matmul(ps[pB][:, 0:512], C_T[:, cq],
                                  Slbg[g2 % 2],
                                  start=True, stop=True)
                tensor.matmul(ps[pS][:, 0:512], Btok,
                              rUg[g2 % 2],
                              start=True, stop=True).then_inc(s_ia.h, 1)
            s_ia.inc(1)

            @b.vector
            def _(vector, q=q, g2=g2):
                vector.wait_ge(s_ia.h, s_ia.v)
                pA = 2 + g2 % 2
                pB = 4 + g2 % 2
                pS = 6 + g2 % 2
                gg = 8 * q + g2
                if gg >= 2:
                    vector.wait_ge(s_ysp[g2 % 2].h, 16 * ((gg - 2) // 2 + 1))
                if q > 0:
                    vector.wait_ge(s_cre.h, s_cre.v)
                    ee = None
                    for h in range(8):
                        hh = 8 * g2 + h
                        ee = vector.tensor_scalar(
                            _col64(Ybb, h), ps[pB][:, h * 64:(h + 1) * 64],
                            cT[:, hh:hh + 1], None, ALU.mult)
                    chE.dep(ee)
                    chE.wait(vector)
                    vector.tensor_tensor(Ysbg[g2 % 2], ps[pA][:, 0:512],
                                         Ybb, ALU.add).then_inc(s_e2.h, 1)
                else:
                    vector.tensor_copy(Ysbg[g2 % 2],
                                       ps[pA][:, 0:512]).then_inc(s_e2.h, 1)
                uu = None
                if q > 0:
                    for h in range(8):
                        hh = 8 * g2 + h
                        uu = vector.scalar_tensor_tensor(
                            S_loc[:, hh * 64:(hh + 1) * 64],
                            S_loc[:, hh * 64:(hh + 1) * 64],
                            dec[:, hh:hh + 1],
                            ps[pS][:, h * 64:(h + 1) * 64],
                            ALU.mult, ALU.add)
                    uu.then_inc(s_su.h, 8)
                else:
                    for h in range(8):
                        hh = 8 * g2 + h
                        uu = vector.tensor_copy(S_loc[:, hh * 64:(hh + 1) * 64],
                                                ps[pS][:, h * 64:(h + 1) * 64])
                    uu.then_inc(s_su.h, 8)
            s_e2.inc(1)
            s_su.inc(8)

        @b.gpsimd
        def _(gpsimd, q=q, par=par):
            for g2 in range(8):
                gg = 8 * q + g2
                gpsimd.wait_ge(s_e2.h, gg + 1)
                gpsimd.dma_start(out=Ypart[q * 128:(q + 1) * 128, g2 * 512:(g2 + 1) * 512],
                                 in_=Ysbg[g2 % 2]).then_inc(s_ysp[g2 % 2].h, 16)
        s_ysp[0].inc(16 * 4)
        s_ysp[1].inc(16 * 4)

    s_pub = C.sem(pref + "pub")
    s_cc2 = C.sem(pref + "cc2")

    @b.gpsimd
    def _(gpsimd):
        gpsimd.wait_ge(s_su.h, s_su.v)
        gpsimd.wait_ge(s_dc.h, s_dc.v)
        gpsimd.dma_start(out=cc_s_in, in_=S_loc[:]).then_inc(s_pub.h, 16)
        gpsimd.dma_start(out=cc_o_in, in_=off[:]).then_inc(s_pub.h, 16)
        gpsimd.wait_ge(s_pub.h, 32)
        gpsimd.collective_compute("AllGather", ALU.bypass,
                                  replica_groups=[[0, 1, 2, 3], [4, 5, 6, 7]],
                                  ins=[cc_s_in], outs=[cc_s_out]).then_inc(s_cc2.h, 1)
        gpsimd.collective_compute("AllGather", ALU.bypass,
                                  replica_groups=[[0, 1, 2, 3], [4, 5, 6, 7]],
                                  ins=[cc_o_in], outs=[cc_o_out]).then_inc(s_cc2.h, 1)
    s_pub.inc(32)
    s_cc2.inc(2)
    for m in (s_cc2, s_pub, s_ysp[0], s_ysp[1], s_e2, s_su, s_ia, s_lf, s_lfd,
              s_dD, s_e, s_m, s_urU, s_sbf, s_lev, s_lmm, s_cre, s_dc, s_ds,
              s_xt[0], s_xt[1], s_xf[0], s_xf[1], s_gev, s_gmm, s_pk, s_pkT, s_btT):
        C.mark(m)
    C.barrier()

    s_cbo = C.sem(pref + "cbo")
    s_cbw = C.sem(pref + "cbw")
    s_cbg = C.sem(pref + "cbg")
    s_cbm = C.sem(pref + "cbm")
    s_cbe = C.sem(pref + "cbe")
    chB = Chain(C, pref + "chB")

    @b.sync
    def _(sync):
        sync.dma_start(out=ofg[:], in_=cc_o_out).then_inc(s_cbo.h, 16)
    s_cbo.inc(16)

    @b.tensor
    def _(tensor):
        tensor.wait_ge(s_cbo.h, 16)
        tensor.matmul(ps[0][0:4, 0:64], coefT[:], ofg[:], start=True, stop=True).then_inc(s_cbm.h, 1)
    s_cbm.inc(1)

    @b.scalar
    def _(scalar):
        scalar.wait_ge(s_cbm.h, 1)
        scalar.activation(Wt[:], ps[0][0:4, 0:64], AF.Exp).then_inc(s_cbe.h, 1)
    s_cbe.inc(1)

    @b.vector
    def _(vector):
        vector.wait_ge(s_cbe.h, 1)
        vector.tensor_scalar(Wt[:], Wt[:], masklt[:], None, ALU.mult).then_inc(s_cbe.h, 1)
    s_cbe.inc(1)

    @b.scalar
    def _(scalar):
        scalar.wait_ge(s_cbe.h, 2)
        for j in range(4):
            scalar.dma_start(out=Wflat[0:1, j * 64:(j + 1) * 64],
                             in_=Wt[j:j + 1, :]).then_inc(s_cbw.h, 16)
    s_cbw.inc(64)

    @b.gpsimd
    def _(gpsimd):
        for j in range(4):
            if j >= 1:
                gpsimd.wait_ge(s_cbe.h, 2 + j)
            gpsimd.dma_start(out=S_g[:], in_=cc_s_out[j * 128:(j + 1) * 128, :]).then_inc(s_cbg.h, 16)
    s_cbg.inc(64)

    @b.tensor
    def _(tensor):
        tensor.wait_ge(s_cbw.h, 64)
        for j in range(4):
            if j >= 1:
                tensor.wait_ge(chB.s.h, j)
            tensor.matmul(ps[1][:, 0:64], ones_row[:], Wflat[0:1, j * 64:(j + 1) * 64],
                          start=True, stop=True).then_inc(s_cbm.h, 1)
    s_cbm.inc(4)

    @b.vector
    def _(vector):
        for j in range(4):
            vector.wait_ge(s_cbm.h, 1 + j + 1)
            vector.wait_ge(s_cbg.h, 16 * (j + 1))
            chB.dep(vector.tensor_copy(dec[:], ps[1][:, 0:64]))
            chB.wait(vector)
            uu = None
            for h in range(64):
                if j == 0:
                    uu = vector.tensor_scalar(S_loc[:, h * 64:(h + 1) * 64],
                                              S_g[:, h * 64:(h + 1) * 64],
                                              dec[:, h:h + 1], None, ALU.mult)
                else:
                    uu = vector.scalar_tensor_tensor(S_loc[:, h * 64:(h + 1) * 64],
                                                     S_g[:, h * 64:(h + 1) * 64],
                                                     dec[:, h:h + 1],
                                                     S_loc[:, h * 64:(h + 1) * 64],
                                                     ALU.mult, ALU.add)
            uu.then_inc(s_cbe.h, 1)
        vector.wait_ge(s_cbe.h, s_cbe.v + 4)
        cp = None
        for i in range(8):
            cp = vector.tensor_copy(Sinbf[:, i * 512:(i + 1) * 512],
                                    S_loc[:, i * 512:(i + 1) * 512])
        cp.then_inc(s_cbe.h, 1)
    s_cbe.inc(5)
    C.mark(s_cbe)
    C.mark(s_cbm)
    C.mark(s_cbo)
    C.mark(s_cbw)
    C.mark(s_cbg)
    C.barrier()

    s_gb = [C.sem(pref + "gb0"), C.sem(pref + "gb1")]
    s_yb = [C.sem(pref + "yb0"), C.sem(pref + "yb1")]
    s_f1 = C.sem(pref + "f1")
    s_pc = C.sem(pref + "pc")
    s_yc = C.sem(pref + "yc")
    s_ya = C.sem(pref + "ya")
    s_rs2 = C.sem(pref + "rs2")
    s_sqw = C.sem(pref + "sqw")
    s_yn = C.sem(pref + "yn")
    s_yt = C.sem(pref + "yt")
    s_yte = C.sem(pref + "yte")
    s_ow = C.sem(pref + "ow")
    s_owc = C.sem(pref + "owc")
    s_oyd = [C.sem(pref + "oyd0"), C.sem(pref + "oyd1")]
    s_ysq = C.sem(pref + "ysq")
    s_omm = C.sem(pref + "omm")
    s_ore = C.sem(pref + "ore")
    chY = Chain(C, pref + "chY")
    chR = Chain(C, pref + "chR")

    for grp in range(2):
        for qi in range(4):
            q = grp * 4 + qi
            par = q % 2
            cq = slice(q * 128, (q + 1) * 128)

            @b.scalar
            def _(scalar, q=q, par=par):
                if q >= 1:
                    scalar.wait_ge(s_ya.h, 8 * q)
                scalar.dma_start(out=gateb[par],
                                 in_=gateT[:, q * 128:(q + 1) * 128],
                                 transpose=True).then_inc(s_gb[par].h, 16)
            s_gb[par].inc(16)

            @b.sync
            def _(sync, q=q, par=par):
                if q >= 1:
                    sync.wait_ge(s_ya.h, 8 * q)
                sync.dma_start(out=yb[par], in_=Ypart[q * 128:(q + 1) * 128, :]).then_inc(s_yb[par].h, 16)
            s_yb[par].inc(16)

            @b.tensor
            def _(tensor, q=q):
                if q % 4 == 0 and q >= 4:
                    tensor.wait_ge(s_ore.h, 16 * (q // 4))
                if q >= 1:
                    tensor.wait_ge(s_f1.h, 2 * q)
                tensor.matmul(ps[2][:, 0:64], ones_row[:], offlat[0:1, q * 64:(q + 1) * 64],
                              start=True, stop=True).then_inc(s_f1.h, 1)
            s_f1.inc(1)

            @b.vector
            def _(vector, q=q):
                vector.wait_ge(s_f1.h, s_f1.v)
                if q >= 1:
                    vector.wait_ge(s_yc.h, 8 * q)
                vector.tensor_tensor(fT[:], ps[2][:, 0:64], LTs[:, q * 64:(q + 1) * 64],
                                     ALU.add).then_inc(s_f1.h, 1)
            s_f1.inc(1)

            @b.scalar
            def _(scalar, q=q):
                scalar.wait_ge(s_f1.h, s_f1.v)
                scalar.activation(fT[:], fT[:], AF.Exp).then_inc(s_rs2.h, 1)
            s_rs2.inc(1)

            for g2 in range(8):
                @b.tensor
                def _(tensor, q=q, g2=g2, cq=cq):
                    pc = g2 % 2
                    if 8 * q + g2 >= 2:
                        tensor.wait_ge(s_yc.h, 8 * q + g2 - 1)
                    tensor.matmul(ps[pc][:, 0:512], C_T[:, cq],
                                  Sinbf[:, g2 * 512:(g2 + 1) * 512],
                                  start=True, stop=True).then_inc(s_pc.h, 1)
                s_pc.inc(1)

                @b.vector
                def _(vector, q=q, g2=g2, par=par):
                    pc = g2 % 2
                    vector.wait_ge(s_pc.h, s_pc.v)
                    vector.wait_ge(s_rs2.h, s_rs2.v)
                    vector.wait_ge(s_yb[par].h, 16 * (q // 2 + 1))
                    vector.wait_ge(s_gb[par].h, 16 * (q // 2 + 1))
                    sl = slice(g2 * 512, (g2 + 1) * 512)
                    ee = None
                    for h in range(8):
                        hh = 8 * g2 + h
                        c0 = g2 * 512 + h * 64
                        ee = vector.scalar_tensor_tensor(
                            ycur[:, c0:c0 + 64],
                            ps[pc][:, h * 64:(h + 1) * 64],
                            fT[:, hh:hh + 1],
                            yb[par][:, c0:c0 + 64],
                            ALU.mult, ALU.add)
                    ee.then_inc(s_yc.h, 1)
                    s_yc.inc(1)
                    vector.wait_ge(s_yc.h, s_yc.v)
                    vector.tensor_tensor(ycur[:, sl], ycur[:, sl], gateb[par][:, sl],
                                         ALU.mult).then_inc(s_ya.h, 1)
                s_ya.inc(1)

            @b.scalar
            def _(scalar, q=q):
                scalar.wait_ge(s_ya.h, 8 * (q + 1))
                if q >= 1:
                    scalar.wait_ge(s_yn.h, q)
                base = s_sqw.v
                for i in range(8):
                    if i >= 2:
                        scalar.wait_ge(s_sqw.h, base + i - 1)
                    scalar.activation(sc["sq"][i % 2][:, 0:512],
                                      ycur[:, i * 512:(i + 1) * 512], AF.Square,
                                      accum_out=ssq[:, i:i + 1]).then_inc(s_sqw.h, 1)
                    s_sqw.inc(1)
                scalar.wait_ge(s_sqw.h, s_sqw.v)
                scalar.nop().then_inc(s_rs2.h, 1)
            s_rs2.inc(1)

            @b.vector
            def _(vector, q=q):
                vector.wait_ge(s_rs2.h, s_rs2.v)
                chR.dep(vector.tensor_reduce(rsc[:], ssq[:].rearrange("p (a b) -> p a b", a=1),
                                             axis=AX.X, op=ALU.add))
                chR.wait(vector)
                chR.dep(vector.tensor_scalar(rsc[:], rsc[:], 1.0 / 4096.0, None, ALU.mult))
                chR.wait(vector)
                vector.tensor_scalar(rsc[:], rsc[:], 1e-5, None, ALU.add).then_inc(s_rs2.h, 1)
            s_rs2.inc(1)

            @b.scalar
            def _(scalar, q=q):
                scalar.wait_ge(s_rs2.h, s_rs2.v)
                scalar.activation(rsc[:], rsc[:], AF.Ln).then_inc(s_rs2.h, 1)
                scalar.wait_ge(s_rs2.h, s_rs2.v + 1)
                scalar.activation(rsc[:], rsc[:], AF.Exp, scale=-0.5).then_inc(s_rs2.h, 1)
            s_rs2.inc(2)

            @b.vector
            def _(vector, q=q, qi=qi):
                vector.wait_ge(s_rs2.h, s_rs2.v)
                if q >= 1:
                    vector.wait_ge(s_yt.h, 32 * q)
                yy = None
                for i in range(8):
                    yy = vector.tensor_scalar(ynsrc[:, i * 512:(i + 1) * 512],
                                              ycur[:, i * 512:(i + 1) * 512],
                                              rsc[:], None, ALU.mult)
                yy.then_inc(s_yn.h, 1)
            s_yn.inc(1)

            @b.tensor
            def _(tensor, q=q, qi=qi):
                tensor.wait_ge(s_yn.h, s_yn.v)
                for i in range(32):
                    g = (q * 32 + i)
                    if g >= 2:
                        tensor.wait_ge(s_yte.h, g - 1)
                    tensor.transpose(ps[3][:, 0:64].bitcast(BF16) if g % 2 == 0 else
                                     ps[6][:, 0:64].bitcast(BF16),
                                     ynsrc[:, i * 128:(i + 1) * 128],
                                     identb[:]).then_inc(s_yt.h, 1)
            s_yt.inc(32)

            @b.vector
            def _(vector, q=q, qi=qi):
                if q >= 1:
                    vector.wait_ge(s_ysq.h, 16 * q)
                for i in range(32):
                    g = (q * 32 + i)
                    vector.wait_ge(s_yt.h, g + 1)
                    src = ps[3][:, 0:64].bitcast(BF16) if g % 2 == 0 else ps[6][:, 0:64].bitcast(BF16)
                    vector.tensor_copy(ynfq[:, i * 128:(i + 1) * 128],
                                       src).then_inc(s_yte.h, 1)
            s_yte.inc(32)

            @b.gpsimd
            def _(gpsimd, q=q):
                gpsimd.wait_ge(s_yte.h, 32 * (q + 1))
                gpsimd.dma_start(out=ynscr.rearrange("(k p) t -> p k t", p=128)[:, :, q * 128:(q + 1) * 128],
                                 in_=ynfq[:].rearrange("p (k t) -> p k t", k=32)
                                 ).then_inc(s_ysq.h, 16)
            s_ysq.inc(16)

        @b.sync
        def _(sync, grp=grp):
            sync.wait_ge(s_ysq.h, 64 * (grp + 1))
            for dgrp in range(2):
                for kb in range(8):
                    gi = (grp * 2 + dgrp) * 8 + kb
                    par2 = gi % 2
                    if gi >= 2:
                        sync.wait_ge(s_omm.h, 4 * (gi - 1))
                    sync.dma_start(
                        out=ynb[par2][:].rearrange("p (k t) -> p k t", k=4),
                        in_=ynscr.rearrange("(k p) t -> p k t", p=128)[:, kb * 4:(kb + 1) * 4,
                                                                       grp * 512:(grp + 1) * 512]
                        ).then_inc(s_oyd[par2].h, 16)
        for gi in range((grp * 2) * 8, (grp * 2 + 2) * 8):
            s_oyd[gi % 2].inc(16)

        @b.scalar
        def _(scalar, grp=grp):
            pass

        @b.gpsimd
        def _(gpsimd, grp=grp):
            for dgrp in range(2):
                for k in range(32):
                    gi = (grp * 2 + dgrp) * 32 + k
                    par2 = gi % 2
                    if gi >= 1:
                        gpsimd.wait_ge(s_owc.h, gi)
                    gpsimd.dma_start(
                        out=w32[:, 0:1024],
                        in_=outWT[k * 128:(k + 1) * 128, dgrp * 1024:(dgrp + 1) * 1024]
                        ).then_inc(s_ow.h, 16)
                    gpsimd.wait_ge(s_ow.h, 16 * (gi + 1))
        for _i in range(64):
            s_ow.inc(16)

        @b.vector
        def _(vector, grp=grp):
            for dgrp in range(2):
                gd = grp * 2 + dgrp
                for k in range(32):
                    gi = gd * 32 + k
                    par2 = gi % 2
                    vector.wait_ge(s_ow.h, 16 * (gi + 1))
                    if gi >= 2:
                        vector.wait_ge(s_omm.h, gi - 1)
                    vector.tensor_scalar(wobk[par2], w32[:, 0:1024], gnw[:, k:k + 1],
                                         None, ALU.mult).then_inc(s_owc.h, 1)
                vector.wait_ge(s_omm.h, 32 * (gd + 1))
                for d8 in range(8):
                    dot = dgrp * 8 + d8
                    sl = slice(dot * T + grp * 512, dot * T + grp * 512 + 512)
                    vector.tensor_tensor(R[:, sl], R[:, sl], ps[d8][:, 0:512],
                                         ALU.add).then_inc(s_ore.h, 1)
        s_owc.inc(64)
        s_ore.inc(16)

        @b.tensor
        def _(tensor, grp=grp):
            for dgrp in range(2):
                gd = grp * 2 + dgrp
                if gd >= 1:
                    tensor.wait_ge(s_ore.h, 8 * gd)
                for k in range(32):
                    gi = gd * 32 + k
                    par2 = gi % 2
                    kpar = (gd * 8 + k // 4) % 2
                    tensor.wait_ge(s_owc.h, gi + 1)
                    tensor.wait_ge(s_oyd[kpar].h, 16 * ((gd * 8 + k // 4) // 2 + 1))
                    mm = None
                    for d8 in range(8):
                        mm = tensor.matmul(ps[d8][:, 0:512],
                                           wobk[par2][:, d8 * 128:(d8 + 1) * 128],
                                           ynb[kpar][:, (k % 4) * 512:(k % 4) * 512 + 512],
                                           start=(k == 0), stop=(k == 31))
                    mm.then_inc(s_omm.h, 4 if False else 1)
        s_omm.inc(64)

    for m in (s_ore, s_omm, s_owc, s_ow, s_yte, s_yt, s_yn, s_rs2, s_sqw, s_ya, s_yc, s_ysq,
              s_oyd[0], s_oyd[1], s_pc, s_f1, s_gb[0], s_gb[1], s_yb[0], s_yb[1]):
        C.mark(m)
    C.barrier()


def _col64(ap, h):
    return ap[:, h * 64:(h + 1) * 64]


def _w32k(ap, k):
    return ap[:, k * 128:(k + 1) * 128]


def _w32c(ap, k):
    return ap[:, k * 64:(k + 1) * 64]


def build_mamba_launch(debug=False):
    C = Ctx()
    nc = C.nc
    dbg = dict(kind="ExternalOutput") if debug else {}
    x_in = nc.dram_tensor("x_in", [DM, T], F32, kind="ExternalInput").ap()
    x_out = nc.dram_tensor("x_out", [DM, T], F32, kind="ExternalOutput").ap()
    inWT = nc.dram_tensor("inWT", [DM, 8576], F32, kind="ExternalInput").ap()
    outWT = nc.dram_tensor("outWT", [4096, DM], F32, kind="ExternalInput").ap()
    dram = {
        "Lf": nc.dram_tensor("Lf_scr", [3, 4096], BF16).ap(),
        "gateT": nc.dram_tensor("gateT", [4096, T], BF16, **dbg).ap(),
        "xs_f": nc.dram_tensor("xs_f_scr", [4096, T], BF16, **dbg).ap(),
        "Ypart": nc.dram_tensor("Ypart", [T, 4096], BF16, **dbg).ap(),
        "ynscr": nc.dram_tensor("ynscr", [4096, T], BF16, **dbg).ap(),
        "cc_s_in": nc.dram_tensor("cc_s_in", [128, 4096], F32).ap(),
        "cc_s_out": nc.dram_tensor("cc_s_out", [512, 4096], F32).ap(),
        "cc_o_in": nc.dram_tensor("cc_o_in", [1, 64], F32).ap(),
        "cc_o_out": nc.dram_tensor("cc_o_out", [4, 64], F32).ap(),
        "h_in": nc.dram_tensor("h_in", [128, 102], BF16).ap(),
        "h_out": nc.dram_tensor("h_out", [512, 102], BF16).ap(),
        "Ddiag": nc.dram_tensor("Ddiag", [4096, 128], F32, kind="ExternalInput").ap(),
    }
    R = C.sbuf("R", [128, KD * T], F32)
    C.start_block()
    sc = make_ln_scratch(C)
    C.const("normW", np.zeros((128, 16), np.float32), F32)
    C.const("gnw", np.zeros((128, 32), np.float32), F32)
    C.const("convw", np.zeros((128, 136), np.float32), F32)
    C.const("convb", np.zeros((128, 34), np.float32), F32)
    C.const("dtb", np.zeros((64, 1), np.float32), F32)
    C.const("alog", np.zeros((64, 1), np.float32), F32)
    C.const("ohprev", np.zeros((128, 4), np.float32), F32)
    C.const("coefT", np.zeros((4, 4), np.float32), F32)
    C.const("masklt", np.zeros((4, 1), np.float32), F32)
    C.const("triu_f32", np.triu(np.ones((128, 128), np.float32)), F32)
    C.const("tril01_bf16", np.triu(np.ones((128, 128), np.float32)), BF16)
    m0 = np.where(np.triu(np.ones((128, 128), np.float32)) > 0, 0.0, -1e38).astype(np.float32)
    C.const("M04_bf16", np.tile(m0, (1, 4)), BF16)
    C.const("ones3_bf16", np.ones((3, 128), np.float32), BF16)
    C.const("negones3_bf16", -np.ones((3, 128), np.float32), BF16)
    C.const("ident_f32", np.eye(128, dtype=np.float32), F32)
    C.const("ident_bf16", np.eye(128, dtype=np.float32), BF16)
    C.const("ones_col_bf16", np.ones((128, 1), np.float32), BF16)
    C.const("ones_col_f32x", np.ones((128, 1), np.float32), F32)
    A = C.sbuf("marena", [128, 6144 + 30976], BF16)
    xn = A[:, 6144 + 13952: 6144 + 13952 + KD * T]
    C.emit_const_loads()
    emit_stream_load(C, R, x_in)
    C.barrier()
    emit_ln(C, sc, R, xn, mode="mamba", h32=None)
    C.barrier()
    if debug:
        xn_d1 = nc.dram_tensor("xn_ln", [DM, T], BF16, kind="ExternalOutput").ap()
        s_dbg1 = C.sem("dbg1")

        @C.block.sync
        def _(sync):
            sync.dma_start(out=xn_d1.rearrange("(k p) t -> p k t", p=128),
                           in_=xn.rearrange("p (k t) -> p k t", k=KD)).then_inc(s_dbg1.h, 16)
        s_dbg1.inc(16)
        C.mark(s_dbg1)
        C.barrier()
    emit_rms_inplace(C, sc, xn)
    C.barrier()
    if debug:
        xn_d2 = nc.dram_tensor("xn_rms", [DM, T], BF16, kind="ExternalOutput").ap()
        m_d = nc.dram_tensor("m_dbg", [1, T], F32, kind="ExternalOutput").ap()
        rs_d = nc.dram_tensor("rs_dbg", [1, T], F32, kind="ExternalOutput").ap()
        brs_d = nc.dram_tensor("brs_dbg", [128, T], F32, kind="ExternalOutput").ap()
        s_dbg2 = C.sem("dbg2")

        @C.block.sync
        def _(sync):
            sync.dma_start(out=xn_d2.rearrange("(k p) t -> p k t", p=128),
                           in_=xn.rearrange("p (k t) -> p k t", k=KD)).then_inc(s_dbg2.h, 16)
            sync.dma_start(out=m_d, in_=sc["m"][:]).then_inc(s_dbg2.h, 16)
            sync.dma_start(out=rs_d, in_=sc["rs"][:]).then_inc(s_dbg2.h, 16)
            sync.dma_start(out=brs_d, in_=sc["Bm"][:]).then_inc(s_dbg2.h, 16)
        s_dbg2.inc(64)
        C.mark(s_dbg2)
        C.barrier()
    emit_mamba(C, sc, R, xn, None, A[:], inWT, outWT, dram)
    emit_stream_store(C, R, x_out)
    return C


def mamba_in_maps(inp, x_stream_percore):
    inW = f32c(np.asarray(inp["L0_attn_in_proj_w"]).T)
    inWT = np.concatenate([inW, np.zeros((DM, 64), np.float32)], axis=1)
    outWT = f32c(np.asarray(inp["L0_attn_out_proj_w"]).T)
    normW = f32c(np.asarray(inp["L0_attn_norm_weight"]).reshape(16, 128).T)
    gnw = f32c(np.asarray(inp["L0_attn_gate_norm_weight"]).reshape(32, 128).T)
    cw = np.asarray(inp["L0_attn_conv_w"])[:, 0, :]
    convw = f32c(cw.reshape(34, 128, 4).transpose(1, 0, 2).reshape(128, 136))
    convb = f32c(np.asarray(inp["L0_attn_conv_b"]).reshape(34, 128).T)
    dtb = f32c(np.asarray(inp["L0_attn_dt_bias"]).reshape(64, 1))
    alog = f32c(np.asarray(inp["L0_attn_A_log"]).reshape(64, 1))
    Drep = np.repeat(np.asarray(inp["L0_attn_D"]), 64)
    Dd = np.zeros((32, 128, 128), np.float32)
    for i in range(32):
        Dd[i] = np.diag(Drep[i * 128:(i + 1) * 128])
    Ddiag = f32c(Dd.reshape(4096, 128))
    maps = []
    for c in range(8):
        p = c % 4
        oh = np.zeros((128, 4), np.float32)
        if p >= 1:
            oh[:, p - 1] = 1.0
        coef = np.zeros((4, 4), np.float32)
        for j in range(4):
            for m2 in range(4):
                if j < m2 < p:
                    coef[j, m2] = 1.0
        mlt = np.zeros((4, 1), np.float32)
        mlt[:p, 0] = 1.0
        maps.append({
            "x_in": x_stream_percore[c],
            "inWT": inWT, "outWT": outWT, "Ddiag": Ddiag,
            "normW": normW, "gnw": gnw, "convw": convw, "convb": convb,
            "dtb": dtb, "alog": alog,
            "ohprev": oh, "coefT": np.ascontiguousarray(coef.T), "masklt": mlt,
        })
    return maps



SQD = 1.0 / (128.0 ** 0.5)
CSH = 20.0


def _mk_tables():
    sc = {}
    pv = {}
    dad = {}
    oev = {}
    n_sc = n_dad = n_oev = 0
    for h in range(4):
        for j in range(8):
            nkb = 4 * j + 4
            for kb in range(nkb):
                n_sc += 1
                sc[(h, j, kb)] = n_sc
                pv[(h, j, kb)] = n_sc
                if kb >= 4 * j:
                    n_dad += 1
                    dad[(h, j, kb)] = n_dad
            for qt in range(4):
                n_oev += 1
                oev[(h, j, qt)] = n_oev
    return sc, pv, dad, oev, n_sc, n_dad, n_oev


def emit_attn(C, sc_ln, R, xn, arena, wqT, wkT, wvT, woT, dram, pref="a"):
    b = C.block
    cin_x, ag_x = dram["cin_x"], dram["ag_x"]
    q_scr, k_scr, v_scr = dram["q_scr"], dram["k_scr"], dram["v_scr"]
    o_scr, o_ag = dram["o_scr"], dram["o_ag"]
    cosd, sind = dram["cosT"], dram["sinT"]
    ps = C.ps8
    identb = C.const_tiles["ident_bf16"]
    masks = C.const_tiles["attn_masks"]
    negC = C.const_tiles["negC"]
    oidx = C.const_tiles["oidx"]

    A = arena
    wqkv = A[:, 0:24576]
    xblk = A[:, 24576:40960]
    qkb = [A[:, 40960 + i * 512:40960 + (i + 1) * 512] for i in range(2)]
    rop = [A[:, 41984 + i * 512:41984 + (i + 1) * 512] for i in range(2)]
    qh = A[:, 0:4096]
    kh = A[:, 4096:8192]
    vh = A[:, 8192:12288]
    vaug = A[:, 12288:16416]
    pbuf = [A[:, 16416 + i * 512:16416 + (i + 1) * 512] for i in range(2)]
    otile = [A[:, 17440 + i * 128:17440 + (i + 1) * 128] for i in range(2)]
    attnT = A[:, 0:16384]
    gat = [A[:, 16384 + i * 512:16384 + (i + 1) * 512] for i in range(2)]
    wobf = [A[:, 17664 + i * 4096:17664 + (i + 1) * 4096] for i in range(2)]

    w32 = C.sbuf(pref + "w32", [128, 2048], F32)
    trig = C.sbuf(pref + "trig", [128, 2048], BF16)
    t64 = [C.sbuf(pref + f"t64_{i}", [64, 512], F32) for i in range(4)]
    sdig = [C.sbuf(pref + f"sdig{i}", [128, 512], F32) for i in range(2)]
    rcol = C.sbuf(pref + "rcol", [128, 1], F32)

    s_st = C.sem(pref + "st")
    s_ag = C.sem(pref + "ag")

    @b.gpsimd
    def _(gpsimd):
        gpsimd.dma_start(out=cin_x.rearrange("(k p) t -> p k t", p=128),
                         in_=xn[:].rearrange("p (k t) -> p k t", k=KD)).then_inc(s_st.h, 16)
        gpsimd.wait_ge(s_st.h, 16)
        gpsimd.collective_compute("AllGather", ALU.bypass,
                                  replica_groups=[[0, 1, 2, 3], [4, 5, 6, 7]],
                                  ins=[cin_x], outs=[ag_x]).then_inc(s_ag.h, 1)
    s_st.inc(16)
    s_ag.inc(1)
    C.mark(s_ag)
    C.mark(s_st)
    C.barrier()

    xg3 = ag_x.rearrange("(rk p) t -> p rk t", p=128)
    wsrcs = [wqT, wkT, wvT]
    scrs = [q_scr, k_scr, v_scr]
    s_wd = C.sem(pref + "wd")
    s_wc = C.sem(pref + "wc")
    s_xd = [C.sem(pref + "xd0"), C.sem(pref + "xd1")]
    s_mm = C.sem(pref + "mm")
    s_ev = C.sem(pref + "ev")
    s_sp = [C.sem(pref + "sp0"), C.sem(pref + "sp1")]
    s_td = [C.sem(pref + "td0"), C.sem(pref + "td1")]
    chP = Chain(C, pref + "chP")

    @b.sync
    def _(sync):
        for wi in range(3):
            for ch in range(4):
                gi = wi * 4 + ch
                if gi >= 1:
                    sync.wait_ge(s_wc.h, gi)
                sync.dma_start(out=w32[:].rearrange("p (k o) -> p k o", k=KD),
                               in_=wsrcs[wi].rearrange("(k p) o -> p k o", p=128)[:, :, ch * 128:(ch + 1) * 128]
                               ).then_inc(s_wd.h, 16)
        for tb in range(8):
            par = tb % 2
            if tb >= 2:
                sync.wait_ge(s_mm.h, 12 * (tb - 1))
            r, tl = divmod(tb, 2)
            sync.dma_start(
                out=xblk[:, par * 8192:(par + 1) * 8192].rearrange("p (k t) -> p k t", k=KD),
                in_=xg3[:, r * 16:(r + 1) * 16, tl * 512:(tl + 1) * 512]).then_inc(s_xd[par].h, 16)
    s_wd.inc(16 * 12)
    for tb in range(8):
        s_xd[tb % 2].inc(16)

    @b.scalar
    def _(scalar):
        for tb in range(8):
            par = tb % 2
            if tb >= 2:
                scalar.wait_ge(s_ev.h, 12 * (tb - 1))
            scalar.dma_start(out=trig[:, par * 1024:par * 1024 + 512],
                             in_=cosd[:, tb * 512:(tb + 1) * 512]).then_inc(s_td[par].h, 16)
            scalar.dma_start(out=trig[:, par * 1024 + 512:par * 1024 + 1024],
                             in_=sind[:, tb * 512:(tb + 1) * 512]).then_inc(s_td[par].h, 16)
    for tb in range(8):
        s_td[tb % 2].inc(32)

    @b.vector
    def _(vector):
        for wi in range(3):
            for ch in range(4):
                gi = wi * 4 + ch
                vector.wait_ge(s_wd.h, 16 * (gi + 1))
                vector.tensor_copy(
                    wqkv.rearrange("p (w k o) -> p w k o", w=3, k=KD)[:, wi, :, ch * 128:(ch + 1) * 128],
                    w32[:].rearrange("p (k o) -> p k o", k=KD)).then_inc(s_wc.h, 1)
        for tb in range(8):
            par = tb % 2
            for sel in range(3):
                for ot in range(4):
                    gidx = tb * 12 + sel * 4 + ot
                    pb = gidx % 2
                    vector.wait_ge(s_mm.h, gidx + 1)
                    if gidx >= 2:
                        vector.wait_ge(s_sp[pb].h, 16 * (gidx // 2))
                    if sel == 2:
                        vector.tensor_copy(rop[pb], ps[4 + pb][:]).then_inc(s_ev.h, 1)
                    else:
                        vector.wait_ge(s_td[par].h, 32 * (tb // 2 + 1))
                        chP.dep(vector.tensor_copy(qkb[pb], ps[4 + pb][:]))
                        chP.wait(vector)
                        c_lo = trig[0:64, par * 1024:par * 1024 + 512]
                        s_lo = trig[0:64, par * 1024 + 512:par * 1024 + 1024]
                        c_hi = trig[64:128, par * 1024:par * 1024 + 512]
                        s_hi = trig[64:128, par * 1024 + 512:par * 1024 + 1024]
                        chP.dep(vector.tensor_tensor(t64[0][:], _pview(qkb[pb], 0, 64), c_lo, ALU.mult))
                        chP.dep(vector.tensor_tensor(t64[1][:], _pview(qkb[pb], 64, 64), s_hi, ALU.mult))
                        chP.dep(vector.tensor_tensor(t64[2][:], _pview(qkb[pb], 0, 64), s_lo, ALU.mult))
                        chP.dep(vector.tensor_tensor(t64[3][:], _pview(qkb[pb], 64, 64), c_hi, ALU.mult))
                        chP.wait(vector)
                        chP.dep(vector.tensor_tensor(_pview(rop[pb], 0, 64), t64[0][:], t64[1][:], ALU.subtract))
                        vector.tensor_tensor(_pview(rop[pb], 64, 64), t64[2][:], t64[3][:],
                                             ALU.add).then_inc(s_ev.h, 1)
    s_wc.inc(12)
    s_ev.inc(96)

    @b.tensor
    def _(tensor):
        for tb in range(8):
            par = tb % 2
            for sel in range(3):
                for ot in range(4):
                    gidx = tb * 12 + sel * 4 + ot
                    pb = gidx % 2
                    if gidx == 0:
                        tensor.wait_ge(s_wc.h, 12)
                    tensor.wait_ge(s_xd[par].h, 16 * (tb // 2 + 1))
                    if gidx >= 2:
                        tensor.wait_ge(s_ev.h, gidx - 1)
                    for k in range(KD):
                        mm = tensor.matmul(
                            ps[4 + pb][:],
                            wqkv[:, sel * 8192 + k * 512 + ot * 128: sel * 8192 + k * 512 + ot * 128 + 128],
                            xblk[:, par * 8192 + k * 512: par * 8192 + k * 512 + 512],
                            start=(k == 0), stop=(k == KD - 1))
                        if k == KD - 1:
                            mm.then_inc(s_mm.h, 1)
    s_mm.inc(96)

    @b.gpsimd
    def _(gpsimd):
        for tb in range(8):
            for sel in range(3):
                for ot in range(4):
                    gidx = tb * 12 + sel * 4 + ot
                    pb = gidx % 2
                    gpsimd.wait_ge(s_ev.h, gidx + 1)
                    gpsimd.dma_start(out=scrs[sel][ot * 128:(ot + 1) * 128, tb * 512:(tb + 1) * 512],
                                     in_=rop[pb]).then_inc(s_sp[pb].h, 16)
    s_sp[0].inc(16 * 48)
    s_sp[1].inc(16 * 48)

    for m in (s_mm, s_ev, s_sp[0], s_sp[1], s_wc, s_wd, s_xd[0], s_xd[1], s_td[0], s_td[1]):
        C.mark(m)
    C.barrier()

    SC, PV, DAD, OEV, N_SC, N_DAD, N_OEV = _mk_tables()
    s_hl = C.sem(pref + "hl")
    s_vt = C.sem(pref + "vt")
    s_va = C.sem(pref + "va")
    s_sc1 = C.sem(pref + "sc1")
    s_px = C.sem(pref + "px")
    s_dad = C.sem(pref + "dad")
    s_pv = C.sem(pref + "pv")
    s_oev = C.sem(pref + "oev")
    s_osp = [C.sem(pref + "osp0"), C.sem(pref + "osp1")]
    s_ms = C.sem(pref + "ms")
    chO = Chain(C, pref + "chO")

    @b.vector
    def _(vector):
        vector.memset(vaug, 1.0).then_inc(s_ms.h, 1)
    s_ms.inc(1)

    @b.sync
    def _(sync):
        for h in range(4):
            if h >= 1:
                sync.wait_ge(s_oev.h, 32 * h)
            sync.dma_start(out=qh, in_=q_scr[h * 128:(h + 1) * 128, :]).then_inc(s_hl.h, 16)
            sync.dma_start(out=kh, in_=k_scr[h * 128:(h + 1) * 128, :]).then_inc(s_hl.h, 16)
            sync.dma_start(out=vh, in_=v_scr[h * 128:(h + 1) * 128, :]).then_inc(s_hl.h, 16)
    s_hl.inc(48 * 4)

    @b.tensor
    def _(tensor):
        for h in range(4):
            tensor.wait_ge(s_hl.h, 48 * (h + 1))
            for kb in range(32):
                g = h * 32 + kb
                if g >= 2:
                    tensor.wait_ge(s_va.h, g - 1)
                if kb == 0:
                    tensor.wait_ge(s_ms.h, 1)
                    if h >= 1:
                        tensor.wait_ge(s_oev.h, 32 * h)
                tensor.transpose(ps[2 + g % 2][:, 0:64].bitcast(BF16),
                                 vh[:, kb * 128:(kb + 1) * 128],
                                 identb[:]).then_inc(s_vt.h, 1)
            for j in range(8):
                nkb = 4 * j + 4
                for kb in range(nkb):
                    if SC[(h, j, kb)] >= 3:
                        tensor.wait_ge(s_px.h, SC[(h, j, kb)] - 2)
                    mm = tensor.matmul(ps[kb % 2][:],
                                       kh[:, kb * 128:(kb + 1) * 128],
                                       qh[:, j * 512:(j + 1) * 512],
                                       start=True, stop=True)
                    mm.then_inc(s_sc1.h, 1)
                    if kb >= 1:
                        _emit_pv(C, tensor, ps, pbuf, vaug, h, j, kb - 1, nkb,
                                 SC, OEV, s_px, s_va, s_oev, s_pv)
                _emit_pv(C, tensor, ps, pbuf, vaug, h, j, nkb - 1, nkb,
                         SC, OEV, s_px, s_va, s_oev, s_pv)
    s_sc1.inc(N_SC)
    s_pv.inc(N_SC)

    @b.scalar
    def _(scalar):
        for h in range(4):
            for j in range(8):
                nkb = 4 * j + 4
                for kb in range(nkb):
                    idx = SC[(h, j, kb)]
                    if idx >= 3:
                        scalar.wait_ge(s_pv.h, idx - 2)
                    if kb >= 4 * j:
                        scalar.wait_ge(s_dad.h, DAD[(h, j, kb)])
                        scalar.activation(pbuf[kb % 2], sdig[kb % 2][:], AF.Exp,
                                          scale=SQD, bias=negC[:]).then_inc(s_px.h, 1)
                    else:
                        scalar.wait_ge(s_sc1.h, idx)
                        scalar.activation(pbuf[kb % 2], ps[kb % 2][:], AF.Exp,
                                          scale=SQD, bias=negC[:]).then_inc(s_px.h, 1)
    s_px.inc(N_SC)

    @b.vector
    def _(vector):
        oev = 0
        for h in range(4):
            for kb in range(32):
                g = h * 32 + kb
                vector.wait_ge(s_vt.h, g + 1)
                vector.tensor_copy(vaug[:, kb * 129:kb * 129 + 128],
                                   ps[2 + g % 2][:, 0:64].bitcast(BF16)).then_inc(s_va.h, 1)
            for j in range(8):
                nkb = 4 * j + 4
                for kb in range(4 * j, nkb):
                    idx = SC[(h, j, kb)]
                    if idx >= 3:
                        vector.wait_ge(s_px.h, idx - 2)
                    vector.wait_ge(s_sc1.h, idx)
                    krel = kb - 4 * j
                    vector.tensor_tensor(sdig[kb % 2][:], ps[kb % 2][:],
                                         masks[:, krel * 512:(krel + 1) * 512],
                                         ALU.add).then_inc(s_dad.h, 1)
                vector.wait_ge(s_pv.h, PV[(h, j, nkb - 1)])
                for qt in range(4):
                    if oev >= 1:
                        vector.wait_ge(s_oev.h, oev)
                    chO.dep(vector.reciprocal(rcol[:], ps[4 + qt][:, 128:129]))
                    chO.wait(vector)
                    opb = oev % 2
                    vector.wait_ge(s_osp[opb].h, 16 * (oev // 2))
                    vector.tensor_scalar(otile[opb], ps[4 + qt][:, 0:128], rcol[:], None,
                                         ALU.mult).then_inc(s_oev.h, 1)
                    oev += 1
    s_va.inc(128)
    s_dad.inc(N_DAD)
    s_oev.inc(N_OEV)

    @b.gpsimd
    def _(gpsimd):
        oev = 0
        for h in range(4):
            for j in range(8):
                for qt in range(4):
                    gpsimd.wait_ge(s_oev.h, oev + 1)
                    pos = j * 512 + qt * 128
                    gpsimd.dma_start(out=o_scr[pos:pos + 128, h * 128:(h + 1) * 128],
                                     in_=otile[oev % 2]).then_inc(s_osp[oev % 2].h, 16)
                    oev += 1
    s_osp[0].inc(16 * 64)
    s_osp[1].inc(16 * 64)

    for m in (s_sc1, s_px, s_pv, s_oev, s_osp[0], s_osp[1], s_va, s_hl, s_dad):
        C.mark(m)
    C.barrier()

    s_ag2 = C.sem(pref + "ag2")

    @b.gpsimd
    def _(gpsimd):
        gpsimd.collective_compute("AllGather", ALU.bypass,
                                  replica_groups=[[0, 1, 2, 3], [4, 5, 6, 7]],
                                  ins=[o_scr], outs=[o_ag]).then_inc(s_ag2.h, 1)
    s_ag2.inc(1)
    C.mark(s_ag2)
    C.barrier()

    from concourse.bass import IndirectOffsetOnAxis
    s_gd = [C.sem(pref + "gd0"), C.sem(pref + "gd1")]
    s_gt2 = C.sem(pref + "gt2")
    s_ge2 = C.sem(pref + "ge2")

    @b.gpsimd
    def _(gpsimd):
        for i in range(32):
            par = i % 2
            if i >= 2:
                gpsimd.wait_ge(s_gt2.h, 4 * (i - 1))
            gpsimd.indirect_dma_start(
                gat[par], None, o_ag,
                IndirectOffsetOnAxis(ap=oidx[:, i:i + 1], axis=0)).then_inc(s_gd[par].h, 16)
    s_gd[0].inc(16 * 16)
    s_gd[1].inc(16 * 16)

    @b.tensor
    def _(tensor):
        for i in range(32):
            r, tt = divmod(i, 8)
            par = i % 2
            tensor.wait_ge(s_gd[par].h, 16 * (i // 2 + 1))
            for hl in range(4):
                g = i * 4 + hl
                if g >= 2:
                    tensor.wait_ge(s_ge2.h, g - 1)
                tensor.transpose(ps[6 + g % 2][:, 0:64].bitcast(BF16),
                                 _fslice(gat[par], hl * 128, 128),
                                 identb[:]).then_inc(s_gt2.h, 1)
    s_gt2.inc(128)

    @b.vector
    def _(vector):
        for i in range(32):
            r, tt = divmod(i, 8)
            for hl in range(4):
                g = i * 4 + hl
                h = 4 * r + hl
                vector.wait_ge(s_gt2.h, g + 1)
                vector.tensor_copy(attnT[:, h * 1024 + tt * 128: h * 1024 + tt * 128 + 128],
                                   ps[6 + g % 2][:, 0:64].bitcast(BF16)).then_inc(s_ge2.h, 1)
    s_ge2.inc(128)

    for m in (s_gt2, s_ge2, s_gd[0], s_gd[1]):
        C.mark(m)
    C.barrier()

    s_od = C.sem(pref + "od")
    s_oc = C.sem(pref + "oc")
    s_omm = C.sem(pref + "omm")
    s_ora = C.sem(pref + "ora")

    @b.sync
    def _(sync):
        for ch in range(16):
            if ch >= 1:
                sync.wait_ge(s_oc.h, ch)
            sync.dma_start(out=w32[:].rearrange("p (k o) -> p k o", k=KD),
                           in_=woT.rearrange("(k p) o -> p k o", p=128)[:, :, ch * 128:(ch + 1) * 128]
                           ).then_inc(s_od.h, 16)
    s_od.inc(16 * 16)

    @b.scalar
    def _(scalar):
        for ch in range(16):
            scalar.wait_ge(s_od.h, 16 * (ch + 1))
            bb = (ch // 2) % 2
            if ch >= 4:
                scalar.wait_ge(s_omm.h, 2 * (ch - 3))
            scalar.activation(
                wobf[bb].rearrange("p (k o) -> p k o", o=256)[:, :, (ch % 2) * 128:(ch % 2) * 128 + 128],
                w32[:].rearrange("p (k o) -> p k o", k=KD),
                AF.Identity).then_inc(s_oc.h, 1)
    s_oc.inc(16)

    @b.tensor
    def _(tensor):
        for u in range(32):
            ob, st = divmod(u, 4)
            sub, tb = divmod(st, 2)
            bb = ob % 2
            pb = 4 + u % 2
            tensor.wait_ge(s_oc.h, 2 * (ob + 1))
            if u >= 2:
                tensor.wait_ge(s_ora.h, u - 1)
            for k in range(KD):
                mm = tensor.matmul(
                    ps[pb][:],
                    _fslice(wobf[bb], k * 256 + sub * 128, 128),
                    attnT[:, k * 1024 + tb * 512: k * 1024 + tb * 512 + 512],
                    start=(k == 0), stop=(k == KD - 1))
                if k == KD - 1:
                    mm.then_inc(s_omm.h, 1)
    s_omm.inc(32)

    @b.vector
    def _(vector):
        for u in range(32):
            ob, st = divmod(u, 4)
            sub, tb = divmod(st, 2)
            pb = 4 + u % 2
            ot = ob * 2 + sub
            vector.wait_ge(s_omm.h, u + 1)
            sl = slice(ot * T + tb * 512, ot * T + tb * 512 + 512)
            vector.tensor_tensor(R[:, sl], R[:, sl], ps[pb][:], ALU.add).then_inc(s_ora.h, 1)
    s_ora.inc(32)

    for m in (s_omm, s_ora, s_oc, s_od):
        C.mark(m)
    C.barrier()


def _pview(ap, p0, np_):
    return ap[p0:p0 + np_, :]


def _fslice(ap, c0, w):
    return ap[:, c0:c0 + w]


def _emit_pv(C, tensor, ps, pbuf, vaug, h, j, kb, nkb, SC, OEV, s_px, s_va, s_oev, s_pv):
    idx = SC[(h, j, kb)]
    tensor.wait_ge(s_px.h, idx)
    tensor.wait_ge(s_va.h, h * 32 + kb + 1)
    for qt in range(4):
        if kb == 0 and (h + j) > 0:
            tensor.wait_ge(s_oev.h, OEV[(h, j, qt)] - 4)
        mm = tensor.matmul(ps[4 + qt][:, 0:129],
                           _fslice(pbuf[kb % 2], qt * 128, 128),
                           _fslice(vaug, kb * 129, 129),
                           start=(kb == 0), stop=(kb == nkb - 1))
        if qt == 3:
            mm.then_inc(s_pv.h, 1)


def build_attn_launch():
    C = Ctx()
    nc = C.nc
    x_in = nc.dram_tensor("x_in", [DM, T], F32, kind="ExternalInput").ap()
    x_out = nc.dram_tensor("x_out", [DM, T], F32, kind="ExternalOutput").ap()
    wqT = nc.dram_tensor("wqT", [DM, 512], F32, kind="ExternalInput").ap()
    wkT = nc.dram_tensor("wkT", [DM, 512], F32, kind="ExternalInput").ap()
    wvT = nc.dram_tensor("wvT", [DM, 512], F32, kind="ExternalInput").ap()
    woT = nc.dram_tensor("woT", [DM, DM], F32, kind="ExternalInput").ap()
    cosT = nc.dram_tensor("cosT", [128, 4096], BF16, kind="ExternalInput").ap()
    sinT = nc.dram_tensor("sinT", [128, 4096], BF16, kind="ExternalInput").ap()
    dram = {
        "cin_x": nc.dram_tensor("cin_x", [DM, T], BF16).ap(),
        "ag_x": nc.dram_tensor("ag_x", [4 * DM, T], BF16).ap(),
        "q_scr": nc.dram_tensor("q_scr", [512, 4096], BF16).ap(),
        "k_scr": nc.dram_tensor("k_scr", [512, 4096], BF16).ap(),
        "v_scr": nc.dram_tensor("v_scr", [512, 4096], BF16).ap(),
        "o_scr": nc.dram_tensor("o_scr", [4096, 512], BF16).ap(),
        "o_ag": nc.dram_tensor("o_ag", [4 * 4096, 512], BF16).ap(),
        "cosT": cosT, "sinT": sinT,
    }
    R = C.sbuf("R", [128, KD * T], F32)
    C.start_block()
    sc = make_ln_scratch(C)
    C.const("ident_bf16", np.eye(128, dtype=np.float32), BF16)
    C.const("attn_masks", np.zeros((128, 2048), np.float32), BF16)
    C.const("negC", np.full((128, 1), -CSH, np.float32), F32)
    C.const("oidx", np.zeros((128, 32), np.int32), I32)
    arena = C.sbuf("arena", [128, 43008], BF16)
    xn = arena[:, 0:KD * T]
    C.emit_const_loads()
    emit_stream_load(C, R, x_in)
    C.barrier()
    emit_ln(C, sc, R, xn, mode="plain")
    C.barrier()
    emit_attn(C, sc, R, xn, arena, wqT, wkT, wvT, woT, dram)
    emit_stream_store(C, R, x_out)
    return C


def rope_perm():
    p = np.zeros(2048, np.int64)
    for h in range(16):
        base = h * 128
        p[base:base + 64] = base + np.arange(0, 128, 2)
        p[base + 64:base + 128] = base + np.arange(1, 128, 2)
    return p


def attn_in_maps(inp, x_stream_percore):
    perm = rope_perm()
    WqTp = f32c(np.asarray(inp["L1_attn_Wq"]).T[:, perm])
    WkTp = f32c(np.asarray(inp["L1_attn_Wk"]).T[:, perm])
    WvT = f32c(np.asarray(inp["L1_attn_Wv"]).T)
    WoT = f32c(np.asarray(inp["L1_attn_Wo"]).T)
    idx = np.arange(0, 128, 2, dtype=np.float32)
    inv = 10000.0 ** (-idx / 128.0)
    ang = np.arange(4096, dtype=np.float32)[:, None] * inv[None, :]
    cosT = bfc(np.concatenate([np.cos(ang).T, np.cos(ang).T], axis=0))
    sinT = bfc(np.concatenate([np.sin(ang).T, np.sin(ang).T], axis=0))
    m = np.zeros((128, 2048), np.float32)
    for krel in range(4):
        for kl in range(128):
            for ql in range(512):
                if krel * 128 + kl > ql:
                    m[kl, krel * 512 + ql] = -1e38
    masks = bfc(m)
    maps = []
    for c in range(8):
        p = c % 4
        oidx = np.zeros((128, 32), np.int32)
        for r in range(4):
            for tt in range(8):
                oidx[:, r * 8 + tt] = r * 4096 + p * 1024 + tt * 128 + np.arange(128)
        maps.append({
            "x_in": x_stream_percore[c],
            "wqT": np.ascontiguousarray(WqTp[:, p * 512:(p + 1) * 512]),
            "wkT": np.ascontiguousarray(WkTp[:, p * 512:(p + 1) * 512]),
            "wvT": np.ascontiguousarray(WvT[:, p * 512:(p + 1) * 512]),
            "woT": WoT,
            "cosT": cosT, "sinT": sinT,
            "attn_masks": masks,
            "negC": np.full((128, 1), -CSH, np.float32),
            "oidx": oidx,
            "ident_bf16": np.eye(128, dtype=np.float32).astype(ml_dtypes.bfloat16),
        })
    return maps



def emit_moe(C, sc, R, xn_xblk, w1d, w3d, w2d, sw1d, sw3d, sw2d, dram, pref="e"):
    b = C.block
    cin_x, cout_x, cin_l, cout_l = dram["cin_x"], dram["cout_x"], dram["cin_l"], dram["cout_l"]
    hidscr, partial, rs_out, hshscr, sh_out = (dram["hidscr"], dram["partial"],
                                               dram["rs_out"], dram["hshscr"], dram["sh_out"])
    gwt = C.const_tiles["gwt128"]
    ident = C.const_tiles["ident_f32"]
    onehot = C.const_tiles["onehot64"]
    ones_col = sc["ones_col"]
    Bm, Brs = sc["Bm"], sc["Brs"]
    ps = C.ps8
    xblk = xn_xblk

    logitsT = C.sbuf(pref + "logitsT", [8, T], F32)
    sg = C.sbuf(pref + "sg", [8, 1], F32)
    t8 = C.sbuf(pref + "t8", [8, 512], F32)
    mw32 = C.sbuf(pref + "mw32", [128, 2048], F32)
    w1h = C.sbuf(pref + "w1h", [128, 16 * 768], BF16)
    w3h = C.sbuf(pref + "w3h", [128, 16 * 768], BF16)
    gs = [C.sbuf(pref + f"gs{i}", [128, 512], BF16) for i in range(2)]
    hb = [C.sbuf(pref + f"hb{i}", [128, 512], BF16) for i in range(2)]
    hidT = C.sbuf(pref + "hidT", [128, 2 * 22 * 128], BF16)
    pout = [C.sbuf(pref + f"pout{i}", [128, 512], F32) for i in range(2)]
    lgt = C.sbuf(pref + "lgt", [128, 64], F32)
    gsc = C.sbuf(pref + "gsc", [128, 384], F32)
    w_all = C.sbuf(pref + "w_all", [128, 64], F32)

    s_lmm = C.sem(pref + "lmm")
    s_lev = C.sem(pref + "lev")
    chL = Chain(C, pref + "chL")

    @b.tensor
    def _(tensor):
        for k in range(KD):
            tensor.matmul(ps[4][0:8, 0:1], gwt[:, 8 * k:8 * k + 8], ones_col[:],
                          start=(k == 0), stop=(k == KD - 1))
        for tb in range(2):
            for k in range(KD):
                mm = tensor.matmul(ps[2 + tb][0:8, :], gwt[:, 8 * k:8 * k + 8],
                                   R[:, k * T + tb * 512: k * T + tb * 512 + 512],
                                   start=(k == 0), stop=(k == KD - 1))
                if k == KD - 1:
                    mm.then_inc(s_lmm.h, 1)
    s_lmm.inc(2)

    @b.vector
    def _(vector):
        vector.wait_ge(s_lmm.h, 2)
        chL.dep(vector.tensor_copy(sg[:], ps[4][0:8, 0:1]))
        for tb in range(2):
            sl = slice(tb * 512, (tb + 1) * 512)
            chL.wait(vector)
            chL.dep(vector.tensor_scalar(t8[:], Bm[0:8, sl], sg[:], None, ALU.mult))
            chL.wait(vector)
            chL.dep(vector.tensor_tensor(t8[:], ps[2 + tb][0:8, :], t8[:], ALU.subtract))
            chL.wait(vector)
            ll = vector.tensor_tensor(logitsT[:, sl], t8[:], Brs[0:8, sl], ALU.mult)
            if tb == 1:
                ll.then_inc(s_lev.h, 1)
            else:
                chL.dep(ll)
    s_lev.inc(1)
    C.mark(s_lev)
    C.mark(s_lmm)
    C.barrier()

    s_cc = C.sem(pref + "cc")
    s_ccd = C.sem(pref + "ccd")
    s_cw = C.sem(pref + "cw")
    s_cwc = C.sem(pref + "cwc")

    @b.gpsimd
    def _(gpsimd):
        gpsimd.dma_start(out=cin_l, in_=logitsT[:]).then_inc(s_ccd.h, 16)
        gpsimd.wait_ge(s_ccd.h, 16)
        gpsimd.collective_compute("AllGather", ALU.bypass,
                                  replica_groups=[[0, 1, 2, 3, 4, 5, 6, 7]],
                                  ins=[cin_x], outs=[cout_x]).then_inc(s_cc.h, 1)
        gpsimd.collective_compute("AllGather", ALU.bypass,
                                  replica_groups=[[0, 1, 2, 3, 4, 5, 6, 7]],
                                  ins=[cin_l], outs=[cout_l]).then_inc(s_cc.h, 1)
    s_cc.inc(2)
    s_ccd.inc(16)

    wsrc = [w1d, w3d]
    wdst = [w1h, w3h]
    oh0_chunks = []
    oh1_chunks = []
    for wi in range(2):
        for oh in range(2):
            HW = [768, 640][oh]
            off = 0
            while off < HW:
                w = min(128, HW - off)
                (oh0_chunks if oh == 0 else oh1_chunks).append((wi, oh, off, w))
                off += w
    n_oh0 = len(oh0_chunks)
    n_oh1 = len(oh1_chunks)

    @b.sync
    def _(sync):
        for ci, (wi, oh, off, w) in enumerate(oh0_chunks):
            if ci >= 1:
                sync.wait_ge(s_cwc.h, ci)
            sync.dma_start(out=mw32[:, 0:16 * w].rearrange("p (k o) -> p k o", k=KD),
                           in_=wsrc[wi].rearrange("(k p) o -> p k o", p=128)[:, :, off:off + w]
                           ).then_inc(s_cw.h, 16)
    s_cw.inc(16 * n_oh0)

    @b.vector
    def _(vector):
        for ci, (wi, oh, off, w) in enumerate(oh0_chunks):
            vector.wait_ge(s_cw.h, 16 * (ci + 1))
            vector.tensor_copy(
                wdst[wi][:].rearrange("p (k o) -> p k o", o=768)[:, :, off:off + w],
                mw32[:, 0:16 * w].rearrange("p (k o) -> p k o", k=KD)).then_inc(s_cwc.h, 1)
    s_cwc.inc(n_oh0)

    C.mark(s_cc)
    C.mark(s_cwc)
    C.mark(s_cw)
    C.barrier()

    s_gl = C.sem(pref + "gl")
    s_gt = C.sem(pref + "gt")
    s_ge = C.sem(pref + "ge")
    s_gd1 = C.sem(pref + "gd1")
    s_gs = C.sem(pref + "gs")
    s_gw = C.sem(pref + "gw")
    chG = Chain(C, pref + "chG")
    lgsb = logitsT

    @b.gpsimd
    def _(gpsimd):
        for r in range(8):
            if r >= 1:
                gpsimd.wait_ge(s_gt.h, 8 * r)
            gpsimd.dma_start(out=lgsb[:], in_=cout_l[8 * r:8 * r + 8, :]).then_inc(s_gl.h, 16)
    s_gl.inc(16 * 8)

    @b.tensor
    def _(tensor):
        for r in range(8):
            tensor.wait_ge(s_gl.h, 16 * (r + 1))
            for t8i in range(8):
                g = r * 8 + t8i
                if g >= 2:
                    tensor.wait_ge(s_ge.h, g - 1)
                tensor.transpose(ps[g % 2][0:128, 0:8],
                                 lgsb[0:8, t8i * 128:(t8i + 1) * 128],
                                 ident[0:8, 0:8]).then_inc(s_gt.h, 1)
    s_gt.inc(64)

    @b.vector
    def _(vector):
        for r in range(8):
            for t8i in range(8):
                g = r * 8 + t8i
                vector.wait_ge(s_gt.h, g + 1)
                vector.tensor_copy(lgt[:, t8i * 8:(t8i + 1) * 8],
                                   ps[g % 2][0:128, 0:8]).then_inc(s_ge.h, 1)
            l3 = lgt[:].rearrange("p (g e) -> p g e", g=8)
            m1 = gsc[:, 0:8]
            m1b = m1.rearrange("p (g e) -> p g e", e=1).broadcast_to([128, 8, 8])
            m2 = gsc[:, 8:16]
            m2b = m2.rearrange("p (g e) -> p g e", e=1).broadcast_to([128, 8, 8])
            den = gsc[:, 16:24]
            denb = den.rearrange("p (g e) -> p g e", e=1).broadcast_to([128, 8, 8])
            d1 = gsc[:, 64:128]
            u = gsc[:, 128:192]
            lm = gsc[:, 192:256]
            sel = gsc[:, 256:320]
            d3 = lambda x: x.rearrange("p (g e) -> p g e", e=8)
            vector.wait_ge(s_ge.h, 8 * (r + 1))
            chG.wait(vector)
            chG.dep(vector.tensor_reduce(m1, l3, axis=AX.X, op=ALU.max))
            chG.wait(vector)
            vector.tensor_tensor(d3(d1), l3, m1b, ALU.subtract).then_inc(s_gd1.h, 1)
            chG.dep(vector.tensor_tensor(d3(lm), l3, m1b, ALU.is_ge))
            chG.wait(vector)
            chG.dep(vector.scalar_tensor_tensor(lm[:], lm[:], -1e38, lgt[:], ALU.mult, ALU.add))
            chG.wait(vector)
            chG.dep(vector.tensor_reduce(m2, d3(lm), axis=AX.X, op=ALU.max))
            chG.wait(vector)
            chG.dep(vector.tensor_tensor(d3(sel), l3, m2b, ALU.is_ge))
            vector.wait_ge(s_gs.h, r + 1)
            chG.wait(vector)
            chG.dep(vector.tensor_tensor(u[:], u[:], sel[:], ALU.mult))
            chG.wait(vector)
            chG.dep(vector.tensor_reduce(den, d3(u), axis=AX.X, op=ALU.add))
            chG.wait(vector)
            chG.dep(vector.reciprocal(den, den))
            chG.wait(vector)
            chG.dep(vector.tensor_tensor(d3(u), d3(u), denb, ALU.mult))
            chG.wait(vector)
            chG.dep(vector.tensor_tensor(u[:], u[:], onehot[:], ALU.mult))
            chG.wait(vector)
            last = vector.tensor_reduce(w_all[:, r * 8:(r + 1) * 8], d3(u), axis=AX.X, op=ALU.add)
            if r == 7:
                last.then_inc(s_gw.h, 1)
            else:
                chG.dep(last)
    s_gd1.inc(8)
    s_gw.inc(1)

    @b.scalar
    def _(scalar):
        for r in range(8):
            scalar.wait_ge(s_gd1.h, r + 1)
            scalar.activation(gsc[:, 128:192], gsc[:, 64:128], AF.Exp).then_inc(s_gs.h, 1)
    s_gs.inc(8)

    C.mark(s_gw)
    C.mark(s_gs)
    C.mark(s_gl)
    C.barrier()

    xg3 = cout_x.rearrange("(rk p) t -> p rk t", p=128)
    hid3 = hidscr.rearrange("(o p) t -> p o t", p=128)
    OH_TILES = [(0, 6), (6, 11)]
    s_ax = [C.sem(pref + "ax0"), C.sem(pref + "ax1")]
    s_amm = C.sem(pref + "amm")
    s_asl = C.sem(pref + "asl")
    s_ahe = C.sem(pref + "ahe")
    s_asp = [C.sem(pref + "asp0"), C.sem(pref + "asp1")]
    s_cwc2 = C.sem(pref + "cwc2")

    units = []
    for oh in range(2):
        for tb in range(16):
            for ot in range(*OH_TILES[oh]):
                units.append((oh, tb, ot, len(units)))
    NUA = len(units)
    blocks = [(oh, tb) for oh in range(2) for tb in range(16)]
    U_through = []
    acc = 0
    for (oh, tb) in blocks:
        acc += OH_TILES[oh][1] - OH_TILES[oh][0]
        U_through.append(acc)
    n_oh0_units = U_through[15]

    @b.sync
    def _(sync):
        def xdma(bi):
            oh, tb = blocks[bi]
            r, tl = divmod(tb, 2)
            par = bi % 2
            if bi >= 2:
                sync.wait_ge(s_amm.h, 2 * U_through[bi - 2])
            sync.dma_start(
                out=xblk[:, par * 8192:(par + 1) * 8192].rearrange("p (k t) -> p k t", k=KD),
                in_=xg3[:, r * 16:(r + 1) * 16, tl * 512:(tl + 1) * 512]).then_inc(s_ax[par].h, 16)
        for bi in range(16):
            xdma(bi)
        for ci, (wi, oh, off, w) in enumerate(oh1_chunks):
            if ci == 0:
                sync.wait_ge(s_amm.h, 2 * n_oh0_units)
            else:
                sync.wait_ge(s_cwc2.h, ci)
            col0 = 768 + off
            sync.dma_start(out=mw32[:, 0:16 * w].rearrange("p (k o) -> p k o", k=KD),
                           in_=wsrc[wi].rearrange("(k p) o -> p k o", p=128)[:, :, col0:col0 + w]
                           ).then_inc(s_cw.h, 16)
        for bi in range(16, 32):
            xdma(bi)
    for bi in range(len(blocks)):
        s_ax[bi % 2].inc(16)
    s_cw.inc(16 * n_oh1)

    @b.vector
    def _(vector):
        def hev(u):
            pb = u % 2
            vector.wait_ge(s_amm.h, 2 * u + 2)
            vector.wait_ge(s_asl.h, u + 1)
            if u >= 2:
                vector.wait_ge(s_asp[u % 2].h, 16 * (u // 2))
            vector.tensor_tensor(hb[pb][:], ps[2 + pb][:], gs[pb][:],
                                 ALU.mult).then_inc(s_ahe.h, 1)
        for u in range(n_oh0_units):
            hev(u)
        cw_base = 16 * n_oh0
        for ci, (wi, oh, off, w) in enumerate(oh1_chunks):
            vector.wait_ge(s_cw.h, cw_base + 16 * (ci + 1))
            vector.tensor_copy(
                wdst[wi][:].rearrange("p (k o) -> p k o", o=768)[:, :, off:off + w],
                mw32[:, 0:16 * w].rearrange("p (k o) -> p k o", k=KD)).then_inc(s_cwc2.h, 1)
        for u in range(n_oh0_units, NUA):
            hev(u)
    s_cwc2.inc(n_oh1)
    s_ahe.inc(NUA)

    @b.tensor
    def _(tensor):
        for (oh, tb, ot, u) in units:
            bi = oh * 16 + tb
            par = bi % 2
            pb = u % 2
            oi = ot - OH_TILES[oh][0]
            tensor.wait_ge(s_ax[par].h, 16 * (bi // 2 + 1))
            if oh == 1:
                tensor.wait_ge(s_cwc2.h, n_oh1)
            if u >= 2:
                tensor.wait_ge(s_asl.h, u - 1)
                tensor.wait_ge(s_ahe.h, u - 1)
            for k in range(KD):
                mm = tensor.matmul(ps[pb][:], w1h[:, k * 768 + oi * 128: k * 768 + oi * 128 + 128],
                                   xblk[:, par * 8192 + k * 512: par * 8192 + k * 512 + 512],
                                   start=(k == 0), stop=(k == KD - 1))
                if k == KD - 1:
                    mm.then_inc(s_amm.h, 1)
            for k in range(KD):
                mm = tensor.matmul(ps[2 + pb][:], w3h[:, k * 768 + oi * 128: k * 768 + oi * 128 + 128],
                                   xblk[:, par * 8192 + k * 512: par * 8192 + k * 512 + 512],
                                   start=(k == 0), stop=(k == KD - 1))
                if k == KD - 1:
                    mm.then_inc(s_amm.h, 1)
    s_amm.inc(2 * NUA)

    @b.scalar
    def _(scalar):
        for (oh, tb, ot, u) in units:
            pb = u % 2
            scalar.wait_ge(s_amm.h, 2 * u + 1)
            if u >= 2:
                scalar.wait_ge(s_ahe.h, u - 1)
            scalar.activation(gs[pb][:], ps[pb][:], AF.Silu).then_inc(s_asl.h, 1)
    s_asl.inc(NUA)

    @b.gpsimd
    def _(gpsimd):
        for (oh, tb, ot, u) in units:
            pb = u % 2
            gpsimd.wait_ge(s_ahe.h, u + 1)
            gpsimd.dma_start(out=hid3[:, ot, tb * 512:(tb + 1) * 512],
                             in_=hb[pb][:]).then_inc(s_asp[u % 2].h, 16)
    s_asp[0].inc(16 * ((NUA + 1) // 2))
    s_asp[1].inc(16 * (NUA // 2))

    for m in (s_amm, s_asl, s_ahe, s_asp[0], s_asp[1], s_cwc2, s_cw, s_ax[0], s_ax[1]):
        C.mark(m)
    C.barrier()

    s_b_wd = C.sem(pref + "bwd")
    s_b_wc = C.sem(pref + "bwc")
    s_b_h = [C.sem(pref + "bh0"), C.sem(pref + "bh1")]
    s_bmm = C.sem(pref + "bmm")
    s_bev = C.sem(pref + "bev")
    s_bp = [C.sem(pref + "bp0"), C.sem(pref + "bp1")]
    w2bf = w3h
    w23 = w2d.rearrange("(k p) o -> p k o", p=128)

    @b.sync
    def _(sync):
        for dh in range(2):
            for ch in range(8):
                gi = dh * 8 + ch
                if gi >= 1:
                    sync.wait_ge(s_b_wc.h, gi)
                if dh == 1 and ch == 0:
                    sync.wait_ge(s_bmm.h, 2 * 64)
                sync.dma_start(out=mw32[:, 0:11 * 128].rearrange("p (k o) -> p k o", k=11),
                               in_=w23[:, :, dh * 1024 + ch * 128: dh * 1024 + ch * 128 + 128]
                               ).then_inc(s_b_wd.h, 16)
    s_b_wd.inc(16 * 16)

    @b.scalar
    def _(scalar):
        for gi in range(16):
            scalar.wait_ge(s_b_wd.h, 16 * (gi + 1))
            ch = gi % 8
            scalar.activation(
                w2bf[:, 0:11 * 1024].rearrange("p (k o) -> p k o", o=1024)[:, :, ch * 128:(ch + 1) * 128],
                mw32[:, 0:11 * 128].rearrange("p (k o) -> p k o", k=11),
                AF.Identity).then_inc(s_b_wc.h, 1)
    s_b_wc.inc(16)

    @b.gpsimd
    def _(gpsimd):
        for dh in range(2):
            for tt in range(64):
                j = dh * 64 + tt
                par = j % 2
                if j >= 2:
                    gpsimd.wait_ge(s_bmm.h, 2 * (j - 1))
                gpsimd.dma_start(
                    out=hidT[:, par * 22 * 128:par * 22 * 128 + 11 * 128].rearrange("p (k t) -> p k t", k=11),
                    in_=hid3[:, :, tt * 128:(tt + 1) * 128]).then_inc(s_b_h[par].h, 16)
                if j >= 1:
                    jj = j - 1
                    tt2 = jj % 64
                    dh2 = jj // 64
                    for dg2 in range(2):
                        gpsimd.wait_ge(s_bev.h, 2 * jj + dg2 + 1)
                        gpsimd.dma_start(
                            out=partial[tt2 * 128:(tt2 + 1) * 128,
                                        dh2 * 1024 + dg2 * 512: dh2 * 1024 + dg2 * 512 + 512],
                            in_=pout[dg2][:]).then_inc(s_bp[dg2].h, 16)
        jj = 127
        for dg2 in range(2):
            gpsimd.wait_ge(s_bev.h, 2 * jj + dg2 + 1)
            gpsimd.dma_start(
                out=partial[(jj % 64) * 128:((jj % 64) + 1) * 128,
                            (jj // 64) * 1024 + dg2 * 512: (jj // 64) * 1024 + dg2 * 512 + 512],
                in_=pout[dg2][:]).then_inc(s_bp[dg2].h, 16)
    s_b_h[0].inc(16 * 64)
    s_b_h[1].inc(16 * 64)
    s_bp[0].inc(16 * 128)
    s_bp[1].inc(16 * 128)

    @b.tensor
    def _(tensor):
        for dh in range(2):
            tensor.wait_ge(s_b_wc.h, 8 * (dh + 1))
            for tt in range(64):
                j = dh * 64 + tt
                par = j % 2
                tensor.wait_ge(s_b_h[par].h, 16 * (j // 2 + 1))
                if j >= 2:
                    tensor.wait_ge(s_bev.h, 2 * (j - 1))
                for dg2 in range(2):
                    bank = (j % 2) * 2 + dg2
                    for k in range(11):
                        mm = tensor.matmul(
                            ps[bank][:],
                            hidT[:, par * 22 * 128 + k * 128: par * 22 * 128 + k * 128 + 128],
                            w2bf[:, k * 1024 + dg2 * 512: k * 1024 + dg2 * 512 + 512],
                            start=(k == 0), stop=(k == 10))
                    mm.then_inc(s_bmm.h, 1)
    s_bmm.inc(2 * 128)

    @b.vector
    def _(vector):
        for dh in range(2):
            for tt in range(64):
                j = dh * 64 + tt
                for dg2 in range(2):
                    bank = (j % 2) * 2 + dg2
                    vector.wait_ge(s_bmm.h, 2 * j + dg2 + 1)
                    vector.wait_ge(s_bp[dg2].h, 16 * j)
                    vector.tensor_scalar(pout[dg2][:], ps[bank][:], w_all[:, tt:tt + 1], None,
                                         ALU.mult).then_inc(s_bev.h, 1)
    s_bev.inc(2 * 128)

    for m in (s_bmm, s_bev, s_bp[0], s_bp[1], s_b_wc, s_b_wd, s_b_h[0], s_b_h[1]):
        C.mark(m)
    C.barrier()

    s_rs = C.sem(pref + "rs")

    @b.gpsimd
    def _(gpsimd):
        gpsimd.collective_compute("ReduceScatter", ALU.add,
                                  replica_groups=[[0, 1, 2, 3, 4, 5, 6, 7]],
                                  ins=[partial], outs=[rs_out]).then_inc(s_rs.h, 1)
    s_rs.inc(1)

    swsrc = [sw1d, sw3d]
    hsh3 = hshscr.rearrange("(o p) t -> p o t", p=128)
    och_tiles = [(0, 6), (6, 12), (12, 18), (18, 22)]
    s_sw = C.sem(pref + "ssw")
    s_swc = C.sem(pref + "sswc")
    s_sx = [C.sem(pref + "ssx0"), C.sem(pref + "ssx1")]
    s_smm = C.sem(pref + "ssmm")
    s_ssl = C.sem(pref + "sssl")
    s_she = C.sem(pref + "sshe")
    s_ssp = [C.sem(pref + "ssp0"), C.sem(pref + "ssp1")]

    sw_chunks = []
    for och in range(4):
        o0, o1 = och_tiles[och]
        HW = (o1 - o0) * 128
        for wi in range(2):
            off = 0
            while off < HW:
                w = min(128, HW - off)
                sw_chunks.append((wi, och, off, w, len(sw_chunks)))
                off += w
    sunits = []
    for och in range(4):
        for tb in range(2):
            for ot in range(*och_tiles[och]):
                sunits.append((och, tb, ot, len(sunits)))
    NUS = len(sunits)
    chunks_through_och = []
    acc = 0
    for och in range(4):
        acc += len([c for c in sw_chunks if c[1] == och])
        chunks_through_och.append(acc)
    su_through_och = []
    acc = 0
    for och in range(4):
        acc += 2 * (och_tiles[och][1] - och_tiles[och][0])
        su_through_och.append(acc)

    @b.sync
    def _(sync):
        for och in range(4):
            for (wi, och_, off, w, ci) in [c for c in sw_chunks if c[1] == och]:
                if ci >= 1:
                    sync.wait_ge(s_swc.h, ci)
                if och >= 1 and off == 0 and wi == 0:
                    sync.wait_ge(s_smm.h, 2 * su_through_och[och - 1])
                o0 = och_tiles[och][0]
                col0 = o0 * 128 + off
                sync.dma_start(out=mw32[:, 0:16 * w].rearrange("p (k o) -> p k o", k=KD),
                               in_=swsrc[wi].rearrange("(k p) o -> p k o", p=128)[:, :, col0:col0 + w]
                               ).then_inc(s_sw.h, 16)
            for tb in range(2):
                si = och * 2 + tb
                par = si % 2
                if si >= 2:
                    prev_units = su_through_och[och - 1] if tb == 0 else \
                        su_through_och[och] - (och_tiles[och][1] - och_tiles[och][0])
                    sync.wait_ge(s_smm.h, 2 * prev_units)
                sync.dma_start(
                    out=xblk[:, par * 8192:(par + 1) * 8192].rearrange("p (k t) -> p k t", k=KD),
                    in_=cin_x.rearrange("(k p) t -> p k t", p=128)[:, :, tb * 512:(tb + 1) * 512]
                    ).then_inc(s_sx[par].h, 16)
    s_sw.inc(16 * len(sw_chunks))
    for si in range(8):
        s_sx[si % 2].inc(16)

    @b.vector
    def _(vector):
        ui = 0
        for och in range(4):
            for (wi, och_, off, w, ci) in [c for c in sw_chunks if c[1] == och]:
                vector.wait_ge(s_sw.h, 16 * (ci + 1))
                vector.tensor_copy(
                    wdst[wi][:].rearrange("p (k o) -> p k o", o=768)[:, :, off:off + w],
                    mw32[:, 0:16 * w].rearrange("p (k o) -> p k o", k=KD)).then_inc(s_swc.h, 1)
            while ui < su_through_och[och]:
                u = ui
                pb = u % 2
                vector.wait_ge(s_smm.h, 2 * u + 2)
                vector.wait_ge(s_ssl.h, u + 1)
                if u >= 2:
                    vector.wait_ge(s_ssp[u % 2].h, 16 * (u // 2))
                vector.tensor_tensor(hb[pb][:], ps[2 + pb][:], gs[pb][:],
                                     ALU.mult).then_inc(s_she.h, 1)
                ui += 1
    s_swc.inc(len(sw_chunks))
    s_she.inc(NUS)

    @b.tensor
    def _(tensor):
        for (och, tb, ot, u) in sunits:
            si = och * 2 + tb
            par = si % 2
            pb = u % 2
            oi = ot - och_tiles[och][0]
            tensor.wait_ge(s_swc.h, chunks_through_och[och])
            tensor.wait_ge(s_sx[par].h, 16 * (si // 2 + 1))
            if u >= 2:
                tensor.wait_ge(s_ssl.h, u - 1)
                tensor.wait_ge(s_she.h, u - 1)
            for k in range(KD):
                mm = tensor.matmul(ps[pb][:], w1h[:, k * 768 + oi * 128: k * 768 + oi * 128 + 128],
                                   xblk[:, par * 8192 + k * 512: par * 8192 + k * 512 + 512],
                                   start=(k == 0), stop=(k == KD - 1))
                if k == KD - 1:
                    mm.then_inc(s_smm.h, 1)
            for k in range(KD):
                mm = tensor.matmul(ps[2 + pb][:], w3h[:, k * 768 + oi * 128: k * 768 + oi * 128 + 128],
                                   xblk[:, par * 8192 + k * 512: par * 8192 + k * 512 + 512],
                                   start=(k == 0), stop=(k == KD - 1))
                if k == KD - 1:
                    mm.then_inc(s_smm.h, 1)
    s_smm.inc(2 * NUS)

    @b.scalar
    def _(scalar):
        for (och, tb, ot, u) in sunits:
            pb = u % 2
            scalar.wait_ge(s_smm.h, 2 * u + 1)
            if u >= 2:
                scalar.wait_ge(s_she.h, u - 1)
            scalar.activation(gs[pb][:], ps[pb][:], AF.Silu).then_inc(s_ssl.h, 1)
    s_ssl.inc(NUS)

    @b.gpsimd
    def _(gpsimd):
        for (och, tb, ot, u) in sunits:
            pb = u % 2
            gpsimd.wait_ge(s_she.h, u + 1)
            gpsimd.dma_start(out=hsh3[:, ot, tb * 512:(tb + 1) * 512],
                             in_=hb[pb][:]).then_inc(s_ssp[u % 2].h, 16)
    s_ssp[0].inc(16 * ((NUS + 1) // 2))
    s_ssp[1].inc(16 * (NUS // 2))

    for m in (s_smm, s_ssl, s_she, s_ssp[0], s_ssp[1], s_swc, s_sw, s_sx[0], s_sx[1]):
        C.mark(m)
    C.barrier()

    s_d2w = C.sem(pref + "d2w")
    s_d2c = C.sem(pref + "d2c")
    s_d2h = [C.sem(pref + "d2h0"), C.sem(pref + "d2h1")]
    s_d2m = C.sem(pref + "d2m")
    s_d2e = C.sem(pref + "d2e")
    s_d2p = [C.sem(pref + "d2p0"), C.sem(pref + "d2p1")]
    sw23 = sw2d.rearrange("(k p) o -> p k o", p=128)
    sw2bf = w3h

    @b.sync
    def _(sync):
        for dgrp in range(4):
            for ch in range(8):
                gi = dgrp * 8 + ch
                if gi >= 1:
                    sync.wait_ge(s_d2c.h, gi)
                if ch == 0 and dgrp >= 1:
                    sync.wait_ge(s_d2m.h, 8 * dgrp)
                sync.dma_start(out=mw32[:, 0:22 * 64].rearrange("p (k o) -> p k o", k=22),
                               in_=sw23[:, :, dgrp * 512 + ch * 64: dgrp * 512 + ch * 64 + 64]
                               ).then_inc(s_d2w.h, 16)
    s_d2w.inc(16 * 32)

    @b.scalar
    def _(scalar):
        for gi in range(32):
            dgrp, ch = divmod(gi, 8)
            scalar.wait_ge(s_d2w.h, 16 * (gi + 1))
            scalar.activation(
                sw2bf[:, 0:22 * 512].rearrange("p (k o) -> p k o", o=512)[:, :, ch * 64:(ch + 1) * 64],
                mw32[:, 0:22 * 64].rearrange("p (k o) -> p k o", k=22),
                AF.Identity).then_inc(s_d2c.h, 1)
    s_d2c.inc(32)

    @b.gpsimd
    def _(gpsimd):
        for dgrp in range(4):
            for tt in range(8):
                j = dgrp * 8 + tt
                par = j % 2
                if j >= 2:
                    gpsimd.wait_ge(s_d2m.h, j - 1)
                gpsimd.dma_start(
                    out=hidT[:, par * 22 * 128:(par + 1) * 22 * 128].rearrange("p (k t) -> p k t", k=22),
                    in_=hsh3[:, :, tt * 128:(tt + 1) * 128]).then_inc(s_d2h[par].h, 16)
                if j >= 1:
                    jj = j - 1
                    gpsimd.wait_ge(s_d2e.h, jj + 1)
                    gpsimd.dma_start(out=sh_out[(jj % 8) * 128:((jj % 8) + 1) * 128,
                                                (jj // 8) * 512:((jj // 8) + 1) * 512],
                                     in_=pout[jj % 2][:]).then_inc(s_d2p[jj % 2].h, 16)
        jj = 31
        gpsimd.wait_ge(s_d2e.h, jj + 1)
        gpsimd.dma_start(out=sh_out[(jj % 8) * 128:((jj % 8) + 1) * 128,
                                    (jj // 8) * 512:((jj // 8) + 1) * 512],
                         in_=pout[jj % 2][:]).then_inc(s_d2p[jj % 2].h, 16)
    s_d2h[0].inc(16 * 16)
    s_d2h[1].inc(16 * 16)
    s_d2p[0].inc(16 * 16)
    s_d2p[1].inc(16 * 16)

    @b.tensor
    def _(tensor):
        for dgrp in range(4):
            tensor.wait_ge(s_d2c.h, 8 * (dgrp + 1))
            for tt in range(8):
                j = dgrp * 8 + tt
                par = j % 2
                bank = 4 + j % 2
                tensor.wait_ge(s_d2h[par].h, 16 * (j // 2 + 1))
                if j >= 2:
                    tensor.wait_ge(s_d2e.h, j - 1)
                mm = None
                for k in range(22):
                    mm = tensor.matmul(
                        ps[bank][:],
                        hidT[:, par * 22 * 128 + k * 128: par * 22 * 128 + k * 128 + 128],
                        sw2bf[:, k * 512:(k + 1) * 512],
                        start=(k == 0), stop=(k == 21))
                mm.then_inc(s_d2m.h, 1)
    s_d2m.inc(32)

    @b.vector
    def _(vector):
        for j in range(32):
            vector.wait_ge(s_d2m.h, j + 1)
            vector.wait_ge(s_d2p[j % 2].h, 16 * (j // 2))
            bank = 4 + j % 2
            vector.tensor_copy(pout[j % 2][:], ps[bank][:]).then_inc(s_d2e.h, 1)
    s_d2e.inc(32)

    for m in (s_d2m, s_d2e, s_d2p[0], s_d2p[1], s_d2c, s_d2w, s_d2h[0], s_d2h[1]):
        C.mark(m)
    C.barrier()

    C.mark(s_rs)
    C.barrier()

    s_fd = [C.sem(pref + "fd0"), C.sem(pref + "fd1")]
    s_ftp = C.sem(pref + "ftp")
    s_fa = C.sem(pref + "fa")

    @b.sync
    def _(sync):
        for i in range(2 * KD):
            si, k = divmod(i, KD)
            src = rs_out if si == 0 else sh_out
            par = i % 2
            if i >= 2:
                sync.wait_ge(s_ftp.h, 8 * (i - 1))
            sync.dma_start(
                out=mw32[:, par * 1024:(par + 1) * 1024].rearrange("p (tt d) -> p tt d", tt=8),
                in_=src.rearrange("(tt p) d -> p tt d", p=128)[:, :, k * 128:(k + 1) * 128]
                ).then_inc(s_fd[par].h, 16)
    s_fd[0].inc(16 * KD)
    s_fd[1].inc(16 * KD)

    @b.tensor
    def _(tensor):
        for i in range(2 * KD):
            si, k = divmod(i, KD)
            par = i % 2
            tensor.wait_ge(s_fd[par].h, 16 * (i // 2 + 1))
            for tt in range(8):
                j = i * 8 + tt
                if j >= 2:
                    tensor.wait_ge(s_fa.h, j - 1)
                tensor.transpose(ps[6 + j % 2][:, 0:128],
                                 mw32[:, par * 1024 + tt * 128: par * 1024 + tt * 128 + 128],
                                 ident[:]).then_inc(s_ftp.h, 1)
    s_ftp.inc(16 * KD)

    @b.vector
    def _(vector):
        for i in range(2 * KD):
            si, k = divmod(i, KD)
            for tt in range(8):
                j = i * 8 + tt
                vector.wait_ge(s_ftp.h, j + 1)
                sl = slice(k * T + tt * 128, k * T + tt * 128 + 128)
                vector.tensor_tensor(R[:, sl], R[:, sl], ps[6 + j % 2][:, 0:128],
                                     ALU.add).then_inc(s_fa.h, 1)
    s_fa.inc(16 * KD)
    C.mark(s_fa)
    C.barrier()


def build_moe_launch():
    C = Ctx()
    nc = C.nc
    x_in = nc.dram_tensor("x_in", [DM, T], F32, kind="ExternalInput").ap()
    x_out = nc.dram_tensor("x_out", [DM, T], F32, kind="ExternalOutput").ap()
    w1d = nc.dram_tensor("moe_w1", [DM, 1408], F32, kind="ExternalInput").ap()
    w3d = nc.dram_tensor("moe_w3", [DM, 1408], F32, kind="ExternalInput").ap()
    w2d = nc.dram_tensor("moe_w2", [1408, DM], F32, kind="ExternalInput").ap()
    sw1d = nc.dram_tensor("sw1", [DM, 2816], F32, kind="ExternalInput").ap()
    sw3d = nc.dram_tensor("sw3", [DM, 2816], F32, kind="ExternalInput").ap()
    sw2d = nc.dram_tensor("sw2", [2816, DM], F32, kind="ExternalInput").ap()
    dram = {
        "cin_x": nc.dram_tensor("cin_x", [DM, T], BF16).ap(),
        "cout_x": nc.dram_tensor("cout_x", [8 * DM, T], BF16).ap(),
        "cin_l": nc.dram_tensor("cin_l", [8, T], F32).ap(),
        "cout_l": nc.dram_tensor("cout_l", [64, T], F32).ap(),
        "hidscr": nc.dram_tensor("hidscr", [1408, 8 * T], BF16).ap(),
        "partial": nc.dram_tensor("partial", [8 * T, DM], F32).ap(),
        "rs_out": nc.dram_tensor("rs_out", [T, DM], F32).ap(),
        "hshscr": nc.dram_tensor("hshscr", [2816, T], BF16).ap(),
        "sh_out": nc.dram_tensor("sh_out", [T, DM], F32).ap(),
    }
    R = C.sbuf("R", [128, KD * T], F32)
    C.start_block()
    sc = make_ln_scratch(C)
    C.const("gwt128", np.zeros((128, 128), np.float32), F32)
    C.const("ident_f32", np.eye(128, dtype=np.float32), F32)
    C.const("onehot64", np.zeros((128, 64), np.float32), F32)
    xblk = C.sbuf("exblk", [128, 2 * 16 * 512], BF16)
    xn = xblk
    C.emit_const_loads()
    emit_stream_load(C, R, x_in)
    C.barrier()
    emit_ln(C, sc, R, xn[:, 0:KD * T], mode="plain")
    C.barrier()
    s_xs = C.sem("xnstage")
    b = C.block

    @b.gpsimd
    def _(gpsimd):
        gpsimd.dma_start(out=dram["cin_x"].rearrange("(k p) t -> p k t", p=128),
                         in_=xn[:, 0:KD * T].rearrange("p (k t) -> p k t", k=KD)).then_inc(s_xs.h, 16)
    s_xs.inc(16)
    C.mark(s_xs)
    C.barrier()
    emit_moe(C, sc, R, xblk, w1d, w3d, w2d, sw1d, sw3d, sw2d, dram)
    emit_stream_store(C, R, x_out)
    return C


def moe_in_maps(inp, x_stream_percore):
    maps = []
    gw = f32c(np.asarray(inp["L1_ffn_gate_w"]).T)
    gwt128 = np.ascontiguousarray(gw.reshape(16, 128, 8).transpose(1, 0, 2).reshape(128, 128))
    for c in range(8):
        onehot = np.zeros((128, 64), np.float32)
        for t8i in range(8):
            onehot[:, t8i * 8 + c] = 1.0
        m = {
            "x_in": x_stream_percore[c],
            "moe_w1": f32c(inp["L1_ffn_w1"][c]),
            "moe_w3": f32c(inp["L1_ffn_w3"][c]),
            "moe_w2": f32c(inp["L1_ffn_w2"][c]),
            "sw1": f32c(inp["L1_ffn_sw1"]),
            "sw3": f32c(inp["L1_ffn_sw3"]),
            "sw2": f32c(inp["L1_ffn_sw2"]),
            "gwt128": gwt128,
            "ident_f32": np.eye(128, dtype=np.float32),
            "onehot64": onehot,
        }
        maps.append(m)
    return maps



def build_fused():
    C = Ctx()
    nc = C.nc
    x_in = nc.dram_tensor("x_in", [DM, T], F32, kind="ExternalInput").ap()
    x_out = nc.dram_tensor("x_out", [DM, T], F32, kind="ExternalOutput").ap()

    inWT = nc.dram_tensor("inWT", [DM, 8576], F32, kind="ExternalInput").ap()
    outWT = nc.dram_tensor("outWT", [4096, DM], F32, kind="ExternalInput").ap()
    m_dram = {
        "Lf": nc.dram_tensor("Lf_scr", [3, 4096], BF16).ap(),
        "gateT": nc.dram_tensor("gateT", [4096, T], BF16).ap(),
        "xs_f": nc.dram_tensor("xs_f_scr", [4096, T], BF16).ap(),
        "Ypart": nc.dram_tensor("Ypart", [T, 4096], BF16).ap(),
        "ynscr": nc.dram_tensor("ynscr", [4096, T], BF16).ap(),
        "cc_s_in": nc.dram_tensor("cc_s_in", [128, 4096], F32).ap(),
        "cc_s_out": nc.dram_tensor("cc_s_out", [512, 4096], F32).ap(),
        "cc_o_in": nc.dram_tensor("cc_o_in", [1, 64], F32).ap(),
        "cc_o_out": nc.dram_tensor("cc_o_out", [4, 64], F32).ap(),
        "h_in": nc.dram_tensor("h_in", [128, 102], BF16).ap(),
        "h_out": nc.dram_tensor("h_out", [512, 102], BF16).ap(),
        "Ddiag": nc.dram_tensor("Ddiag", [4096, 128], F32, kind="ExternalInput").ap(),
    }
    wgT = nc.dram_tensor("wgT", [DM, 8192], F32, kind="ExternalInput").ap()
    wuT = nc.dram_tensor("wuT", [DM, 8192], F32, kind="ExternalInput").ap()
    wdT = nc.dram_tensor("wdT", [8192, DM], F32, kind="ExternalInput").ap()
    hscr = nc.dram_tensor("hscr", [8192, T], BF16)
    wqT = nc.dram_tensor("wqT", [DM, 512], F32, kind="ExternalInput").ap()
    wkT = nc.dram_tensor("wkT", [DM, 512], F32, kind="ExternalInput").ap()
    wvT = nc.dram_tensor("wvT", [DM, 512], F32, kind="ExternalInput").ap()
    woT = nc.dram_tensor("woT", [DM, DM], F32, kind="ExternalInput").ap()
    cosT = nc.dram_tensor("cosT", [128, 4096], BF16, kind="ExternalInput").ap()
    sinT = nc.dram_tensor("sinT", [128, 4096], BF16, kind="ExternalInput").ap()
    a_dram = {
        "cin_x": nc.dram_tensor("a_cin_x", [DM, T], BF16).ap(),
        "ag_x": nc.dram_tensor("ag_x", [4 * DM, T], BF16).ap(),
        "q_scr": nc.dram_tensor("q_scr", [512, 4096], BF16).ap(),
        "k_scr": nc.dram_tensor("k_scr", [512, 4096], BF16).ap(),
        "v_scr": nc.dram_tensor("v_scr", [512, 4096], BF16).ap(),
        "o_scr": nc.dram_tensor("o_scr", [4096, 512], BF16).ap(),
        "o_ag": nc.dram_tensor("o_ag", [4 * 4096, 512], BF16).ap(),
        "cosT": cosT, "sinT": sinT,
    }
    w1d = nc.dram_tensor("moe_w1", [DM, 1408], F32, kind="ExternalInput").ap()
    w3d = nc.dram_tensor("moe_w3", [DM, 1408], F32, kind="ExternalInput").ap()
    w2d = nc.dram_tensor("moe_w2", [1408, DM], F32, kind="ExternalInput").ap()
    sw1d = nc.dram_tensor("sw1", [DM, 2816], F32, kind="ExternalInput").ap()
    sw3d = nc.dram_tensor("sw3", [DM, 2816], F32, kind="ExternalInput").ap()
    sw2d = nc.dram_tensor("sw2", [2816, DM], F32, kind="ExternalInput").ap()
    e_dram = {
        "cin_x": nc.dram_tensor("e_cin_x", [DM, T], BF16).ap(),
        "cout_x": nc.dram_tensor("cout_x", [8 * DM, T], BF16).ap(),
        "cin_l": nc.dram_tensor("cin_l", [8, T], F32).ap(),
        "cout_l": nc.dram_tensor("cout_l", [64, T], F32).ap(),
        "hidscr": nc.dram_tensor("hidscr", [1408, 8 * T], BF16).ap(),
        "partial": nc.dram_tensor("partial", [8 * T, DM], F32).ap(),
        "rs_out": nc.dram_tensor("rs_out", [T, DM], F32).ap(),
        "hshscr": nc.dram_tensor("hshscr", [2816, T], BF16).ap(),
        "sh_out": nc.dram_tensor("sh_out", [T, DM], F32).ap(),
    }

    R = C.sbuf("R", [128, KD * T], F32)
    C.start_block()
    sc = make_ln_scratch(C)
    C.const("ident_f32", np.eye(128, dtype=np.float32), F32)
    C.const("ident_bf16", np.eye(128, dtype=np.float32), BF16)
    C.const("ones_col_bf16", np.ones((128, 1), np.float32), BF16)
    C.const("ones_col_f32x", np.ones((128, 1), np.float32), F32)
    C.emit_const_loads()
    emit_stream_load(C, R, x_in)
    C.barrier()

    with C.scope():
        C.const("normW", np.zeros((128, 16), np.float32), F32)
        C.const("gnw", np.zeros((128, 32), np.float32), F32)
        C.const("convw", np.zeros((128, 136), np.float32), F32)
        C.const("convb", np.zeros((128, 34), np.float32), F32)
        C.const("dtb", np.zeros((64, 1), np.float32), F32)
        C.const("alog", np.zeros((64, 1), np.float32), F32)
        C.const("ohprev", np.zeros((128, 4), np.float32), F32)
        C.const("coefT", np.zeros((4, 4), np.float32), F32)
        C.const("masklt", np.zeros((4, 1), np.float32), F32)
        C.const("triu_f32", np.triu(np.ones((128, 128), np.float32)), F32)
        C.const("tril01_bf16", np.triu(np.ones((128, 128), np.float32)), BF16)
        m0 = np.where(np.triu(np.ones((128, 128), np.float32)) > 0, 0.0, -1e38).astype(np.float32)
        C.const("M04_bf16", np.tile(m0, (1, 4)), BF16)
        C.const("ones3_bf16", np.ones((3, 128), np.float32), BF16)
        C.const("negones3_bf16", -np.ones((3, 128), np.float32), BF16)
        C.emit_const_loads()
        A = C.sbuf("marena", [128, 6144 + 30976], BF16)
        xn = A[:, 6144 + 13952: 6144 + 13952 + KD * T]
        emit_ln(C, sc, R, xn, mode="mamba", h32=None, tag="B0")
        C.barrier()
        emit_rms_inplace(C, sc, xn)
        C.barrier()
        emit_mamba(C, sc, R, xn, None, A[:], inWT, outWT, m_dram)
        C.barrier()

    with C.scope():
        xn = C.sbuf("f_xn", [128, KD * T], BF16)
        emit_ln(C, sc, R, xn, mode="plain", tag="B1")
        C.barrier()
        emit_ffn(C, R, xn, wgT, wuT, wdT, hscr.ap())
        C.barrier()

    with C.scope():
        C.const("attn_masks", np.zeros((128, 2048), np.float32), BF16)
        C.const("negC", np.full((128, 1), -CSH, np.float32), F32)
        C.const("oidx", np.zeros((128, 32), np.int32), I32)
        C.emit_const_loads()
        arena = C.sbuf("arena", [128, 43008], BF16)
        xn = arena[:, 0:KD * T]
        emit_ln(C, sc, R, xn, mode="plain", tag="B2")
        C.barrier()
        emit_attn(C, sc, R, xn, arena, wqT, wkT, wvT, woT, a_dram)
        C.barrier()

    with C.scope():
        C.const("gwt128", np.zeros((128, 128), np.float32), F32)
        C.const("onehot64", np.zeros((128, 64), np.float32), F32)
        C.emit_const_loads()
        xblk = C.sbuf("exblk", [128, 2 * 16 * 512], BF16)
        xn = xblk
        emit_ln(C, sc, R, xn[:, 0:KD * T], mode="plain", tag="B3")
        C.barrier()
        s_xs = C.sem("xnstage")
        b = C.block

        @b.gpsimd
        def _(gpsimd):
            gpsimd.dma_start(out=e_dram["cin_x"].rearrange("(k p) t -> p k t", p=128),
                             in_=xn[:, 0:KD * T].rearrange("p (k t) -> p k t", k=KD)
                             ).then_inc(s_xs.h, 16)
        s_xs.inc(16)
        C.mark(s_xs)
        C.barrier()
        emit_moe(C, sc, R, xblk, w1d, w3d, w2d, sw1d, sw3d, sw2d, e_dram)
        C.barrier()

    emit_stream_store(C, R, x_out)
    return C


def fused_in_maps(inp, x_stream_percore):
    mm = mamba_in_maps(inp, x_stream_percore)
    fm = ffn_in_maps(inp, x_stream_percore)
    am = attn_in_maps(inp, x_stream_percore)
    em = moe_in_maps(inp, x_stream_percore)
    maps = []
    for c in range(8):
        m = {}
        m.update(fm[c])
        m.update(am[c])
        m.update(em[c])
        m.update(mm[c])
        maps.append(m)
    return maps





def kernel(**inputs):
    x0 = np.asarray(inputs["L0_attn_hidden_states"], dtype=np.float32)
    C = build_fused()
    maps = fused_in_maps(inputs, shard_stream(x0))
    for m in maps:
        for n, a in C.const_arrays.items():
            if n not in m:
                m[n] = a
    res = run_bass_kernel_spmd(C.nc, maps, core_ids=list(range(8)))
    return unshard_stream([res.results[c]["x_out"] for c in range(8)])
